# version 05
# baseline (speedup 1.0000x reference)
import numpy as np
from contextlib import ExitStack

import concourse.bass as bass
import concourse.mybir as mybir
from concourse.bass_utils import run_bass_kernel_spmd

F32 = mybir.dt.float32
BF16 = mybir.dt.bfloat16
AF = mybir.ActivationFunctionType
ALU = mybir.AluOpType
AX = mybir.AxisListType

NCORES = 8
D = 1024
T_FULL = 16384
TPC = T_FULL // NCORES

ENGINES = ("sync", "tensor", "vector", "scalar", "gpsimd")


class Sem:
    def __init__(self, ctx, nc, name):
        self.h = ctx.enter_context(nc.semaphore(name))
        self.v = 0
        self.name = name


class Consts:

    def __init__(self, ctx, nc, values, name="consts"):
        self.values = list(dict.fromkeys(float(v) for v in values))
        n = max(1, len(self.values))
        self.tile = ctx.enter_context(nc.sbuf_tensor(name, [128, n], F32))
        for j, v in enumerate(self.values):
            nc.const_aps.register(F32, v, self.tile[:, j:j + 1])

    def emit(self, p, sem):
        fns = [(lambda E, j=j, v=v: E.memset(self.tile[:, j:j + 1], v))
               for j, v in enumerate(self.values)]
        p.run("gpsimd", fns, inc=sem)


class Prog:

    def __init__(self):
        self.ev = {e: [] for e in ENGINES}

    def emit(self, eng, fn):
        self.ev[eng].append(fn)

    def wait(self, eng, sem, val=None):
        v = sem.v if val is None else val
        h = sem.h if isinstance(sem, Sem) else sem
        self.ev[eng].append(lambda E, s=h, v=v: E.wait_ge(s, v))

    def run(self, eng, fns, deps=(), inc=None, inc_n=1):
        for s in deps:
            self.wait(eng, s)
        if not isinstance(fns, (list, tuple)):
            fns = [fns]
        for i, fn in enumerate(fns):
            if inc is not None and i == len(fns) - 1:
                sem = inc
                n = inc_n
                sem.v += n
                self.ev[eng].append(lambda E, f=fn, s=sem.h, n=n: f(E).then_inc(s, n))
            else:
                self.ev[eng].append(fn)

    def dma(self, eng, out, in_, deps=(), inc=None):
        def f(E, o=out, i=in_):
            return E.dma_start(out=o, in_=i)
        self.run(eng, f, deps=deps, inc=inc, inc_n=16)

    def build(self, nc):
        blk_ctx = nc.Block()
        blk = blk_ctx.__enter__()
        for eng in ENGINES:
            fns = self.ev[eng]
            if not fns:
                continue

            def make(fns):
                def body(E):
                    for f in fns:
                        f(E)
                return body
            getattr(blk, eng)(make(fns))
        blk_ctx.__exit__(None, None, None)



def build_prep(rows, cols):
    nc = bass.Bass("TRN2", target_bir_lowering=False, debug=False)
    src = nc.dram_tensor("src", [rows, cols], F32, kind="ExternalInput").ap()
    dst = nc.dram_tensor("dst", [rows, cols], BF16, kind="ExternalOutput").ap()
    ntiles = (rows + 127) // 128
    CB = min(cols, 16384)
    ncb = (cols + CB - 1) // CB
    with ExitStack() as ctx:
        e = ctx.enter_context
        a = [e(nc.sbuf_tensor(f"a{i}", [128, CB], F32)) for i in range(2)]
        b = [e(nc.sbuf_tensor(f"b{i}", [128, CB], BF16)) for i in range(2)]
        ld = [Sem(ctx, nc, f"ld{k}") for k in range(2)]
        cv = Sem(ctx, nc, "cv")
        st = [Sem(ctx, nc, f"st{k}") for k in range(2)]
        p = Prog()
        idx = 0
        for i in range(ntiles):
            r0, r1 = i * 128, min((i + 1) * 128, rows)
            for j in range(ncb):
                c0, c1 = j * CB, min((j + 1) * CB, cols)
                k = idx % 2
                if idx >= 2:
                    p.wait("sync", st[k].h, 16 * ((idx - 2) // 2 + 1))
                p.dma("sync", a[k][0:r1 - r0, 0:c1 - c0], src[r0:r1, c0:c1], inc=ld[k])
                p.wait("scalar", ld[k].h, 16 * (idx // 2 + 1))
                p.run("scalar",
                      lambda E, k=k, nr=r1 - r0, ncl=c1 - c0:
                      E.activation(out=b[k][0:nr, 0:ncl], in_=a[k][0:nr, 0:ncl], func=AF.Copy),
                      inc=cv)
                p.run("gpsimd",
                      lambda E, k=k, r0=r0, r1=r1, c0=c0, c1=c1:
                      E.dma_start(out=dst[r0:r1, c0:c1], in_=b[k][0:r1 - r0, 0:c1 - c0]),
                      deps=(cv,), inc=st[k], inc_n=16)
                idx += 1
        p.wait("gpsimd", st[0].h, st[0].v)
        p.wait("gpsimd", st[1].h, st[1].v)
        p.build(nc)
    return nc


_NC_CACHE = {}


def _cached(key, fn):
    if key not in _NC_CACHE:
        _NC_CACHE[key] = fn()
    return _NC_CACHE[key]


def cast_bf16_dev(arr, launches):
    a = np.ascontiguousarray(arr, dtype=np.float32)
    shape = a.shape
    n = a.size
    cols = shape[-1]
    rows = n // cols
    a2 = a.reshape(rows, cols)
    rpc = (rows + NCORES - 1) // NCORES
    pad = rpc * NCORES - rows
    if pad:
        a2 = np.concatenate([a2, np.zeros((pad, cols), np.float32)], axis=0)
    nc = _cached(("prep", rpc, cols), lambda: build_prep(rpc, cols))
    in_maps = [{"src": np.ascontiguousarray(a2[i * rpc:(i + 1) * rpc])} for i in range(NCORES)]
    res = launches(nc, in_maps)
    out = np.concatenate([res[i]["dst"] for i in range(NCORES)], axis=0)
    if pad:
        out = out[:rows]
    return out.reshape(shape)



def emit_ln_tile(p, t, x_f32, out_bf, sq, sums, mu, rstd, s_a, s_v,
                 eps=1e-5, out_f32=None):
    p.run("scalar", lambda E: E.activation(out=sq[:], in_=x_f32, func=AF.Square),
          inc=s_a)
    p.run("vector", [
        lambda E: E.reduce_sum(sums[:, 0:1], x_f32, axis=AX.X),
    ])
    p.run("vector", [
        lambda E: E.reduce_sum(sums[:, 1:2], sq[:], axis=AX.X),
    ], deps=(s_a,), inc=s_v)
    p.run("vector", [
        lambda E: E.tensor_scalar(mu[:], sums[:, 0:1], 1.0 / D, None, ALU.mult),
    ], deps=(s_v,), inc=s_v)
    p.run("vector", [
        lambda E: E.tensor_tensor(sums[:, 2:3], mu[:], mu[:], ALU.mult),
    ], deps=(s_v,), inc=s_v)
    p.run("vector", [
        lambda E: E.tensor_scalar(sums[:, 3:4], sums[:, 1:2], 1.0 / D, None, ALU.mult),
    ], deps=(s_v,), inc=s_v)
    p.run("vector", [
        lambda E: E.tensor_sub(sums[:, 3:4], sums[:, 3:4], sums[:, 2:3]),
    ], deps=(s_v,), inc=s_v)
    p.run("scalar", lambda E: E.activation(out=rstd[:], in_=sums[:, 3:4],
                                           func=AF.Sqrt, bias=float(eps)),
          deps=(s_v,), inc=s_a)
    p.run("vector", [lambda E: E.reciprocal(rstd[:], rstd[:])], deps=(s_a,), inc=s_v)
    out_aps = [out_bf] if out_f32 is None else (
        [out_f32] if out_bf is None else [out_bf, out_f32])
    outs = [(lambda E, o=o: E.tensor_scalar(o, x_f32, mu[:], rstd[:],
                                            ALU.subtract, ALU.mult))
            for o in out_aps]
    p.run("vector", outs, deps=(s_v,), inc=s_v)


def emit_tp_in(p, src_bf, dst, ident, ps_view, s_tp, s_ev, nsrc_part=128,
               kchunks=8, deps=(), evac_eng="vector"):
    tps = [
        (lambda E, k=k: E.transpose(
            ps_view(k),
            src_bf[0:nsrc_part, k * 128:(k + 1) * 128],
            ident[0:nsrc_part, 0:nsrc_part]))
        for k in range(kchunks)
    ]
    p.run("tensor", tps, deps=(s_ev,) + tuple(deps), inc=s_tp)
    p.run(evac_eng, [lambda E, k=k: E.tensor_copy(dst(k), ps_view(k))
                     for k in range(kchunks)], deps=(s_tp,), inc=s_ev)






def build_f0(ntok=TPC, tg=256):
    nc = bass.Bass("TRN2", target_bir_lowering=False, debug=False)
    x_in = nc.dram_tensor("x", [ntok, D], F32, kind="ExternalInput").ap()
    w1bd = nc.dram_tensor("w1bd", [128, 32 * 128], BF16, kind="ExternalInput").ap()
    w2bd = nc.dram_tensor("w2bd", [128, 32 * 512], BF16, kind="ExternalInput").ap()
    wcT = nc.dram_tensor("wcT", [16384, 1024], BF16, kind="ExternalInput").ap()
    ident_in = nc.dram_tensor("identin", [128, 128], F32, kind="ExternalInput").ap()
    x_out = nc.dram_tensor("xo", [ntok, D], F32, kind="ExternalOutput").ap()

    ngrp = ntok // tg
    nsub = tg // 128

    with ExitStack() as ctx:
        e = ctx.enter_context
        w1 = e(nc.sbuf_tensor("w1", [128, 32, 128], BF16))
        w2 = e(nc.sbuf_tensor("w2", [128, 32, 512], BF16))
        identf = e(nc.sbuf_tensor("identf", [128, 128], F32))
        identb = e(nc.sbuf_tensor("identb", [128, 128], BF16))
        wc = [e(nc.sbuf_tensor(f"wc{i}", [128, 1024], BF16)) for i in range(2)]
        xt = [e(nc.sbuf_tensor(f"xt{i}", [128, D], F32)) for i in range(2 * nsub)]
        sq = e(nc.sbuf_tensor("sq", [128, D], F32))
        sums = e(nc.sbuf_tensor("sums", [128, 4], F32))
        mu = e(nc.sbuf_tensor("mu", [128, 1], F32))
        rstd = e(nc.sbuf_tensor("rstd", [128, 1], F32))
        xln = e(nc.sbuf_tensor("xln", [128, D], BF16))
        xlnT = e(nc.sbuf_tensor("xlnT", [128, 8, tg], BF16))
        yT = e(nc.sbuf_tensor("yT", [128, 8, tg], BF16))
        hg = e(nc.sbuf_tensor("hg", [128, 32, tg], BF16))
        outT = e(nc.sbuf_tensor("outT", [128, 8, tg], F32))
        fin = [e(nc.sbuf_tensor(f"fin{i}", [128, D], F32)) for i in range(2)]
        ps_tp = e(nc.psum_tensor("ps_tp", [128, 1024], F32))
        ps_s = e(nc.psum_tensor("ps_s", [128, 1024], F32))
        ps_out = e(nc.psum_tensor("ps_out", [128, 8, tg], F32))

        def tpb(k):
            return ps_tp[:].bitcast(BF16)[:, k * 128:(k + 1) * 128]

        s_w = Sem(ctx, nc, "s_w")
        s_x = [Sem(ctx, nc, f"s_x{j}") for j in range(2 * nsub)]
        s_wc = [Sem(ctx, nc, f"s_wc{j}") for j in range(2)]
        s_a = Sem(ctx, nc, "s_a")
        s_v = Sem(ctx, nc, "s_v")
        s_tp = Sem(ctx, nc, "s_tp")
        s_ev = Sem(ctx, nc, "s_ev")
        s_mm = Sem(ctx, nc, "s_mm")
        s_gel = Sem(ctx, nc, "s_gel")
        s_st = [Sem(ctx, nc, f"s_st{j}") for j in range(2)]
        s_fin = Sem(ctx, nc, "s_fin")
        s_c = Sem(ctx, nc, "s_c")
        cst = Consts(ctx, nc, [1e-5])

        p = Prog()
        cst.emit(p, s_c)
        p.dma("sync", w1[:], w1bd.rearrange("p (a m) -> p a m", a=32), inc=s_w)
        p.dma("sync", w2[:], w2bd.rearrange("p (a m) -> p a m", a=32), inc=s_w)
        p.dma("sync", identf[:], ident_in, inc=s_w)
        p.run("scalar", lambda E: E.activation(out=identb[:], in_=identf[:], func=AF.Copy),
              deps=(s_w, s_c), inc=s_a)
        p.wait("vector", s_c)
        for eng in ("tensor", "vector"):
            p.wait(eng, s_w)
            p.wait(eng, s_a)

        wc_idx = 0
        wc_mm_done = []
        for g in range(ngrp):
            for s in range(nsub):
                i = g * nsub + s
                j = i % (2 * nsub)
                if i >= 2 * nsub:
                    p.wait("sync", s_fin, i - 2 * nsub + 1)
                p.dma("sync", xt[j][:],
                      x_in[g * tg + s * 128: g * tg + (s + 1) * 128, :], inc=s_x[j])

            for s in range(nsub):
                i = g * nsub + s
                j = i % (2 * nsub)
                buf = xt[j]
                p.wait("scalar", s_x[j], 16 * (i // (2 * nsub) + 1))
                p.wait("vector", s_x[j], 16 * (i // (2 * nsub) + 1))
                p.wait("scalar", s_v)
                p.wait("vector", s_tp)
                emit_ln_tile(p, i, buf[:], xln[:], sq, sums, mu, rstd, s_a, s_v)
                emit_tp_in(p, xln, (lambda k, s=s: xlnT[:, k, s * 128:(s + 1) * 128]),
                           identb, tpb, s_tp, s_ev, deps=(s_v,))

            for h in range(4):
                ev0 = s_ev.v
                for k in range(8):
                    off = (k % 2) * 512
                    if k >= 2:
                        p.wait("tensor", s_ev, ev0 + (k - 2) + 1)
                    elif k == 0:
                        p.wait("tensor", s_ev)
                        p.wait("tensor", s_gel)
                    p.run("tensor",
                          lambda E, h=h, k=k, off=off:
                          E.matmul(ps_s[:, off:off + tg],
                                   w1[:, h * 8 + k, :], xlnT[:, k, 0:tg],
                                   start=True, stop=True),
                          inc=s_mm)
                    p.run("vector",
                          lambda E, k=k, off=off:
                          E.tensor_copy(yT[:, k, 0:tg], ps_s[:, off:off + tg]),
                          deps=(s_mm,), inc=s_ev)
                for ky in range(8):
                    p.wait("tensor", s_gel)
                    if ky == 0:
                        p.wait("tensor", s_ev)
                    mms = [
                        (lambda E, h=h, ky=ky, m=m:
                         E.matmul(ps_s[:, m * tg:(m + 1) * tg],
                                  w2[:, h * 8 + ky, m * 128:(m + 1) * 128],
                                  yT[:, ky, 0:tg], start=True, stop=True))
                        for m in range(4)
                    ]
                    p.run("tensor", mms, inc=s_mm)
                    p.run("scalar",
                          lambda E, ky=ky:
                          E.activation(
                              out=hg[:, ky * 4:(ky + 1) * 4, 0:tg].rearrange("p a t -> p (a t)"),
                              in_=ps_s[:, 0:4 * tg], func=AF.Gelu),
                          deps=(s_mm,), inc=s_gel)
                p.wait("tensor", s_gel)
                for kcl in range(32):
                    kc = h * 32 + kcl
                    b = wc_idx % 2
                    if wc_idx >= 2:
                        p.wait("gpsimd", s_mm, wc_mm_done[wc_idx - 2])
                    p.dma("gpsimd", wc[b][:],
                          wcT[kc * 128:(kc + 1) * 128, :], inc=s_wc[b])
                    p.wait("tensor", s_wc[b], 16 * (wc_idx // 2 + 1))
                    first = (h == 0 and kcl == 0)
                    last = (kc == 127)
                    mms = [
                        (lambda E, m=m, kcl=kcl, first=first, last=last, b=wc_idx % 2:
                         E.matmul(ps_out[:, m, 0:tg], wc[b][:, m * 128:(m + 1) * 128],
                                  hg[:, kcl, 0:tg],
                                  start=(first and m % 2 == 0),
                                  stop=(last and m % 2 == 1)))
                        for m in range(8)
                    ]
                    p.run("tensor", mms, inc=s_mm)
                    wc_mm_done.append(s_mm.v)
                    wc_idx += 1

            p.run("vector", [lambda E: E.tensor_copy(outT[:, 0:4, :], ps_out[:, 0:4, :])],
                  deps=(s_mm,), inc=s_ev)
            p.run("scalar", [lambda E: E.activation(
                out=outT[:, 4:8, :].rearrange("p a t -> p (a t)"),
                in_=ps_out[:, 4:8, :].rearrange("p a t -> p (a t)"), func=AF.Copy)],
                deps=(s_mm,), inc=s_a)
            for s in range(nsub):
                i = g * nsub + s
                for half in range(2):
                    ks = range(half * 4, half * 4 + 4)
                    if not (g == 0 and s == 0 and half == 0):
                        p.wait("tensor", s_v)
                    if s == 0 and half == 0:
                        p.wait("tensor", s_ev)
                        p.wait("tensor", s_a)
                    p.run("tensor",
                          [(lambda E, k=k, s=s, half=half:
                            E.transpose(ps_tp[:, half * 512 + (k % 4) * 128:
                                              half * 512 + (k % 4) * 128 + 128],
                                        outT[:, k, s * 128:(s + 1) * 128],
                                        identf[:]))
                           for k in ks], inc=s_tp)
                    if half == 0 and i >= 2:
                        p.wait("vector", s_st[i % 2], 16 * ((i - 2) // 2 + 1))
                    p.run("vector",
                          [(lambda E, i=i, k=k, half=half:
                            E.scalar_tensor_tensor(
                                fin[i % 2][:, k * 128:(k + 1) * 128],
                                ps_tp[:, half * 512 + (k % 4) * 128:
                                      half * 512 + (k % 4) * 128 + 128], 0.5,
                                xt[i % (2 * nsub)][:, k * 128:(k + 1) * 128],
                                ALU.mult, ALU.add))
                           for k in ks], deps=(s_tp,), inc=s_v)
                p.dma("sync", x_out[g * tg + s * 128: g * tg + (s + 1) * 128, :],
                      fin[i % 2][:], deps=(s_v,), inc=s_st[i % 2])
                p.run("gpsimd", lambda E: E.engine_nop(), deps=(s_v,), inc=s_fin)
        p.wait("sync", s_st[0])
        p.wait("sync", s_st[1])
        p.build(nc)
    return nc


def f0_weights_host(rail_maker, rail_up_proj, w_cproj):
    w1 = np.zeros((128, 32, 128), np.float32)
    w2 = np.zeros((128, 32, 512), np.float32)
    for h in range(4):
        for kc in range(8):
            for r in range(8):
                w1[16 * r:16 * r + 16, h * 8 + kc, 16 * r:16 * r + 16] = \
                    rail_maker[h, 8 * kc + r]
                w2[16 * r:16 * r + 16, h * 8 + kc, 64 * r:64 * r + 64] = \
                    rail_up_proj[h, 8 * kc + r].T
    wcT = np.ascontiguousarray(w_cproj.T)
    return (w1.reshape(128, 32 * 128), w2.reshape(128, 32 * 512), wcT)



def build_scan(nsteps=2048, CH=128):
    nc = bass.Bass("TRN2", target_bir_lowering=False, debug=False)
    WT_in = nc.dram_tensor("WT", [128, 8 * 1024], BF16, kind="ExternalInput").ap()
    h0_in = nc.dram_tensor("h0", [128, 16], F32, kind="ExternalInput").ap()
    a_pl = [nc.dram_tensor(f"a{b}", [128, nsteps, 8], F32, kind="ExternalInput").ap()
            for b in range(2)]
    c_pl = [nc.dram_tensor(f"c{b}", [128, nsteps, 8], F32, kind="ExternalInput").ap()
            for b in range(2)]
    ident_in = nc.dram_tensor("identin", [128, 128], F32, kind="ExternalInput").ap()
    hs_pl = [nc.dram_tensor(f"hs{b}", [128, nsteps, 8], F32, kind="ExternalOutput").ap()
             for b in range(2)]
    hf_out = nc.dram_tensor("hf", [128, 16], F32, kind="ExternalOutput").ap()

    nch = nsteps // CH
    with ExitStack() as ctx:
        e = ctx.enter_context
        WTs = e(nc.sbuf_tensor("WTs", [128, 8, 1024], BF16))
        identf = e(nc.sbuf_tensor("identf", [128, 128], F32))
        identb = e(nc.sbuf_tensor("identb", [128, 128], BF16))
        hTf = e(nc.sbuf_tensor("hTf", [128, 16], F32))
        hTb = e(nc.sbuf_tensor("hTb", [128, 16], BF16))
        u_sb = e(nc.sbuf_tensor("u_sb", [2, 1024], BF16))
        s_sb = e(nc.sbuf_tensor("s_sb", [128, 16], F32))
        g_sb = e(nc.sbuf_tensor("g_sb", [128, 16], F32))
        d_sb = e(nc.sbuf_tensor("d_sb", [128, 16], F32))
        m_sb = e(nc.sbuf_tensor("m_sb", [128, 16], F32))
        aring = [[e(nc.sbuf_tensor(f"ar{b}_{j}", [128, CH, 8], F32)) for j in range(2)]
                 for b in range(2)]
        cring = [[e(nc.sbuf_tensor(f"cr{b}_{j}", [128, CH, 8], F32)) for j in range(2)]
                 for b in range(2)]
        stg = [[e(nc.sbuf_tensor(f"st{b}_{j}", [128, CH, 8], F32)) for j in range(2)]
               for b in range(2)]
        u_ps = e(nc.psum_tensor("u_ps", [128, 256], F32))
        uT_ps = e(nc.psum_tensor("uT_ps", [128, 16], BF16))

        s_w = Sem(ctx, nc, "s_w")
        s_ra = [[Sem(ctx, nc, f"s_ra{b}_{j}") for j in range(2)] for b in range(2)]
        s_rc = [[Sem(ctx, nc, f"s_rc{b}_{j}") for j in range(2)] for b in range(2)]
        s_so = [[Sem(ctx, nc, f"s_so{b}_{j}") for j in range(2)] for b in range(2)]
        s_mm = Sem(ctx, nc, "s_mm")
        s_ev = Sem(ctx, nc, "s_ev")
        s_tp = Sem(ctx, nc, "s_tp")
        s_sg = Sem(ctx, nc, "s_sg")
        s_sig = Sem(ctx, nc, "s_sig")
        s_d = Sem(ctx, nc, "s_d")
        s_m = Sem(ctx, nc, "s_m")
        s_h = Sem(ctx, nc, "s_h")
        s_cast = Sem(ctx, nc, "s_cast")
        s_stg = Sem(ctx, nc, "s_stg")

        s_i = Sem(ctx, nc, "s_i")
        p = Prog()
        p.dma("sync", WTs[:], WT_in.rearrange("p (k n) -> p k n", k=8), inc=s_w)
        p.dma("sync", hTf[:], h0_in, inc=s_w)
        p.dma("sync", identf[:], ident_in, inc=s_w)
        p.run("scalar", lambda E: E.activation(out=identb[:], in_=identf[:], func=AF.Copy),
              deps=(s_w,), inc=s_i)
        p.run("scalar", lambda E: E.activation(out=hTb[:], in_=hTf[:], func=AF.Copy),
              inc=s_cast)
        p.wait("tensor", s_i)
        p.wait("vector", s_i)

        for ch in range(2):
            for b in range(2):
                p.dma("sync", aring[b][ch][:],
                      a_pl[b][:, ch * CH:(ch + 1) * CH, :], inc=s_ra[b][ch])
                p.dma("sync", cring[b][ch][:],
                      c_pl[b][:, ch * CH:(ch + 1) * CH, :], inc=s_rc[b][ch])

        for t in range(nsteps):
            ch = t // CH
            sl = ch % 2
            tt = t % CH

            if tt == 0:
                ldv = 16 * (ch // 2 + 1)
                for b in range(2):
                    p.wait("tensor", s_ra[b][sl].h, ldv)
                    p.wait("tensor", s_rc[b][sl].h, ldv)
                    p.wait("vector", s_ra[b][sl].h, ldv)
                    p.wait("vector", s_rc[b][sl].h, ldv)
                if ch >= 2:
                    sov = 16 * (ch // 2)
                    p.wait("gpsimd", s_so[0][sl].h, sov)
                    p.wait("gpsimd", s_so[1][sl].h, sov)
                nch_pre = ch + 2
                if nch_pre < nch:
                    p.wait("sync", s_h, nch_pre * CH - CH)
                    for b in range(2):
                        p.dma("sync", aring[b][nch_pre % 2][:],
                              a_pl[b][:, nch_pre * CH:(nch_pre + 1) * CH, :],
                              inc=s_ra[b][nch_pre % 2])
                        p.dma("sync", cring[b][nch_pre % 2][:],
                              c_pl[b][:, nch_pre * CH:(nch_pre + 1) * CH, :],
                              inc=s_rc[b][nch_pre % 2])

            p.wait("tensor", s_cast, t + 1)
            if t > 0:
                p.wait("tensor", s_ev, 4 * t)
            mms = []
            for k in range(8):
                for c4 in range(4):
                    mms.append(lambda E, k=k, c4=c4:
                               E.matmul(u_ps[32 * c4:32 * c4 + 2, 0:256],
                                        hTb[:, 2 * k:2 * k + 2],
                                        WTs[:, k, 256 * c4:256 * c4 + 256],
                                        start=(k == 0), stop=(k == 7),
                                        tile_position=(0, 32 * c4)))
            p.run("tensor", mms, inc=s_mm)

            p.wait("vector", s_cast, t + 1)
            p.run("vector",
                  [(lambda E, b=b, sl=sl, tt=tt:
                    E.tensor_sub(d_sb[:, b:16:2], aring[b][sl][:, tt, :],
                                 hTf[:, b:16:2]))
                   for b in range(2)], inc=s_d, inc_n=2)

            p.wait("vector", s_mm, t + 1)
            p.run("vector", lambda E: E.tensor_copy(u_sb[0:2, 0:256], u_ps[0:2, 0:256]),
                  inc=s_ev)
            p.run("vector", lambda E: E.tensor_copy(u_sb[0:2, 256:512], u_ps[32:34, 0:256]),
                  inc=s_ev)
            p.wait("scalar", s_mm, t + 1)
            p.run("scalar", lambda E: E.copy(u_sb[0:2, 512:768], u_ps[64:66, 0:256]),
                  inc=s_ev)
            p.run("scalar", lambda E: E.copy(u_sb[0:2, 768:1024], u_ps[96:98, 0:256]),
                  inc=s_ev)

            p.wait("tensor", s_ev, 4 * (t + 1))
            if t > 0:
                p.wait("tensor", s_sg, 2 * t)
            tps = [(lambda E, c=c:
                    E.transpose(uT_ps[:, 2 * c:2 * c + 2],
                                u_sb[0:2, 128 * c:128 * c + 128],
                                identb[0:2, 0:2]))
                   for c in range(8)]
            p.run("tensor", tps, inc=s_tp)

            p.wait("vector", s_tp, t + 1)
            p.run("vector",
                  [(lambda E, b=b, sl=sl, tt=tt:
                    E.tensor_add(s_sb[:, b:16:2], uT_ps[:, b:16:2],
                                 cring[b][sl][:, tt, :]))
                   for b in range(2)], inc=s_sg, inc_n=2)

            p.wait("scalar", s_sg, 2 * (t + 1))
            p.run("scalar", lambda E: E.activation(out=g_sb[:], in_=s_sb[:],
                                                   func=AF.Sigmoid), inc=s_sig)

            p.wait("vector", s_sig, t + 1)
            p.wait("vector", s_d, 2 * (t + 1))
            p.run("vector", lambda E: E.tensor_mul(m_sb[:], g_sb[:], d_sb[:]), inc=s_m)
            p.wait("vector", s_m, t + 1)
            if t > 0:
                p.wait("vector", s_stg, 2 * t)
            p.run("vector", lambda E: E.tensor_add(hTf[:], hTf[:], m_sb[:]), inc=s_h)

            p.wait("scalar", s_h, t + 1)
            p.run("scalar", lambda E: E.activation(out=hTb[:], in_=hTf[:], func=AF.Copy),
                  inc=s_cast)
            p.wait("gpsimd", s_h, t + 1)
            p.run("gpsimd",
                  [(lambda E, b=b, sl=sl, tt=tt:
                    E.tensor_copy(stg[b][sl][:, tt, :], hTf[:, b:16:2]))
                   for b in range(2)], inc=s_stg, inc_n=2)

            if tt == CH - 1:
                p.wait("sync", s_stg, 2 * (t + 1))
                for b in range(2):
                    p.dma("sync", hs_pl[b][:, ch * CH:(ch + 1) * CH, :],
                          stg[b][sl][:], inc=s_so[b][sl])

        p.wait("sync", s_h, nsteps)
        p.dma("sync", hf_out, hTf[:], inc=s_w)
        for b in range(2):
            for j in range(2):
                p.wait("sync", s_so[b][j].h, s_so[b][j].v)
        p.wait("sync", s_w, s_w.v)
        p.build(nc)
    return nc



def host_wT(W):
    return np.ascontiguousarray(
        W.T.reshape(8, 128, 8, 128).transpose(1, 0, 2, 3).reshape(128, 8 * 8 * 128))


def host_colvec(v):
    return np.ascontiguousarray(v.reshape(8, 128).T)


def emit_dense8(p, nc, lhsT_of, rhs_of, ps, epi, s_mm, s_ep, nk=8,
                epi_eng="scalar", tg=128, extra=None):
    eps_after = []
    for m in range(8):
        off = (m % 2) * 512
        if m >= 2:
            p.wait("tensor", s_ep, eps_after[m - 2])
        else:
            p.wait("tensor", s_ep)
        mms = [(lambda E, m=m, k=k, off=off:
                E.matmul(ps[:, off:off + tg], lhsT_of(m, k), rhs_of(k),
                         start=(k == 0), stop=(k == nk - 1 and extra is None)))
               for k in range(nk)]
        if extra is not None:
            mms += [(lambda E, m=m, k=k, off=off:
                     E.matmul(ps[:, off:off + tg], *extra(m, k),
                              start=False, stop=(k == nk - 1)))
                    for k in range(nk)]
        p.run("tensor", mms, inc=s_mm)
        p.wait(epi_eng, s_mm)
        epi(m, ps[:, off:off + tg])
        eps_after.append(s_ep.v)


def build_a0pre(ntok=TPC):
    nc = bass.Bass("TRN2", target_bir_lowering=False, debug=False)
    x_in = nc.dram_tensor("x", [ntok, D], F32, kind="ExternalInput").ap()
    wiT = nc.dram_tensor("wiT", [128, 8192], BF16, kind="ExternalInput").ap()
    wifT = nc.dram_tensor("wifT", [128, 8192], BF16, kind="ExternalInput").ap()
    gam_in = nc.dram_tensor("gam", [1024], F32, kind="ExternalInput")
    lb_in = nc.dram_tensor("lb", [128, 8], F32, kind="ExternalInput").ap()
    ident_in = nc.dram_tensor("identin", [128, 128], F32, kind="ExternalInput").ap()
    xn_out = nc.dram_tensor("xn", [ntok, D], F32, kind="ExternalOutput").ap()
    a_out = nc.dram_tensor("aop", [128, ntok, 8], F32, kind="ExternalOutput").ap()
    c_out = nc.dram_tensor("cop", [128, ntok, 8], F32, kind="ExternalOutput").ap()

    nt = ntok // 128
    with ExitStack() as ctx:
        e = ctx.enter_context
        wi = e(nc.sbuf_tensor("wi", [128, 8, 8, 128], BF16))
        wif = e(nc.sbuf_tensor("wif", [128, 8, 8, 128], BF16))
        identf = e(nc.sbuf_tensor("identf", [128, 128], F32))
        identb = e(nc.sbuf_tensor("identb", [128, 128], BF16))
        gamr = e(nc.sbuf_tensor("gamr", [128, 1024], F32))
        lb = e(nc.sbuf_tensor("lbs", [128, 8], F32))
        xt = [e(nc.sbuf_tensor(f"xt{i}", [128, D], F32)) for i in range(2)]
        sq = e(nc.sbuf_tensor("sq", [128, D], F32))
        sums = e(nc.sbuf_tensor("sums", [128, 4], F32))
        mu = e(nc.sbuf_tensor("mu", [128, 1], F32))
        rstd = e(nc.sbuf_tensor("rstd", [128, 1], F32))
        lno = e(nc.sbuf_tensor("lno", [128, D], F32))
        xn_f = [e(nc.sbuf_tensor(f"xnf{i}", [128, D], F32)) for i in range(2)]
        xn_b = e(nc.sbuf_tensor("xnb", [128, D], BF16))
        xnT = e(nc.sbuf_tensor("xnT", [128, 8, 128], BF16))
        tmp_a = e(nc.sbuf_tensor("tmpa", [128, 128], F32))
        astg = [e(nc.sbuf_tensor(f"astg{i}", [128, 128, 8], F32)) for i in range(2)]
        cstg = [e(nc.sbuf_tensor(f"cstg{i}", [128, 128, 8], F32)) for i in range(2)]
        ps_tp = e(nc.psum_tensor("ps_tp", [128, 1024], F32))
        ps_mm = e(nc.psum_tensor("ps_mm", [128, 1024], F32))

        def tpb(k):
            return ps_tp[:].bitcast(BF16)[:, k * 128:(k + 1) * 128]

        s_w = Sem(ctx, nc, "s_w")
        s_x = [Sem(ctx, nc, f"s_x{j}") for j in range(2)]
        s_a = Sem(ctx, nc, "s_a")
        s_v = Sem(ctx, nc, "s_v")
        s_tp = Sem(ctx, nc, "s_tp")
        s_ev = Sem(ctx, nc, "s_ev")
        s_mm = Sem(ctx, nc, "s_mm")
        s_ep = Sem(ctx, nc, "s_ep")
        s_st = [Sem(ctx, nc, f"s_st{j}") for j in range(2)]
        s_stn = [Sem(ctx, nc, f"s_stn{j}") for j in range(2)]
        cst = Consts(ctx, nc, [1e-5])
        s_c = Sem(ctx, nc, "s_c")

        p = Prog()
        cst.emit(p, s_c)
        p.dma("sync", wi[:], wiT.rearrange("p (a b m) -> p a b m", a=8, b=8), inc=s_w)
        p.dma("sync", wif[:], wifT.rearrange("p (a b m) -> p a b m", a=8, b=8), inc=s_w)
        p.dma("sync", identf[:], ident_in, inc=s_w)
        p.dma("sync", gamr[:], bass.AP(gam_in, 0, [[0, 128], [1, 1024]]), inc=s_w)
        p.dma("sync", lb[:], lb_in, inc=s_w)
        p.run("scalar", lambda E: E.activation(out=identb[:], in_=identf[:], func=AF.Copy),
              deps=(s_w, s_c), inc=s_a)
        p.run("vector", lambda E: E.tensor_scalar(gamr[:], gamr[:], 1.0, float(np.sqrt(D)),
                                                  ALU.add, ALU.mult),
              deps=(s_w, s_c), inc=s_v)
        for eng in ("tensor", "vector", "scalar"):
            p.wait(eng, s_w)
            p.wait(eng, s_a)
            p.wait(eng, s_v)

        for i in range(nt):
            j = i % 2
            if i >= 2:
                p.wait("sync", s_stn[j].h, 16 * (i // 2))
            p.dma("sync", xt[j][:], x_in[i * 128:(i + 1) * 128, :], inc=s_x[j])
            p.wait("scalar", s_x[j].h, 16 * (i // 2 + 1))
            p.wait("vector", s_x[j].h, 16 * (i // 2 + 1))
            p.wait("scalar", s_v)
            emit_ln_tile(p, i, xt[j][:], None, sq, sums, mu, rstd, s_a, s_v,
                         out_f32=lno[:])
            p.run("scalar", lambda E: E.activation(out=sq[:], in_=lno[:], func=AF.Square),
                  deps=(s_v,), inc=s_a)
            p.run("vector", [lambda E: E.reduce_sum(sums[:, 0:1], sq[:], axis=AX.X)],
                  deps=(s_a,), inc=s_v)
            p.run("scalar", lambda E: E.activation(out=sums[:, 1:2], in_=sums[:, 0:1],
                                                   func=AF.Sqrt), deps=(s_v,), inc=s_a)
            p.run("vector", [lambda E: E.tensor_single_scalar(sums[:, 1:2], sums[:, 1:2],
                                                              1e-12, ALU.max)],
                  deps=(s_a,), inc=s_v)
            p.run("vector", [lambda E: E.reciprocal(rstd[:], sums[:, 1:2])],
                  deps=(s_v,), inc=s_v)
            p.run("vector", [lambda E, j=j: E.tensor_scalar(xn_f[j][:], lno[:], rstd[:],
                                                            None, ALU.mult)],
                  deps=(s_v,), inc=s_v)
            if i >= 2:
                p.wait("vector", s_stn[j].h, 16 * (i // 2))
            p.run("vector", [lambda E, j=j: E.tensor_mul(xn_f[j][:], xn_f[j][:], gamr[:])],
                  deps=(s_v,), inc=s_v)
            p.run("vector", [lambda E, j=j: E.tensor_copy(xn_b[:], xn_f[j][:])],
                  deps=(s_v,), inc=s_v)
            p.dma("sync", xn_out[i * 128:(i + 1) * 128, :], xn_f[j][:],
                  deps=(s_v,), inc=s_stn[j])
            p.wait("vector", s_mm)
            emit_tp_in(p, xn_b, (lambda k: xnT[:, k, :]), identb, tpb, s_tp, s_ev,
                       deps=(s_v,))
            p.wait("tensor", s_ev)
            if i >= 2:
                p.wait("scalar", s_st[j].h, 16 * 2 * (i // 2))

            def epi_a(m, reg, j=j):
                p.run("scalar", (lambda E, reg=reg: E.activation(out=tmp_a[:], in_=reg,
                                                                 func=AF.Tanh)),
                      deps=(s_ep,), inc=s_ep)
                p.run("scalar", (lambda E, m=m, j=j: E.activation(
                    out=astg[j][:, :, m], in_=tmp_a[:], func=AF.Tanh)),
                    deps=(s_ep,), inc=s_ep)

            emit_dense8(p, nc, (lambda m, k: wi[:, k, m, :]),
                        (lambda k: xnT[:, k, :]), ps_mm, epi_a, s_mm, s_ep)

            def epi_c(m, reg, j=j):
                p.run("scalar", (lambda E, m=m, reg=reg, j=j: E.activation(
                    out=cstg[j][:, :, m], in_=reg, func=AF.Identity, bias=lb[:, m:m + 1])),
                    inc=s_ep)

            emit_dense8(p, nc, (lambda m, k: wif[:, k, m, :]),
                        (lambda k: xnT[:, k, :]), ps_mm, epi_c, s_mm, s_ep)
            p.dma("sync", a_out[:, i * 128:(i + 1) * 128, :], astg[j][:],
                  deps=(s_ep,), inc=s_st[j])
            p.dma("sync", c_out[:, i * 128:(i + 1) * 128, :], cstg[j][:],
                  deps=(s_ep,), inc=s_st[j])
        for j in range(2):
            p.wait("sync", s_st[j].h, s_st[j].v)
            p.wait("sync", s_stn[j].h, s_stn[j].v)
        p.build(nc)
    return nc


def build_a0mid(ntok=TPC):
    nc = bass.Bass("TRN2", target_bir_lowering=False, debug=False)
    hs_in = nc.dram_tensor("hsp", [128, ntok, 8], F32, kind="ExternalInput").ap()
    wiT = nc.dram_tensor("wiT", [128, 8192], BF16, kind="ExternalInput").ap()
    wifT = nc.dram_tensor("wifT", [128, 8192], BF16, kind="ExternalInput").ap()
    lb_in = nc.dram_tensor("lb", [128, 8], F32, kind="ExternalInput").ap()
    a_out = nc.dram_tensor("aop", [128, ntok, 8], F32, kind="ExternalOutput").ap()
    c_out = nc.dram_tensor("cop", [128, ntok, 8], F32, kind="ExternalOutput").ap()

    nt = ntok // 128
    with ExitStack() as ctx:
        e = ctx.enter_context
        wi = e(nc.sbuf_tensor("wi", [128, 8, 8, 128], BF16))
        wif = e(nc.sbuf_tensor("wif", [128, 8, 8, 128], BF16))
        lb = e(nc.sbuf_tensor("lbs", [128, 8], F32))
        ht = [e(nc.sbuf_tensor(f"ht{i}", [128, 128, 8], F32)) for i in range(2)]
        hT = e(nc.sbuf_tensor("hT", [128, 8, 128], BF16))
        tmp_a = e(nc.sbuf_tensor("tmpa", [128, 128], F32))
        astg = [e(nc.sbuf_tensor(f"astg{i}", [128, 128, 8], F32)) for i in range(2)]
        cstg = [e(nc.sbuf_tensor(f"cstg{i}", [128, 128, 8], F32)) for i in range(2)]
        ps_mm = e(nc.psum_tensor("ps_mm", [128, 1024], F32))

        s_w = Sem(ctx, nc, "s_w")
        s_x = [Sem(ctx, nc, f"s_x{j}") for j in range(2)]
        s_v = Sem(ctx, nc, "s_v")
        s_mm = Sem(ctx, nc, "s_mm")
        s_ep = Sem(ctx, nc, "s_ep")
        s_st = [Sem(ctx, nc, f"s_st{j}") for j in range(2)]

        p = Prog()
        p.dma("sync", wi[:], wiT.rearrange("p (a b m) -> p a b m", a=8, b=8), inc=s_w)
        p.dma("sync", wif[:], wifT.rearrange("p (a b m) -> p a b m", a=8, b=8), inc=s_w)
        p.dma("sync", lb[:], lb_in, inc=s_w)
        for eng in ("tensor", "vector", "scalar"):
            p.wait(eng, s_w)

        for i in range(nt):
            j = i % 2
            p.dma("sync", ht[j][:], hs_in[:, i * 128:(i + 1) * 128, :], inc=s_x[j])
            p.wait("vector", s_x[j].h, 16 * (i // 2 + 1))
            p.wait("vector", s_mm)
            p.run("vector", [(lambda E, k=k, j=j: E.tensor_copy(hT[:, k, :], ht[j][:, :, k]))
                             for k in range(8)], inc=s_v)
            p.wait("tensor", s_v)
            if i >= 2:
                p.wait("scalar", s_st[j].h, 16 * 2 * (i // 2))

            def epi_a(m, reg, j=j):
                p.run("scalar", (lambda E, reg=reg: E.activation(out=tmp_a[:], in_=reg,
                                                                 func=AF.Tanh)),
                      deps=(s_ep,), inc=s_ep)
                p.run("scalar", (lambda E, m=m, j=j: E.activation(
                    out=astg[j][:, :, m], in_=tmp_a[:], func=AF.Tanh)),
                    deps=(s_ep,), inc=s_ep)

            emit_dense8(p, nc, (lambda m, k: wi[:, k, m, :]),
                        (lambda k: hT[:, k, :]), ps_mm, epi_a, s_mm, s_ep)

            def epi_c(m, reg, j=j):
                p.run("scalar", (lambda E, m=m, reg=reg, j=j: E.activation(
                    out=cstg[j][:, :, m], in_=reg, func=AF.Identity, bias=lb[:, m:m + 1])),
                    deps=(s_ep,), inc=s_ep)

            emit_dense8(p, nc, (lambda m, k: wif[:, k, m, :]),
                        (lambda k: hT[:, k, :]), ps_mm, epi_c, s_mm, s_ep)
            p.dma("sync", a_out[:, i * 128:(i + 1) * 128, :], astg[j][:],
                  deps=(s_ep,), inc=s_st[j])
            p.dma("sync", c_out[:, i * 128:(i + 1) * 128, :], cstg[j][:],
                  deps=(s_ep,), inc=s_st[j])
        for j in range(2):
            p.wait("sync", s_st[j].h, s_st[j].v)
        p.build(nc)
    return nc


def build_a0post(ntok=TPC):
    nc = bass.Bass("TRN2", target_bir_lowering=False, debug=False)
    x_in = nc.dram_tensor("x", [ntok, D], F32, kind="ExternalInput").ap()
    xn_in = nc.dram_tensor("xnin", [ntok, D], F32, kind="ExternalInput").ap()
    hs_in = nc.dram_tensor("hsp", [128, ntok, 8], F32, kind="ExternalInput").ap()
    wiT = nc.dram_tensor("wiT", [128, 8192], BF16, kind="ExternalInput").ap()
    wifT = nc.dram_tensor("wifT", [128, 8192], BF16, kind="ExternalInput").ap()
    whfT = nc.dram_tensor("whfT", [128, 8192], BF16, kind="ExternalInput").ap()
    gb_in = nc.dram_tensor("gb", [128, 8], F32, kind="ExternalInput").ap()
    ident_in = nc.dram_tensor("identin", [128, 128], F32, kind="ExternalInput").ap()
    x_out = nc.dram_tensor("xo", [ntok, D], F32, kind="ExternalOutput").ap()

    nt = ntok // 128
    with ExitStack() as ctx:
        e = ctx.enter_context
        wi = e(nc.sbuf_tensor("wi", [128, 8, 8, 128], BF16))
        wif = e(nc.sbuf_tensor("wif", [128, 8, 8, 128], BF16))
        whf = e(nc.sbuf_tensor("whf", [128, 8, 8, 128], BF16))
        identf = e(nc.sbuf_tensor("identf", [128, 128], F32))
        identb = e(nc.sbuf_tensor("identb", [128, 128], BF16))
        gb = e(nc.sbuf_tensor("gbs", [128, 8], F32))
        xt = [e(nc.sbuf_tensor(f"xt{i}", [128, D], F32)) for i in range(2)]
        xnt = [e(nc.sbuf_tensor(f"xnt{i}", [128, D], F32)) for i in range(2)]
        ht = [e(nc.sbuf_tensor(f"ht{i}", [128, 128, 8], F32)) for i in range(2)]
        hT = e(nc.sbuf_tensor("hT", [128, 8, 128], BF16))
        sq = e(nc.sbuf_tensor("sq", [128, D], F32))
        sums = e(nc.sbuf_tensor("sums", [128, 4], F32))
        mu = e(nc.sbuf_tensor("mu", [128, 1], F32))
        rstd = e(nc.sbuf_tensor("rstd", [128, 1], F32))
        lno = e(nc.sbuf_tensor("lno", [128, D], F32))
        xnb = e(nc.sbuf_tensor("xnb", [128, D], BF16))
        xnT_b = e(nc.sbuf_tensor("xnTb", [128, 8, 128], BF16))
        xnT_f = e(nc.sbuf_tensor("xnTf", [128, 8, 128], F32))
        tni = e(nc.sbuf_tensor("tni", [128, 8, 128], F32))
        gsb = e(nc.sbuf_tensor("gsb", [128, 8, 128], F32))
        yT = e(nc.sbuf_tensor("yT", [128, 8, 128], F32))
        fin = [e(nc.sbuf_tensor(f"fin{i}", [128, D], F32)) for i in range(2)]
        ps_tp = e(nc.psum_tensor("ps_tp", [128, 1024], F32))
        ps_mm = e(nc.psum_tensor("ps_mm", [128, 1024], F32))

        def tpb(k):
            return ps_tp[:].bitcast(BF16)[:, k * 128:(k + 1) * 128]

        s_w = Sem(ctx, nc, "s_w")
        s_x = [Sem(ctx, nc, f"s_x{j}") for j in range(2)]
        s_a = Sem(ctx, nc, "s_a")
        s_v = Sem(ctx, nc, "s_v")
        s_tp = Sem(ctx, nc, "s_tp")
        s_ev = Sem(ctx, nc, "s_ev")
        s_mm = Sem(ctx, nc, "s_mm")
        s_ep = Sem(ctx, nc, "s_ep")
        s_st = [Sem(ctx, nc, f"s_st{j}") for j in range(2)]
        cst = Consts(ctx, nc, [1e-5])
        s_c = Sem(ctx, nc, "s_c")

        p = Prog()
        cst.emit(p, s_c)
        p.dma("sync", wi[:], wiT.rearrange("p (a b m) -> p a b m", a=8, b=8), inc=s_w)
        p.dma("sync", wif[:], wifT.rearrange("p (a b m) -> p a b m", a=8, b=8), inc=s_w)
        p.dma("sync", whf[:], whfT.rearrange("p (a b m) -> p a b m", a=8, b=8), inc=s_w)
        p.dma("sync", identf[:], ident_in, inc=s_w)
        p.dma("sync", gb[:], gb_in, inc=s_w)
        p.run("scalar", lambda E: E.activation(out=identb[:], in_=identf[:], func=AF.Copy),
              deps=(s_w, s_c), inc=s_a)
        for eng in ("tensor", "vector", "scalar"):
            p.wait(eng, s_w)
            p.wait(eng, s_a)

        for i in range(nt):
            j = i % 2
            if i >= 2:
                p.wait("sync", s_st[j].h, 16 * (i // 2))
            p.dma("sync", xt[j][:], x_in[i * 128:(i + 1) * 128, :], inc=s_x[j])
            p.dma("sync", xnt[j][:], xn_in[i * 128:(i + 1) * 128, :], inc=s_x[j])
            p.dma("sync", ht[j][:], hs_in[:, i * 128:(i + 1) * 128, :], inc=s_x[j])
            p.wait("scalar", s_x[j].h, 48 * (i // 2 + 1))
            p.wait("vector", s_x[j].h, 48 * (i // 2 + 1))
            p.wait("vector", s_mm)
            p.run("vector", [(lambda E, k=k, j=j: E.tensor_copy(hT[:, k, :], ht[j][:, :, k]))
                             for k in range(8)], inc=s_v)
            p.run("vector", [(lambda E, j=j: E.tensor_copy(xnb[:], xnt[j][:]))],
                  deps=(s_v,), inc=s_v)
            emit_tp_in(p, xnb, (lambda k: xnT_b[:, k, :]), identb, tpb, s_tp, s_ev,
                       deps=(s_v,))
            tps = [(lambda E, k=k, j=j: E.transpose(
                ps_tp[:, (k % 4) * 128 + (k // 4) * 512:(k % 4) * 128 + (k // 4) * 512 + 128],
                xnt[j][0:128, k * 128:(k + 1) * 128], identf[:]))
                   for k in range(8)]
            p.run("tensor", tps, deps=(s_ev,), inc=s_tp)
            p.run("vector", [(lambda E, k=k: E.tensor_copy(
                xnT_f[:, k, :],
                ps_tp[:, (k % 4) * 128 + (k // 4) * 512:(k % 4) * 128 + (k // 4) * 512 + 128]))
                for k in range(8)], deps=(s_tp,), inc=s_ev)
            p.wait("tensor", s_ev)

            p.wait("scalar", s_v)

            def epi_ni(m, reg):
                p.run("scalar", (lambda E, m=m, reg=reg: E.activation(
                    out=tni[:, m, :], in_=reg, func=AF.Tanh)), deps=(s_ep,), inc=s_ep)
                p.run("scalar", (lambda E, m=m: E.activation(
                    out=tni[:, m, :], in_=tni[:, m, :], func=AF.Tanh)),
                    deps=(s_ep,), inc=s_ep)

            emit_dense8(p, nc, (lambda m, k: wi[:, k, m, :]),
                        (lambda k: hT[:, k, :]), ps_mm, epi_ni, s_mm, s_ep)

            def epi_g(m, reg):
                p.run("scalar", (lambda E, m=m, reg=reg: E.activation(
                    out=gsb[:, m, :], in_=reg, func=AF.Sigmoid, bias=gb[:, m:m + 1])),
                    deps=(s_ep,), inc=s_ep)

            emit_dense8(p, nc, (lambda m, k: whf[:, k, m, :]),
                        (lambda k: xnT_b[:, k, :]), ps_mm, epi_g, s_mm, s_ep,
                        extra=(lambda m, k: (wif[:, k, m, :], hT[:, k, :])))

            p.run("vector", [(lambda E: E.tensor_sub(tni[:], tni[:], xnT_f[:]))],
                  deps=(s_ep,), inc=s_v)
            p.run("vector", [(lambda E: E.tensor_mul(tni[:], gsb[:], tni[:]))],
                  deps=(s_v,), inc=s_v)
            p.run("vector", [(lambda E: E.tensor_add(yT[:], xnT_f[:], tni[:]))],
                  deps=(s_v,), inc=s_v)
            p.wait("scalar", s_v)
            emit_ln_tile(p, i, xt[j][:], None, sq, sums, mu, rstd, s_a, s_v,
                         out_f32=lno[:])
            p.run("vector", [(lambda E, j=j: E.tensor_add(lno[:], lno[:], xt[j][:]))],
                  deps=(s_v,), inc=s_v)
            for half in range(2):
                ks = range(half * 4, half * 4 + 4)
                p.wait("tensor", s_v)
                tps = [(lambda E, k=k, half=half: E.transpose(
                    ps_tp[:, half * 512 + (k % 4) * 128: half * 512 + (k % 4) * 128 + 128],
                    yT[:, k, :], identf[:])) for k in ks]
                p.run("tensor", tps, inc=s_tp)
                p.run("vector",
                      [(lambda E, k=k, half=half, j=j:
                        E.scalar_tensor_tensor(
                            fin[j][:, k * 128:(k + 1) * 128],
                            ps_tp[:, half * 512 + (k % 4) * 128:
                                  half * 512 + (k % 4) * 128 + 128], 1.0,
                            lno[:, k * 128:(k + 1) * 128], ALU.mult, ALU.add))
                       for k in ks], deps=(s_tp,), inc=s_v)
            p.dma("sync", x_out[i * 128:(i + 1) * 128, :], fin[j][:],
                  deps=(s_v,), inc=s_st[j])
        for j in range(2):
            p.wait("sync", s_st[j].h, s_st[j].v)
        p.build(nc)
    return nc



def _launch(nc, in_maps):
    return run_bass_kernel_spmd(nc, in_maps, core_ids=list(range(NCORES))).results


def _shard_tok(x):
    return [np.ascontiguousarray(x[i * TPC:(i + 1) * TPC]) for i in range(NCORES)]


def _ident():
    return np.eye(128, dtype=np.float32)


def h0_layout(v):
    out = np.zeros((128, 16), np.float32)
    for k in range(8):
        out[:, 2 * k] = v[k * 128:(k + 1) * 128]
        out[:, 2 * k + 1] = v[k * 128:(k + 1) * 128]
    return out


def run_f0(x, w1b, w2b, wcb, launch=_launch):
    nc = _cached(("f0",), lambda: build_f0())
    ins = [{"x": s, "w1bd": w1b, "w2bd": w2b, "wcT": wcb, "identin": _ident()}
           for s in _shard_tok(x)]
    res = launch(nc, ins)
    return np.concatenate([r["xo"] for r in res], axis=0)


def run_a0(x, W, launch=_launch):
    ident = _ident()
    nc = _cached(("a0pre",), lambda: build_a0pre())
    ins = [{"x": s, "wiT": W["lWiT0"], "wifT": W["lWifT0"], "gam": W["gamma"],
            "lb": W["lbhf0"], "identin": ident} for s in _shard_tok(x)]
    res = launch(nc, ins)
    xn = np.concatenate([r["xn"] for r in res], axis=0)
    a_pl = [np.concatenate([res[b * 4 + i]["aop"] for i in range(4)], axis=1)
            for b in range(2)]
    c_pl = [np.concatenate([res[b * 4 + i]["cop"] for i in range(4)], axis=1)
            for b in range(2)]

    nc = _cached(("a0mega",), lambda: build_a0mega(nsteps=8192, CH=128))
    im = {"WTA": W["WhfS0"], "WTB": W["WhfS1"],
          "h0A": h0_layout(W["l_init0"]), "h0B": h0_layout(W["l_init1"]),
          "wiT": W["lWiT1"], "wifT": W["lWifT1"], "lb": W["lbhf1"],
          "identin": ident}
    for b in range(2):
        im[f"a{b}"] = np.ascontiguousarray(a_pl[b])
        im[f"c{b}"] = np.ascontiguousarray(c_pl[b])
    res = launch(nc, [im] * NCORES)
    hs2 = [res[0][f"hs{b}"] for b in range(2)]

    nc = _cached(("a0post",), lambda: build_a0post())
    ins = [{"x": _shard_tok(x)[c], "xnin": _shard_tok(xn)[c],
            "hsp": np.ascontiguousarray(hs2[c // 4][:, (c % 4) * TPC:(c % 4 + 1) * TPC, :]),
            "wiT": W["gWiT"], "wifT": W["gWifT"], "whfT": W["gWhfT"],
            "gb": W["gbhf"], "identin": ident} for c in range(NCORES)]
    res = launch(nc, ins)
    return np.concatenate([r["xo"] for r in res], axis=0)


def host_scanWT(Whf):
    return np.ascontiguousarray(
        Whf.T.reshape(8, 128, 1024).transpose(1, 0, 2).reshape(128, 8192))


def prep_weights(inp, launch=_launch):
    W = {}

    def cast(name, arr):
        W[name] = cast_bf16_dev(arr, launch)

    w1, w2, wcT = f0_weights_host(inp["L0_ffn_rail_maker"],
                                  inp["L0_ffn_rail_up_proj"], inp["L0_ffn_w_cproj"])
    cast("f0w1", w1); cast("f0w2", w2); cast("f0wcT", wcT)
    cast("lWiT0", host_wT(inp["L0_attn_lWi"][0]))
    cast("lWifT0", host_wT(inp["L0_attn_lWif"][0]))
    cast("lWiT1", host_wT(inp["L0_attn_lWi"][1]))
    cast("lWifT1", host_wT(inp["L0_attn_lWif"][1]))
    cast("gWiT", host_wT(inp["L0_attn_gWi"]))
    cast("gWifT", host_wT(inp["L0_attn_gWif"]))
    cast("gWhfT", host_wT(inp["L0_attn_gWhf"]))
    cast("WhfS0", host_scanWT(inp["L0_attn_lWhf"][0]))
    cast("WhfS1", host_scanWT(inp["L0_attn_lWhf"][1]))
    W["gamma"] = np.ascontiguousarray(inp["L0_attn_gamma"], np.float32)
    W["lbhf0"] = host_colvec(inp["L0_attn_lbhf"][0])
    W["lbhf1"] = host_colvec(inp["L0_attn_lbhf"][1])
    W["gbhf"] = host_colvec(inp["L0_attn_gbhf"])
    W["l_init0"] = np.ascontiguousarray(inp["L0_attn_l_init_hidden"][0], np.float32)
    W["l_init1"] = np.ascontiguousarray(inp["L0_attn_l_init_hidden"][1], np.float32)
    return W


def prep_weights2(inp, launch=_launch):
    W = prep_weights(inp, launch)
    Wt = [None, None]
    for l in range(2):
        Wl = t1l_weights_host(inp, l)
        for k in ("wqkv", "wo", "w1", "w2"):
            Wl[k] = cast_bf16_dev(Wl[k], launch)
        Wt[l] = Wl
    W["t1l"] = Wt
    W["WqT"] = cast_bf16_dev(host_swapT(inp["L1_attn_Wq"]), launch)
    W["WkT"] = cast_bf16_dev(host_swapT(inp["L1_attn_Wk"]), launch)
    W["WvT"] = cast_bf16_dev(host_swapT(inp["L1_attn_Wv"]), launch)
    W["WprojT"] = cast_bf16_dev(host_wT(inp["L1_attn_Wproj"]), launch)
    W["lq1"] = np.asarray(inp["L1_attn_lambda_q1"], np.float32)
    W["lk1"] = np.asarray(inp["L1_attn_lambda_k1"], np.float32)
    W["lq2"] = np.asarray(inp["L1_attn_lambda_q2"], np.float32)
    W["lk2"] = np.asarray(inp["L1_attn_lambda_k2"], np.float32)
    return W


def kernel(**inp):
    x = np.ascontiguousarray(np.asarray(inp["L0_attn_x"], np.float32).reshape(T_FULL, D))
    W = prep_weights2(inp)
    x = run_f0(x, W["f0w1"], W["f0w2"], W["f0wcT"])
    x = run_a0(x, W)
    x = run_f0(x, W["f0w1"], W["f0w2"], W["f0wcT"])
    x = run_l1ffn_macaron(x, W["t1l"][0], W["t1l"][1])
    x = run_l1attn(x, W)
    x = run_l1ffn_macaron(x, W["t1l"][0], W["t1l"][1])
    return x.reshape(2, 8192, 1024)



HALO = 640


def build_t1l_attn(ntok=TPC):
    N = ntok + HALO
    nt = N // 128
    NQB = ntok // 512
    nc = bass.Bass("TRN2", target_bir_lowering=False, debug=False)
    z2 = nc.dram_tensor("z2", [N + 1, D], F32, kind="ExternalInput").ap()
    wq_in = nc.dram_tensor("wqkv", [128, 8 * 1536], BF16, kind="ExternalInput").ap()
    wo_in = nc.dram_tensor("wo", [128, 4 * 1024], BF16, kind="ExternalInput").ap()
    tab_in = nc.dram_tensor("tabs", [N, 4, 64], F32, kind="ExternalInput").ap()
    qs_in = nc.dram_tensor("qs", [64], F32, kind="ExternalInput")
    ks_in = nc.dram_tensor("ks", [64], F32, kind="ExternalInput")
    mask_in = nc.dram_tensor("mask4", [NQB, 512, 1024], F32, kind="ExternalInput").ap()
    ident_in = nc.dram_tensor("identin", [128, 128], F32, kind="ExternalInput").ap()
    xo = nc.dram_tensor("xo", [ntok, D], F32, kind="ExternalOutput").ap()

    with ExitStack() as ctx:
        e = ctx.enter_context
        wq = e(nc.sbuf_tensor("wq", [128, 8, 1536], BF16))
        wo = e(nc.sbuf_tensor("wos", [128, 4, 1024], BF16))
        identf = e(nc.sbuf_tensor("identf", [128, 128], F32))
        identb = e(nc.sbuf_tensor("identb", [128, 128], BF16))
        qs_r = e(nc.sbuf_tensor("qsr", [128, 64], F32))
        ks_r = e(nc.sbuf_tensor("ksr", [128, 64], F32))
        zt = [e(nc.sbuf_tensor(f"zt{i}", [128, D], F32)) for i in range(2)]
        zs = [e(nc.sbuf_tensor(f"zs{i}", [128, D], F32)) for i in range(2)]
        zb = e(nc.sbuf_tensor("zb", [128, D], BF16))
        zsb = e(nc.sbuf_tensor("zsb", [128, D], BF16))
        hT = e(nc.sbuf_tensor("hTt", [128, 8, 128], BF16))
        tab = [e(nc.sbuf_tensor(f"tab{i}", [128, 4, 64], F32)) for i in range(2)]
        qf = e(nc.sbuf_tensor("qf", [128, 8, 64], F32))
        kf = e(nc.sbuf_tensor("kf", [128, 8, 64], F32))
        rt = e(nc.sbuf_tensor("rt", [128, 8, 64], F32))
        nrm = e(nc.sbuf_tensor("nrm", [128, 8], F32))
        sqs = e(nc.sbuf_tensor("sqs", [128, 8, 64], F32))
        qT = e(nc.sbuf_tensor("qT", [128, 4, N], BF16))
        kT = e(nc.sbuf_tensor("kT", [128, 4, N], BF16))
        vtok = e(nc.sbuf_tensor("vtok", [128, nt, 512], BF16))
        mask = e(nc.sbuf_tensor("maskt", [128, 4, 1024], F32))
        negmax = e(nc.sbuf_tensor("negmax", [128, 1], F32))
        ssum = e(nc.sbuf_tensor("ssum", [128, 1], F32))
        rs = e(nc.sbuf_tensor("rs", [128, 1], F32))
        P = e(nc.sbuf_tensor("P", [128, 1024], BF16))
        PT = e(nc.sbuf_tensor("PT", [128, 8, 128], BF16))
        osb = e(nc.sbuf_tensor("osb", [128, 4, 512], F32))
        ob = e(nc.sbuf_tensor("ob", [128, 512], BF16))
        oT = e(nc.sbuf_tensor("oT", [128, 4, 128], BF16))
        xq = [e(nc.sbuf_tensor(f"xq{i}", [128, D], F32)) for i in range(2)]
        fin = [e(nc.sbuf_tensor(f"fin{i}", [128, D], F32)) for i in range(2)]
        ps_qkv = e(nc.psum_tensor("ps_qkv", [128, 1536], F32))
        ps_s = e(nc.psum_tensor("ps_s", [128, 1024], F32))
        ps_tp = e(nc.psum_tensor("ps_tp", [128, 1024], F32))
        ps_o = e(nc.psum_tensor("ps_o", [128, 128], F32))

        def tpb(k):
            return ps_tp[:].bitcast(BF16)[:, k * 128:(k + 1) * 128]

        s_w = Sem(ctx, nc, "s_w")
        s_z = [Sem(ctx, nc, f"s_z{j}") for j in range(2)]
        s_tb = [Sem(ctx, nc, f"s_tb{j}") for j in range(2)]
        s_m = Sem(ctx, nc, "s_m")
        s_xq = [Sem(ctx, nc, f"s_xq{j}") for j in range(2)]
        s_a = Sem(ctx, nc, "s_a")
        s_v = Sem(ctx, nc, "s_v")
        s_tp = Sem(ctx, nc, "s_tp")
        s_ev = Sem(ctx, nc, "s_ev")
        s_mm = Sem(ctx, nc, "s_mm")
        s_st = [Sem(ctx, nc, f"s_st{j}") for j in range(2)]
        s_zc = Sem(ctx, nc, "s_zc")
        cst = Consts(ctx, nc, [1e-12])
        s_c = Sem(ctx, nc, "s_c")

        p = Prog()
        cst.emit(p, s_c)
        p.dma("sync", wq[:], wq_in.rearrange("p (a m) -> p a m", a=8), inc=s_w)
        p.dma("sync", wo[:], wo_in.rearrange("p (a m) -> p a m", a=4), inc=s_w)
        p.dma("sync", identf[:], ident_in, inc=s_w)
        p.dma("sync", qs_r[:], bass.AP(qs_in, 0, [[0, 128], [1, 64]]), inc=s_w)
        p.dma("sync", ks_r[:], bass.AP(ks_in, 0, [[0, 128], [1, 64]]), inc=s_w)
        p.run("scalar", lambda E: E.activation(out=identb[:], in_=identf[:], func=AF.Copy),
              deps=(s_w, s_c), inc=s_a)
        p.run("vector", [lambda E: E.tensor_single_scalar(qs_r[:], qs_r[:], 8.0, ALU.mult)],
              deps=(s_w,), inc=s_v)
        for eng in ("tensor", "vector", "scalar"):
            p.wait(eng, s_w)
            p.wait(eng, s_a)
            p.wait(eng, s_v)

        def rotary_norm(src_f, scale_rep, tabc_col, tabs_col, dstT, i, sl):
            p.run("scalar", lambda E: E.activation(out=sqs[:], in_=src_f[:], func=AF.Square),
                  deps=(s_v,), inc=s_a)
            p.run("vector", [lambda E: E.tensor_reduce(nrm[:], sqs[:], axis=AX.X, op=ALU.add)],
                  deps=(s_a,), inc=s_v)
            p.run("scalar", lambda E: E.activation(out=nrm[:], in_=nrm[:], func=AF.Sqrt),
                  deps=(s_v,), inc=s_a)
            p.run("vector", [lambda E: E.tensor_single_scalar(nrm[:], nrm[:], 1e-12, ALU.max)],
                  deps=(s_a,), inc=s_v)
            p.run("vector", [lambda E: E.reciprocal(nrm[:], nrm[:])], deps=(s_v,), inc=s_v)
            p.run("vector", [lambda E: E.tensor_tensor(
                src_f[:], src_f[:],
                bass.AP(nrm.tensor if hasattr(nrm, 'tensor') else nrm, 0,
                        [[8, 128], [1, 8], [0, 64]]), ALU.mult)], deps=(s_v,), inc=s_v)
            p.run("vector", [lambda E: E.tensor_tensor(
                src_f[:], src_f[:],
                bass.AP(scale_rep, 0, [[64, 128], [0, 8], [1, 64]]), ALU.mult)],
                deps=(s_v,), inc=s_v)
            p.run("vector", [
                lambda E: E.tensor_single_scalar(rt[:, :, 0:32], src_f[:, :, 32:64],
                                                 -1.0, ALU.mult),
                lambda E: E.tensor_copy(rt[:, :, 32:64], src_f[:, :, 0:32]),
            ], deps=(s_v,), inc=s_v)
            p.run("vector", [
                lambda E: E.tensor_tensor(
                    src_f[:], src_f[:],
                    bass.AP(tab[sl].tensor if hasattr(tab[sl], 'tensor') else tab[sl],
                            tabc_col * 64, [[256, 128], [0, 8], [1, 64]]), ALU.mult),
                lambda E: E.tensor_tensor(
                    rt[:], rt[:],
                    bass.AP(tab[sl].tensor if hasattr(tab[sl], 'tensor') else tab[sl],
                            tabs_col * 64, [[256, 128], [0, 8], [1, 64]]), ALU.mult),
            ], deps=(s_v,), inc=s_v)
            p.run("vector", [lambda E: E.tensor_add(src_f[:], src_f[:], rt[:])],
                  deps=(s_v,), inc=s_v)
            p.wait("vector", s_tp)
            p.run("vector", [lambda E: E.tensor_copy(zb[:, 0:512],
                                                     src_f[:].rearrange("p a b -> p (a b)"))],
                  deps=(s_v,), inc=s_v)
            p.wait("tensor", s_ev)
            tps = [(lambda E, c=c: E.transpose(tpb(c), zb[0:128, c * 128:(c + 1) * 128],
                                               identb[:])) for c in range(4)]
            p.run("tensor", tps, deps=(s_v,), inc=s_tp)
            p.run("vector", [(lambda E, c=c, i=i: E.tensor_copy(
                dstT[:, c, i * 128:(i + 1) * 128], tpb(c))) for c in range(4)],
                deps=(s_tp,), inc=s_ev)

        zc_vals = []
        for i in range(nt):
            j = i % 2
            if i >= 2:
                p.wait("sync", s_v, zc_vals[i - 2])
            p.dma("sync", zt[j][:], z2[1 + i * 128: 1 + (i + 1) * 128, :], inc=s_z[j])
            p.dma("sync", zs[j][:], z2[i * 128: (i + 1) * 128, :], inc=s_z[j])
            p.dma("sync", tab[j][:], tab_in[i * 128:(i + 1) * 128, :, :], inc=s_tb[j])
            p.wait("vector", s_z[j].h, 32 * (i // 2 + 1))
            p.wait("vector", s_tb[j].h, 16 * (i // 2 + 1))
            p.wait("vector", s_tp)
            p.run("vector", [
                (lambda E, j=j: E.tensor_copy(zb[:, 0:512], zt[j][:, 0:512])),
                (lambda E, j=j: E.tensor_copy(zsb[:, 512:1024], zs[j][:, 512:1024])),
            ], deps=(s_v,), inc=s_v)
            zc_vals.append(s_v.v)
            p.wait("vector", s_mm)
            srcs = [zb, zb, zb, zb, zsb, zsb, zsb, zsb]
            p.wait("tensor", s_ev)
            tps = [(lambda E, k=k: E.transpose(tpb(k),
                                               srcs[k][0:128, k * 128:(k + 1) * 128],
                                               identb[:])) for k in range(8)]
            p.run("tensor", tps, deps=(s_v,), inc=s_tp)
            p.run("vector", [(lambda E, k=k: E.tensor_copy(hT[:, k, :], tpb(k)))
                             for k in range(8)], deps=(s_tp,), inc=s_ev)
            p.wait("tensor", s_ev)
            p.wait("tensor", s_a)
            p.wait("tensor", s_v)
            mms = [(lambda E, k=k, n=n: E.matmul(
                ps_qkv[:, n * 512:(n + 1) * 512], hT[:, k, :],
                wq[:, k, n * 512:(n + 1) * 512], start=(k == 0), stop=(k == 7)))
                for n in range(3) for k in range(8)]
            p.run("tensor", mms, inc=s_mm)
            p.wait("vector", s_mm)
            p.run("vector", [lambda E: E.tensor_copy(qf[:].rearrange("p a b -> p (a b)"),
                                                     ps_qkv[:, 0:512])], deps=(s_v,), inc=s_v)
            p.run("vector", [lambda E: E.tensor_copy(kf[:].rearrange("p a b -> p (a b)"),
                                                     ps_qkv[:, 512:1024])], inc=s_v)
            p.wait("scalar", s_mm)
            p.run("scalar", [(lambda E, i=i: E.activation(out=vtok[:, i, :],
                                                          in_=ps_qkv[:, 1024:1536],
                                                          func=AF.Copy))], inc=s_a)
            rotary_norm(qf, qs_r.tensor if hasattr(qs_r, 'tensor') else qs_r, 0, 1, qT, i, j)
            rotary_norm(kf, ks_r.tensor if hasattr(ks_r, 'tensor') else ks_r, 2, 3, kT, i, j)
            zc_vals[i] = s_v.v

        p.wait("tensor", s_ev)
        p.wait("tensor", s_a)
        p.wait("tensor", s_v)
        QK_SCALE = 8.0
        for wl in range(NQB):
            p.dma("sync", mask[:].rearrange("p a m -> p (a m)") if False else mask[:],
                  mask_in[wl, :, :].rearrange("(a p) m -> p a m", p=128), inc=s_m)
            kt0 = (128 + 512 * wl) // 128
            for it in range(4):
                qtile = 5 + 4 * wl + it
                for h in range(8):
                    qpart = 64 * (h % 2)
                    p.wait("tensor", s_v)
                    mms = [(lambda E, half=half, h=h, qpart=qpart, qtile=qtile, kt0=kt0:
                            E.matmul(ps_s[:, half * 512:(half + 1) * 512],
                                     qT[qpart:qpart + 64, h // 2,
                                        qtile * 128:(qtile + 1) * 128],
                                     kT[qpart:qpart + 64, h // 2,
                                        kt0 * 128 + half * 512:kt0 * 128 + (half + 1) * 512],
                                     start=True, stop=True))
                           for half in range(2)]
                    p.run("tensor", mms, inc=s_mm)
                    p.wait("vector", s_mm)
                    p.wait("vector", s_m[0].h if False else s_m.h, 16 * (wl + 1))
                    p.run("vector", [(lambda E, it=it: E.tensor_tensor(
                        ps_s[:, 0:1024], ps_s[:, 0:1024], mask[:, it, :], ALU.add))],
                        inc=s_v)
                    p.run("vector", [lambda E: E.reduce_max(negmax[:], ps_s[:, 0:1024],
                                                            axis=AX.X, negate=True)],
                          deps=(s_v,), inc=s_v)
                    p.run("scalar", (lambda E: E.activation(
                        out=P[:], in_=ps_s[:, 0:1024], func=AF.Exp,
                        bias=negmax[:], accum_out=ssum[:])), deps=(s_v,), inc=s_a)
                    p.run("vector", [lambda E: E.reciprocal(rs[:], ssum[:])],
                          deps=(s_a,), inc=s_v)
                    p.wait("tensor", s_ev)
                    p.wait("tensor", s_a)
                    tps = [(lambda E, c=c: E.transpose(tpb(c),
                                                      P[0:128, c * 128:(c + 1) * 128],
                                                      identb[:])) for c in range(8)]
                    p.run("tensor", tps, inc=s_tp)
                    p.run("vector", [(lambda E, c=c: E.tensor_copy(PT[:, c, :], tpb(c)))
                                     for c in range(8)], deps=(s_tp,), inc=s_ev)
                    p.wait("tensor", s_ev)
                    mms = [(lambda E, c=c, h=h, kt0=kt0:
                            E.matmul(ps_o[:, 0:64], PT[:, c, :],
                                     vtok[:, kt0 + c, h * 64:(h + 1) * 64],
                                     start=(c == 0), stop=(c == 7)))
                           for c in range(8)]
                    p.run("tensor", mms, inc=s_mm)
                    p.wait("vector", s_mm)
                    p.run("vector", [(lambda E, it=it, h=h: E.tensor_scalar(
                        osb[:, it, h * 64:(h + 1) * 64], ps_o[:, 0:64], rs[:],
                        float(QK_SCALE ** 2) if False else None, ALU.mult))], inc=s_v)
            for it in range(4):
                qtile = 5 + 4 * wl + it
                j = it % 2
                row0 = wl * 512 + it * 128
                if wl * 4 + it >= 2:
                    p.wait("sync", s_st[j].h, 16 * ((wl * 4 + it) // 2))
                p.dma("sync", xq[j][:], z2[1 + qtile * 128:1 + (qtile + 1) * 128, :],
                      inc=s_xq[j])
                p.run("vector", [(lambda E, it=it: E.tensor_copy(
                    ob[:], osb[:, it, :]))], deps=(s_v,), inc=s_v)
                p.wait("tensor", s_ev)
                p.wait("tensor", s_v)
                tps = [(lambda E, c=c: E.transpose(tpb(c), ob[0:128, c * 128:(c + 1) * 128],
                                                   identb[:])) for c in range(4)]
                p.run("tensor", tps, inc=s_tp)
                p.run("vector", [(lambda E, c=c: E.tensor_copy(oT[:, c, :], tpb(c)))
                                 for c in range(4)], deps=(s_tp,), inc=s_ev)
                p.wait("tensor", s_ev)
                mms = [(lambda E, c=c, n=n: E.matmul(
                    ps_s[:, n * 512:(n + 1) * 512], oT[:, c, :],
                    wo[:, c, n * 512:(n + 1) * 512], start=(c == 0), stop=(c == 3)))
                    for n in range(2) for c in range(4)]
                p.run("tensor", mms, inc=s_mm)
                p.wait("vector", s_mm)
                p.wait("vector", s_xq[j].h, 16 * ((wl * 4 + it) // 2 + 1))
                p.run("vector", [(lambda E, j=j: E.tensor_add(
                    fin[j][:], ps_s[:, 0:1024], xq[j][:]))], inc=s_v)
                p.dma("sync", xo[row0:row0 + 128, :], fin[j][:], deps=(s_v,),
                      inc=s_st[j])
        for j in range(2):
            p.wait("sync", s_st[j].h, s_st[j].v)
        p.build(nc)
    return nc


def build_t1l_ffn(ntok=TPC):
    nt = ntok // 128
    nc = bass.Bass("TRN2", target_bir_lowering=False, debug=False)
    z2 = nc.dram_tensor("z2", [ntok + 1, D], F32, kind="ExternalInput").ap()
    w1_in = nc.dram_tensor("w1", [128, 8 * 44 * 128], BF16, kind="ExternalInput").ap()
    w2_in = nc.dram_tensor("w2", [128, 22 * 8 * 128], BF16, kind="ExternalInput").ap()
    b1_in = nc.dram_tensor("b1", [1, 5632], F32, kind="ExternalInput").ap()
    b2_in = nc.dram_tensor("b2", [128, 8], F32, kind="ExternalInput").ap()
    lng_in = nc.dram_tensor("lng", [128, 8], F32, kind="ExternalInput").ap()
    lnb_in = nc.dram_tensor("lnb", [128, 8], F32, kind="ExternalInput").ap()
    ident_in = nc.dram_tensor("identin", [128, 128], F32, kind="ExternalInput").ap()
    xo = nc.dram_tensor("xo", [ntok, D], F32, kind="ExternalOutput").ap()

    with ExitStack() as ctx:
        e = ctx.enter_context
        w1 = e(nc.sbuf_tensor("w1s", [128, 8, 44, 128], BF16))
        w2 = [e(nc.sbuf_tensor(f"w2s{i}", [128, 8, 128], BF16)) for i in range(2)]
        b2 = e(nc.sbuf_tensor("b2s", [128, 8], F32))
        lng = e(nc.sbuf_tensor("lngs", [128, 8], F32))
        lnb = e(nc.sbuf_tensor("lnbs", [128, 8], F32))
        lnb_b = e(nc.sbuf_tensor("lnbb", [128, 8], BF16))
        btot = e(nc.sbuf_tensor("btot", [1, 5632], F32))
        btot_b = e(nc.sbuf_tensor("btotb", [1, 5632], BF16))
        ones = e(nc.sbuf_tensor("ones", [1, 128], BF16))
        identf = e(nc.sbuf_tensor("identf", [128, 128], F32))
        identb = e(nc.sbuf_tensor("identb", [128, 128], BF16))
        zt = [e(nc.sbuf_tensor(f"zt{i}", [128, D], F32)) for i in range(2)]
        zs = [e(nc.sbuf_tensor(f"zs{i}", [128, D], F32)) for i in range(2)]
        sq = e(nc.sbuf_tensor("sq", [128, D], F32))
        sums = e(nc.sbuf_tensor("sums", [128, 4], F32))
        mu = e(nc.sbuf_tensor("mu", [128, 1], F32))
        rstd = e(nc.sbuf_tensor("rstd", [128, 1], F32))
        nh = e(nc.sbuf_tensor("nh", [128, D], F32))
        nhb = e(nc.sbuf_tensor("nhb", [128, D], BF16))
        nT = e(nc.sbuf_tensor("nT", [128, 8, 128], BF16))
        gl = e(nc.sbuf_tensor("gl", [128, 128], F32))
        hgT = e(nc.sbuf_tensor("hgT", [128, 22, 128], BF16))
        outT = e(nc.sbuf_tensor("outT", [128, 8, 128], F32))
        fin = [e(nc.sbuf_tensor(f"fin{i}", [128, D], F32)) for i in range(2)]
        ps_tp = e(nc.psum_tensor("ps_tp", [128, 1024], F32))
        ps_w1 = e(nc.psum_tensor("ps_w1", [128, 1024], F32))
        ps_w2 = e(nc.psum_tensor("ps_w2", [128, 1024], F32))
        ps_bs = e(nc.psum_tensor("ps_bs", [1, 512], F32))

        def tpb(k):
            return ps_tp[:].bitcast(BF16)[:, k * 128:(k + 1) * 128]

        s_w = Sem(ctx, nc, "s_w")
        s_z = [Sem(ctx, nc, f"s_z{j}") for j in range(2)]
        s_a = Sem(ctx, nc, "s_a")
        s_v = Sem(ctx, nc, "s_v")
        s_tp = Sem(ctx, nc, "s_tp")
        s_ev = Sem(ctx, nc, "s_ev")
        s_mm = Sem(ctx, nc, "s_mm")
        s_ep = Sem(ctx, nc, "s_ep")
        s_st = [Sem(ctx, nc, f"s_st{j}") for j in range(2)]
        s_w2 = [Sem(ctx, nc, f"s_w2{j}") for j in range(2)]
        cst = Consts(ctx, nc, [1e-5])
        s_c = Sem(ctx, nc, "s_c")

        p = Prog()
        cst.emit(p, s_c)
        p.run("gpsimd", [lambda E: E.memset(ones[:], 1.0)], inc=s_c)
        p.dma("sync", w1[:], w1_in.rearrange("p (a b m) -> p a b m", a=8, b=44), inc=s_w)
        p.dma("sync", b2[:], b2_in, inc=s_w)
        p.dma("sync", lng[:], lng_in, inc=s_w)
        p.dma("sync", lnb[:], lnb_in, inc=s_w)
        p.dma("sync", identf[:], ident_in, inc=s_w)
        p.dma("sync", btot[:], b1_in, inc=s_w)
        p.run("scalar", lambda E: E.activation(out=identb[:], in_=identf[:], func=AF.Copy),
              deps=(s_w, s_c), inc=s_a)
        for eng in ("tensor", "vector", "scalar"):
            p.wait(eng, s_w)
            p.wait(eng, s_a)
            p.wait(eng, s_c)
        p.run("vector", [lambda E: E.tensor_copy(lnb_b[:], lnb[:])], inc=s_v)
        p.wait("tensor", s_v)
        for n in range(11):
            mms = [(lambda E, k=k, n=n: E.matmul(
                ps_bs[:, 0:512], lnb_b[:, k:k + 1],
                w1[:, k, :, :].rearrange("p a m -> p (a m)")[:, n * 512:(n + 1) * 512],
                start=(k == 0), stop=(k == 7))) for k in range(8)]
            p.run("tensor", mms, deps=(s_ep,), inc=s_mm)
            p.run("vector", [(lambda E, n=n: E.tensor_add(
                btot[:, n * 512:(n + 1) * 512], btot[:, n * 512:(n + 1) * 512],
                ps_bs[:, 0:512]))], deps=(s_mm,), inc=s_ep)
        p.run("vector", [lambda E: E.tensor_copy(btot_b[:], btot[:])],
              deps=(s_ep,), inc=s_v)
        p.wait("tensor", s_v)

        w2mm_done = []
        for i in range(nt):
            j = i % 2
            if i >= 2:
                p.wait("sync", s_st[j].h, 16 * (i // 2))
            p.dma("sync", zt[j][:], z2[1 + i * 128: 1 + (i + 1) * 128, :], inc=s_z[j])
            p.dma("sync", zs[j][:], z2[i * 128: (i + 1) * 128, :], inc=s_z[j])
            p.wait("scalar", s_z[j].h, 32 * (i // 2 + 1))
            p.wait("vector", s_z[j].h, 32 * (i // 2 + 1))
            p.wait("scalar", s_v)
            p.run("scalar", lambda E, j=j: E.activation(out=sq[:, 0:512], in_=zt[j][:, 0:512],
                                                        func=AF.Square), deps=(s_ep,), inc=s_a)
            p.run("scalar", lambda E, j=j: E.activation(out=sq[:, 512:1024], in_=zs[j][:, 512:1024],
                                                        func=AF.Square), deps=(s_a,), inc=s_a)
            p.run("vector", [
                lambda E, j=j: E.reduce_sum(sums[:, 0:1], zt[j][:, 0:512], axis=AX.X),
                lambda E, j=j: E.reduce_sum(sums[:, 1:2], zs[j][:, 512:1024], axis=AX.X),
            ], deps=(s_v,), inc=s_v)
            p.run("vector", [lambda E: E.reduce_sum(sums[:, 2:3], sq[:], axis=AX.X)],
                  deps=(s_a,), inc=s_v)
            p.run("vector", [lambda E: E.tensor_add(sums[:, 0:1], sums[:, 0:1], sums[:, 1:2])],
                  deps=(s_v,), inc=s_v)
            p.run("vector", [lambda E: E.tensor_scalar(mu[:], sums[:, 0:1], 1.0 / D, None,
                                                       ALU.mult)], deps=(s_v,), inc=s_v)
            p.run("vector", [lambda E: E.tensor_tensor(sums[:, 1:2], mu[:], mu[:], ALU.mult)],
                  deps=(s_v,), inc=s_v)
            p.run("vector", [lambda E: E.tensor_scalar(sums[:, 3:4], sums[:, 2:3], 1.0 / D,
                                                       None, ALU.mult)], deps=(s_v,), inc=s_v)
            p.run("vector", [lambda E: E.tensor_sub(sums[:, 3:4], sums[:, 3:4], sums[:, 1:2])],
                  deps=(s_v,), inc=s_v)
            p.run("scalar", lambda E: E.activation(out=rstd[:], in_=sums[:, 3:4],
                                                   func=AF.Sqrt, bias=1e-5),
                  deps=(s_v,), inc=s_a)
            p.run("vector", [lambda E: E.reciprocal(rstd[:], rstd[:])], deps=(s_a,), inc=s_v)
            p.run("vector", [
                lambda E, j=j: E.tensor_scalar(nh[:, 0:512], zt[j][:, 0:512], mu[:], rstd[:],
                                               ALU.subtract, ALU.mult),
                lambda E, j=j: E.tensor_scalar(nh[:, 512:1024], zs[j][:, 512:1024], mu[:],
                                               rstd[:], ALU.subtract, ALU.mult),
            ], deps=(s_v,), inc=s_v)
            p.wait("vector", s_tp)
            p.run("vector", [lambda E: E.tensor_copy(nhb[:], nh[:])], deps=(s_v,), inc=s_v)
            p.wait("vector", s_mm)
            p.wait("tensor", s_ev)
            tps = [(lambda E, k=k: E.transpose(tpb(k), nhb[0:128, k * 128:(k + 1) * 128],
                                               identb[:])) for k in range(8)]
            p.run("tensor", tps, deps=(s_v,), inc=s_tp)
            p.run("vector", [(lambda E, k=k: E.tensor_scalar(nT[:, k, :], tpb(k),
                                                             lng[:, k:k + 1], None, ALU.mult))
                             for k in range(8)], deps=(s_tp,), inc=s_ev)
            p.wait("tensor", s_ev)
            ep0v = []
            for mp in range(22):
                bk = mp % 2
                aoff = bk * 512
                goff = bk * 512 + 128
                if mp >= 2:
                    p.wait("tensor", s_ep, ep0v[mp - 2])
                else:
                    p.wait("tensor", s_ep)
                mms = []
                for k in range(8):
                    mms.append(lambda E, k=k, mp=mp, aoff=aoff: E.matmul(
                        ps_w1[:, aoff:aoff + 128], w1[:, k, mp, :], nT[:, k, :],
                        start=(k == 0), stop=False))
                mms.append(lambda E, mp=mp, aoff=aoff: E.matmul(
                    ps_w1[:, aoff:aoff + 128], btot_b[:, mp * 128:(mp + 1) * 128],
                    ones[:], start=False, stop=True))
                for k in range(8):
                    mms.append(lambda E, k=k, mp=mp, goff=goff: E.matmul(
                        ps_w1[:, goff:goff + 128], w1[:, k, 22 + mp, :], nT[:, k, :],
                        start=(k == 0), stop=False))
                mms.append(lambda E, mp=mp, goff=goff: E.matmul(
                    ps_w1[:, goff:goff + 128], btot_b[:, (22 + mp) * 128:(23 + mp) * 128],
                    ones[:], start=False, stop=True))
                p.run("tensor", mms, inc=s_mm)
                p.wait("scalar", s_mm)
                p.run("scalar", (lambda E, goff=goff: E.activation(
                    out=gl[:], in_=ps_w1[:, goff:goff + 128], func=AF.Gelu)),
                    deps=(s_ep,), inc=s_ep)
                p.run("vector", [(lambda E, mp=mp, aoff=aoff: E.tensor_tensor(
                    hgT[:, mp, :], ps_w1[:, aoff:aoff + 128], gl[:], ALU.mult))],
                    deps=(s_ep,), inc=s_ep)
                ep0v.append(s_ep.v)
            p.wait("tensor", s_ep)
            for k in range(22):
                w2i = i * 22 + k
                sl2 = w2i % 2
                if w2i >= 2:
                    p.wait("gpsimd", s_mm, w2mm_done[w2i - 2])
                p.dma("gpsimd", w2[sl2][:],
                      w2_in.rearrange("p (a b m) -> p a b m", a=22, b=8)[:, k, :, :],
                      inc=s_w2[sl2])
                p.wait("tensor", s_w2[sl2].h, 16 * (w2i // 2 + 1))
                mms = [(lambda E, k=k, m2=m2, sl2=sl2: E.matmul(
                    ps_w2[:, m2 * 128:(m2 + 1) * 128], w2[sl2][:, m2, :], hgT[:, k, :],
                    start=(k == 0 and m2 % 4 == 0), stop=(k == 21 and m2 % 4 == 3)))
                    for m2 in range(8)]
                p.run("tensor", mms, inc=s_mm)
                w2mm_done.append(s_mm.v)
            for m2 in range(8):
                p.wait("scalar", s_mm)
                p.run("scalar", (lambda E, m2=m2: E.activation(
                    out=outT[:, m2, :], in_=ps_w2[:, m2 * 128:(m2 + 1) * 128],
                    func=AF.Identity, bias=b2[:, m2:m2 + 1])),
                    deps=(s_ep,) if m2 == 0 else (), inc=s_ep)
            for half in range(2):
                ks = range(half * 4, half * 4 + 4)
                p.wait("tensor", s_v)
                p.wait("tensor", s_ep)
                tps = [(lambda E, k=k, half=half: E.transpose(
                    ps_tp[:, half * 512 + (k % 4) * 128: half * 512 + (k % 4) * 128 + 128],
                    outT[:, k, :], identf[:])) for k in ks]
                p.run("tensor", tps, inc=s_tp)
                p.run("vector",
                      [(lambda E, k=k, half=half, j=j:
                        E.scalar_tensor_tensor(
                            fin[j][:, k * 128:(k + 1) * 128],
                            ps_tp[:, half * 512 + (k % 4) * 128:
                                  half * 512 + (k % 4) * 128 + 128], 1.0,
                            zt[j][:, k * 128:(k + 1) * 128], ALU.mult, ALU.add))
                       for k in ks], deps=(s_tp,), inc=s_v)
            p.dma("sync", xo[i * 128:(i + 1) * 128, :], fin[j][:], deps=(s_v,),
                  inc=s_st[j])
        for j in range(2):
            p.wait("sync", s_st[j].h, s_st[j].v)
        p.build(nc)
    return nc


def build_ln(ntok=TPC):
    nc = bass.Bass("TRN2", target_bir_lowering=False, debug=False)
    x_in = nc.dram_tensor("x", [ntok, D], F32, kind="ExternalInput").ap()
    xo = nc.dram_tensor("xo", [ntok, D], F32, kind="ExternalOutput").ap()
    nt = ntok // 128
    with ExitStack() as ctx:
        e = ctx.enter_context
        xt = [e(nc.sbuf_tensor(f"xt{i}", [128, D], F32)) for i in range(2)]
        out = [e(nc.sbuf_tensor(f"ot{i}", [128, D], F32)) for i in range(2)]
        sq = e(nc.sbuf_tensor("sq", [128, D], F32))
        sums = e(nc.sbuf_tensor("sums", [128, 4], F32))
        mu = e(nc.sbuf_tensor("mu", [128, 1], F32))
        rstd = e(nc.sbuf_tensor("rstd", [128, 1], F32))
        s_x = [Sem(ctx, nc, f"s_x{j}") for j in range(2)]
        s_a = Sem(ctx, nc, "s_a")
        s_v = Sem(ctx, nc, "s_v")
        s_st = [Sem(ctx, nc, f"s_st{j}") for j in range(2)]
        cst = Consts(ctx, nc, [1e-5])
        s_c = Sem(ctx, nc, "s_c")
        p = Prog()
        cst.emit(p, s_c)
        p.wait("scalar", s_c)
        p.wait("vector", s_c)
        for i in range(nt):
            j = i % 2
            if i >= 2:
                p.wait("sync", s_st[j].h, 16 * (i // 2))
            p.dma("sync", xt[j][:], x_in[i * 128:(i + 1) * 128, :], inc=s_x[j])
            p.wait("scalar", s_x[j].h, 16 * (i // 2 + 1))
            p.wait("vector", s_x[j].h, 16 * (i // 2 + 1))
            p.wait("scalar", s_v)
            emit_ln_tile(p, i, xt[j][:], None, sq, sums, mu, rstd, s_a, s_v,
                         out_f32=out[j][:])
            p.dma("sync", xo[i * 128:(i + 1) * 128, :], out[j][:], deps=(s_v,),
                  inc=s_st[j])
        for j in range(2):
            p.wait("sync", s_st[j].h, s_st[j].v)
        p.build(nc)
    return nc


def build_axpy(ntok=TPC, alpha=0.5):
    nc = bass.Bass("TRN2", target_bir_lowering=False, debug=False)
    a_in = nc.dram_tensor("a", [ntok, D], F32, kind="ExternalInput").ap()
    b_in = nc.dram_tensor("b", [ntok, D], F32, kind="ExternalInput").ap()
    xo = nc.dram_tensor("xo", [ntok, D], F32, kind="ExternalOutput").ap()
    nt = ntok // 128
    with ExitStack() as ctx:
        e = ctx.enter_context
        at = [e(nc.sbuf_tensor(f"at{i}", [128, D], F32)) for i in range(2)]
        bt = [e(nc.sbuf_tensor(f"bt{i}", [128, D], F32)) for i in range(2)]
        ot = [e(nc.sbuf_tensor(f"ot{i}", [128, D], F32)) for i in range(2)]
        s_x = [Sem(ctx, nc, f"s_x{j}") for j in range(2)]
        s_v = Sem(ctx, nc, "s_v")
        s_st = [Sem(ctx, nc, f"s_st{j}") for j in range(2)]
        p = Prog()
        for i in range(nt):
            j = i % 2
            if i >= 2:
                p.wait("sync", s_st[j].h, 16 * (i // 2))
            p.dma("sync", at[j][:], a_in[i * 128:(i + 1) * 128, :], inc=s_x[j])
            p.dma("sync", bt[j][:], b_in[i * 128:(i + 1) * 128, :], inc=s_x[j])
            p.wait("vector", s_x[j].h, 32 * (i // 2 + 1))
            p.run("vector", [(lambda E, j=j: E.scalar_tensor_tensor(
                ot[j][:], bt[j][:], float(alpha), at[j][:], ALU.mult, ALU.add))],
                deps=(s_v,), inc=s_v)
            p.dma("sync", xo[i * 128:(i + 1) * 128, :], ot[j][:], deps=(s_v,),
                  inc=s_st[j])
        for j in range(2):
            p.wait("sync", s_st[j].h, s_st[j].v)
        p.build(nc)
    return nc



def t1l_tables():
    n, DH, W = 8192, 64, 512
    half = np.arange(0, DH, 2, dtype=np.float32)
    inv_freq = 1.0 / 10000.0 ** (half / DH)
    t = np.arange(n, dtype=np.float32)
    freqs = t[:, None] * inv_freq[None, :]
    freqs = np.concatenate([freqs, freqs], axis=-1)
    base = (half + 0.4 * DH) / (1.4 * DH)
    power = (t - n // 2) / float(W // 2)
    scale = base[None, :] ** power[:, None]
    scale = np.concatenate([scale, scale], axis=-1)
    cos, sin = np.cos(freqs), np.sin(freqs)
    tabs = np.stack([cos * scale, sin * scale, cos / scale, sin / scale], axis=1)
    return np.ascontiguousarray(tabs.astype(np.float32))


def t1l_masks():
    i = np.arange(512)[:, None]
    jj = np.arange(1024)[None, :]
    j_abs = jj - 512
    validB = (i >= j_abs) & (i <= j_abs + 512)
    maskB = np.where(validB, 0.0, -1e30).astype(np.float32)
    validA = validB & (jj >= 512)
    maskA = np.where(validA, 0.0, -1e30).astype(np.float32)
    return maskA, maskB


def t1l_weights_host(inp, l):
    wqkv = inp["L1_ffn_wqkv"][l]
    wo = inp["L1_ffn_wo"][l]
    w1 = inp["L1_ffn_w1"][l]
    b1 = inp["L1_ffn_b1"][l]
    w2 = inp["L1_ffn_w2"][l]
    w1p = np.zeros((1024, 5632), np.float32)
    w1p[:, 0:2730] = w1[:, 0:2730]
    w1p[:, 2816:2816 + 2730] = w1[:, 2730:5460]
    b1p = np.zeros((1, 5632), np.float32)
    b1p[0, 0:2730] = b1[0:2730]
    b1p[0, 2816:2816 + 2730] = b1[2730:5460]
    w2p = np.zeros((2816, 1024), np.float32)
    w2p[0:2730] = w2
    return {
        "wqkv": np.ascontiguousarray(wqkv.reshape(8, 128, 1536).transpose(1, 0, 2)
                                     .reshape(128, 8 * 1536)),
        "wo": np.ascontiguousarray(wo.reshape(4, 128, 1024).transpose(1, 0, 2)
                                   .reshape(128, 4 * 1024)),
        "w1": np.ascontiguousarray(w1p.reshape(8, 128, 44, 128).transpose(1, 0, 2, 3)
                                   .reshape(128, 8 * 44 * 128)),
        "w2": np.ascontiguousarray(w2p.reshape(22, 128, 8, 128).transpose(1, 0, 2, 3)
                                   .reshape(128, 22 * 8 * 128)),
        "b1p": b1p,
        "b2": host_colvec(np.asarray(inp["L1_ffn_b2"][l], np.float32)),
        "lng": host_colvec(np.asarray(inp["L1_ffn_ln_g"][l], np.float32)),
        "lnb": host_colvec(np.asarray(inp["L1_ffn_ln_b"][l], np.float32)),
        "qs": np.ascontiguousarray(inp["L1_ffn_q_scale"][l], np.float32),
        "ks": np.ascontiguousarray(inp["L1_ffn_k_scale"][l], np.float32),
    }


def _z2_slices(z, halo):
    out = []
    for c in range(NCORES):
        b = c // 4
        t0 = c * TPC
        lo = t0 - halo - 1
        blo = b * 8192
        sl = np.zeros((TPC + halo + 1, D), np.float32)
        src_lo = max(lo, blo)
        sl[src_lo - lo:] = z[src_lo:t0 + TPC]
        out.append(sl)
    return out


def run_t1l(z, Wb, launch=_launch):
    ident = _ident()
    tabs = t1l_tables()
    maskA, maskB = t1l_masks()
    nc_a = _cached(("t1l_attn",), lambda: build_t1l_attn())
    ins = []
    z2s = _z2_slices(z, HALO)
    for c in range(NCORES):
        t0 = c * TPC
        blo = (c // 4) * 8192
        pos = np.clip(np.arange(t0 - HALO, t0 + TPC) - blo, 0, 8191)
        m4 = np.stack([maskA if (t0 + 512 * wl) % 8192 == 0 else maskB
                       for wl in range(4)])
        ins.append({"z2": z2s[c], "wqkv": Wb["wqkv"], "wo": Wb["wo"],
                    "tabs": np.ascontiguousarray(tabs[pos]),
                    "qs": Wb["qs"], "ks": Wb["ks"],
                    "mask4": m4, "identin": ident})
    res = launch(nc_a, ins)
    zmid = np.concatenate([r["xo"] for r in res], axis=0)

    nc_f = _cached(("t1l_ffn",), lambda: build_t1l_ffn())
    z2s = _z2_slices(zmid, 0)
    ins = [{"z2": z2s[c], "w1": Wb["w1"], "w2": Wb["w2"], "b1": Wb["b1p"],
            "b2": Wb["b2"], "lng": Wb["lng"], "lnb": Wb["lnb"],
            "identin": ident} for c in range(NCORES)]
    res = launch(nc_f, ins)
    return np.concatenate([r["xo"] for r in res], axis=0)


def run_l1ffn_macaron(x, Wb0, Wb1, launch=_launch):
    nc_ln = _cached(("ln",), lambda: build_ln())
    res = launch(nc_ln, [{"x": s} for s in _shard_tok(x)])
    z = np.concatenate([r["xo"] for r in res], axis=0)
    z = run_t1l(z, Wb0, launch)
    z = run_t1l(z, Wb1, launch)
    nc_ax = _cached(("axpy",), lambda: build_axpy())
    res = launch(nc_ax, [{"a": a, "b": b} for a, b in
                         zip(_shard_tok(x), _shard_tok(z))])
    return np.concatenate([r["xo"] for r in res], axis=0)



def build_l1a_pre(ntok=TPC):
    nt = ntok // 128
    nc = bass.Bass("TRN2", target_bir_lowering=False, debug=False)
    z_in = nc.dram_tensor("z", [ntok, D], F32, kind="ExternalInput").ap()
    wq_in = nc.dram_tensor("wq", [128, 8 * 1024], BF16, kind="ExternalInput").ap()
    wk_in = nc.dram_tensor("wk", [128, 8 * 1024], BF16, kind="ExternalInput").ap()
    wv_in = nc.dram_tensor("wv", [128, 8 * 1024], BF16, kind="ExternalInput").ap()
    tab_in = nc.dram_tensor("tabs", [ntok, 2, 32], F32, kind="ExternalInput").ap()
    ident_in = nc.dram_tensor("identin", [128, 128], F32, kind="ExternalInput").ap()
    qT_out = nc.dram_tensor("qTo", [128, 8, ntok], BF16, kind="ExternalOutput").ap()
    kT_out = nc.dram_tensor("kTo", [128, 8, ntok], BF16, kind="ExternalOutput").ap()
    v_out = nc.dram_tensor("vo", [ntok, D], BF16, kind="ExternalOutput").ap()

    with ExitStack() as ctx:
        e = ctx.enter_context
        wq = e(nc.sbuf_tensor("wqs", [128, 8, 1024], BF16))
        wk = e(nc.sbuf_tensor("wks", [128, 8, 1024], BF16))
        wv = e(nc.sbuf_tensor("wvs", [128, 8, 1024], BF16))
        identf = e(nc.sbuf_tensor("identf", [128, 128], F32))
        identb = e(nc.sbuf_tensor("identb", [128, 128], BF16))
        zt = [e(nc.sbuf_tensor(f"zt{i}", [128, D], F32)) for i in range(2)]
        zb = e(nc.sbuf_tensor("zb", [128, D], BF16))
        hT = e(nc.sbuf_tensor("hTt", [128, 8, 128], BF16))
        qf = e(nc.sbuf_tensor("qf", [128, 16, 64], F32))
        kf = e(nc.sbuf_tensor("kf", [128, 16, 64], F32))
        sqs = e(nc.sbuf_tensor("sqs", [128, 16, 64], F32))
        nrm = e(nc.sbuf_tensor("nrm", [128, 16], F32))
        rot = e(nc.sbuf_tensor("rot", [128, 16, 64], F32))
        tab = [e(nc.sbuf_tensor(f"tab{i}", [128, 2, 32], F32)) for i in range(2)]
        qb = e(nc.sbuf_tensor("qb", [128, D], BF16))
        vb = [e(nc.sbuf_tensor(f"vb{i}", [128, D], BF16)) for i in range(2)]
        qstg = [e(nc.sbuf_tensor(f"qstg{i}", [128, 8, 128], BF16)) for i in range(2)]
        kstg = [e(nc.sbuf_tensor(f"kstg{i}", [128, 8, 128], BF16)) for i in range(2)]
        ps_q = e(nc.psum_tensor("ps_q", [128, 1024], F32))
        ps_k = e(nc.psum_tensor("ps_k", [128, 1024], F32))
        ps_v = e(nc.psum_tensor("ps_v", [128, 1024], F32))
        ps_tp = e(nc.psum_tensor("ps_tp", [128, 1024], F32))

        def tpb(k):
            return ps_tp[:].bitcast(BF16)[:, k * 128:(k + 1) * 128]

        s_w = Sem(ctx, nc, "s_w")
        s_z = [Sem(ctx, nc, f"s_z{j}") for j in range(2)]
        s_tb = [Sem(ctx, nc, f"s_tb{j}") for j in range(2)]
        s_a = Sem(ctx, nc, "s_a")
        s_v = Sem(ctx, nc, "s_v")
        s_tp = Sem(ctx, nc, "s_tp")
        s_ev = Sem(ctx, nc, "s_ev")
        s_mm = Sem(ctx, nc, "s_mm")
        s_stv = [Sem(ctx, nc, f"s_stv{j}") for j in range(2)]
        s_stq = [Sem(ctx, nc, f"s_stq{j}") for j in range(2)]
        s_stk = [Sem(ctx, nc, f"s_stk{j}") for j in range(2)]
        cst = Consts(ctx, nc, [float(np.finfo(np.float32).eps)])
        s_c = Sem(ctx, nc, "s_c")

        p = Prog()
        cst.emit(p, s_c)
        p.dma("sync", wq[:], wq_in.rearrange("p (a m) -> p a m", a=8), inc=s_w)
        p.dma("sync", wk[:], wk_in.rearrange("p (a m) -> p a m", a=8), inc=s_w)
        p.dma("sync", wv[:], wv_in.rearrange("p (a m) -> p a m", a=8), inc=s_w)
        p.dma("sync", identf[:], ident_in, inc=s_w)
        p.run("scalar", lambda E: E.activation(out=identb[:], in_=identf[:], func=AF.Copy),
              deps=(s_w, s_c), inc=s_a)
        for eng in ("tensor", "vector", "scalar"):
            p.wait(eng, s_w)
            p.wait(eng, s_a)
            p.wait(eng, s_c)

        def rms_rot(src, dstT, i, sl):
            p.run("scalar", lambda E, src=src: E.activation(out=sqs[:], in_=src[:],
                                                            func=AF.Square),
                  deps=(s_v,), inc=s_a)
            p.run("vector", [lambda E: E.tensor_reduce(nrm[:], sqs[:], axis=AX.X,
                                                       op=ALU.add)],
                  deps=(s_a,), inc=s_v)
            p.run("scalar", lambda E: E.activation(
                out=nrm[:], in_=nrm[:], func=AF.Sqrt, scale=float(1.0 / 64.0),
                bias=float(np.finfo(np.float32).eps)), deps=(s_v,), inc=s_a)
            p.run("vector", [lambda E: E.reciprocal(nrm[:], nrm[:])],
                  deps=(s_a,), inc=s_v)
            p.run("vector", [lambda E, src=src: E.tensor_tensor(
                src[:], src[:],
                bass.AP(nrm.tensor if hasattr(nrm, 'tensor') else nrm, 0,
                        [[16, 128], [1, 16], [0, 64]]), ALU.mult)],
                deps=(s_v,), inc=s_v)
            tabh = tab[sl].tensor if hasattr(tab[sl], 'tensor') else tab[sl]
            cAP = bass.AP(tabh, 0, [[64, 128], [0, 16], [1, 32]])
            sAP = bass.AP(tabh, 32, [[64, 128], [0, 16], [1, 32]])
            p.run("vector", [
                (lambda E, src=src, cAP=cAP: E.tensor_tensor(
                    rot[:, :, 0:32], src[:, :, 0:32], cAP, ALU.mult)),
                (lambda E, src=src, cAP=cAP: E.tensor_tensor(
                    rot[:, :, 32:64], src[:, :, 32:64], cAP, ALU.mult)),
            ], deps=(s_v,), inc=s_v)
            p.run("vector", [
                (lambda E, src=src, sAP=sAP: E.tensor_tensor(
                    src[:, :, 32:64], src[:, :, 32:64], sAP, ALU.mult)),
                (lambda E, src=src, sAP=sAP: E.tensor_tensor(
                    src[:, :, 0:32], src[:, :, 0:32], sAP, ALU.mult)),
            ], deps=(s_v,), inc=s_v)
            p.run("vector", [
                (lambda E, src=src: E.tensor_add(rot[:, :, 0:32], rot[:, :, 0:32],
                                                 src[:, :, 32:64])),
                (lambda E, src=src: E.tensor_sub(rot[:, :, 32:64], rot[:, :, 32:64],
                                                 src[:, :, 0:32])),
            ], deps=(s_v,), inc=s_v)
            p.wait("vector", s_tp)
            p.run("vector", [lambda E: E.tensor_copy(qb[:],
                                                     rot[:].rearrange("p a b -> p (a b)"))],
                  deps=(s_v,), inc=s_v)
            p.wait("tensor", s_ev)
            tps = [(lambda E, c=c: E.transpose(tpb(c), qb[0:128, c * 128:(c + 1) * 128],
                                               identb[:])) for c in range(8)]
            p.run("tensor", tps, deps=(s_v,), inc=s_tp)
            p.run("vector", [(lambda E, c=c, dstT=dstT: E.tensor_copy(
                dstT[:, c, :], tpb(c)))
                for c in range(8)], deps=(s_tp,), inc=s_ev)

        zc_vals = []
        for i in range(nt):
            j = i % 2
            if i >= 2:
                p.wait("sync", s_v, zc_vals[i - 2])
            p.dma("sync", zt[j][:], z_in[i * 128:(i + 1) * 128, :], inc=s_z[j])
            p.dma("sync", tab[j][:], tab_in[i * 128:(i + 1) * 128, :, :], inc=s_tb[j])
            p.wait("vector", s_z[j].h, 16 * (i // 2 + 1))
            p.wait("vector", s_tb[j].h, 16 * (i // 2 + 1))
            p.wait("vector", s_tp)
            p.run("vector", [(lambda E, j=j: E.tensor_copy(zb[:], zt[j][:]))],
                  deps=(s_v,), inc=s_v)
            p.wait("vector", s_mm)
            p.wait("tensor", s_ev)
            tps = [(lambda E, k=k: E.transpose(tpb(k), zb[0:128, k * 128:(k + 1) * 128],
                                               identb[:])) for k in range(8)]
            p.run("tensor", tps, deps=(s_v,), inc=s_tp)
            p.run("vector", [(lambda E, k=k: E.tensor_copy(hT[:, k, :], tpb(k)))
                             for k in range(8)], deps=(s_tp,), inc=s_ev)
            p.wait("tensor", s_ev)
            p.wait("tensor", s_v)
            p.wait("tensor", s_a)
            mms = []
            for (ps, w) in ((ps_q, wq), (ps_k, wk), (ps_v, wv)):
                for n in range(2):
                    for k in range(8):
                        mms.append(lambda E, ps=ps, w=w, n=n, k=k: E.matmul(
                            ps[:, n * 512:(n + 1) * 512], hT[:, k, :],
                            w[:, k, n * 512:(n + 1) * 512],
                            start=(k == 0), stop=(k == 7)))
            p.run("tensor", mms, inc=s_mm)
            p.wait("vector", s_mm)
            p.run("vector", [lambda E: E.tensor_copy(qf[:].rearrange("p a b -> p (a b)"),
                                                     ps_q[:, 0:1024])], deps=(s_v,), inc=s_v)
            p.run("vector", [lambda E: E.tensor_copy(kf[:].rearrange("p a b -> p (a b)"),
                                                     ps_k[:, 0:1024])], inc=s_v)
            p.wait("scalar", s_mm)
            if i >= 2:
                p.wait("scalar", s_stv[j].h, 16 * (i // 2))
            p.run("scalar", [(lambda E, j=j: E.activation(out=vb[j][:], in_=ps_v[:, 0:1024],
                                                          func=AF.Copy))], inc=s_a)
            p.dma("sync", v_out[i * 128:(i + 1) * 128, :], vb[j][:], deps=(s_a,),
                  inc=s_stv[j])
            if i >= 2:
                p.wait("vector", s_stq[j].h, 16 * (i // 2))
                p.wait("vector", s_stk[j].h, 16 * (i // 2))
            rms_rot(qf, qstg[j], i, j)
            p.dma("sync", qT_out[:, :, i * 128:(i + 1) * 128], qstg[j][:],
                  deps=(s_ev,), inc=s_stq[j])
            rms_rot(kf, kstg[j], i, j)
            p.dma("sync", kT_out[:, :, i * 128:(i + 1) * 128], kstg[j][:],
                  deps=(s_ev,), inc=s_stk[j])
            zc_vals.append(s_v.v)
        for j in range(2):
            p.wait("sync", s_stq[j].h, s_stq[j].v)
            p.wait("sync", s_stk[j].h, s_stk[j].v)
            p.wait("sync", s_stv[j].h, s_stv[j].v)
        p.build(nc)
    return nc


def build_l1a_flash(T=8192, NU=4):
    nc = bass.Bass("TRN2", target_bir_lowering=False, debug=False)
    qT4 = nc.dram_tensor("qT4", [NU, 64, T], BF16, kind="ExternalInput").ap()
    kT4 = nc.dram_tensor("kT4", [NU, 64, T], BF16, kind="ExternalInput").ap()
    v4 = nc.dram_tensor("v4", [NU, T, 128], BF16, kind="ExternalInput").ap()
    ident_in = nc.dram_tensor("identin", [128, 128], F32, kind="ExternalInput").ap()
    y4 = nc.dram_tensor("y4", [NU, T, 128], F32, kind="ExternalOutput").ap()

    NQT = T // 512
    with ExitStack() as ctx:
        e = ctx.enter_context
        qsb = e(nc.sbuf_tensor("qsb", [64, T], BF16))
        ksb = e(nc.sbuf_tensor("ksb", [64, T], BF16))
        vsb = e(nc.sbuf_tensor("vsb", [128, T // 128, 128], BF16))
        onesb = e(nc.sbuf_tensor("onesb", [128, 1], BF16))
        identf = e(nc.sbuf_tensor("identf", [128, 128], F32))
        P = [e(nc.sbuf_tensor(f"P{i}", [128, 512], BF16)) for i in range(2)]
        acc_sb = e(nc.sbuf_tensor("acc_sb", [128, 512], F32))
        srow = e(nc.sbuf_tensor("srow", [1, 512], F32))
        scol = e(nc.sbuf_tensor("scol", [128, 4], F32))
        ystg = [e(nc.sbuf_tensor(f"ystg{i}", [128, 4, 128], F32)) for i in range(2)]
        ps_sc = e(nc.psum_tensor("ps_sc", [128, 1024], F32))
        ps_acc = e(nc.psum_tensor("ps_acc", [128, 512], F32))
        ps_sum = e(nc.psum_tensor("ps_sum", [1, 512], F32))
        ps_tp = e(nc.psum_tensor("ps_tp", [128, 1024], F32))

        s_w = Sem(ctx, nc, "s_w")
        s_u = Sem(ctx, nc, "s_u")
        s_a = Sem(ctx, nc, "s_a")
        s_v = Sem(ctx, nc, "s_v")
        s_g = Sem(ctx, nc, "s_g")
        s_tp = Sem(ctx, nc, "s_tp")
        s_mm = Sem(ctx, nc, "s_mm")
        s_st = [Sem(ctx, nc, f"s_st{j}") for j in range(2)]
        cst = Consts(ctx, nc, [0.125, -8.0])
        s_c = Sem(ctx, nc, "s_c")

        p = Prog()
        cst.emit(p, s_c)
        p.run("gpsimd", [lambda E: E.memset(onesb[:], 1.0)], inc=s_c)
        p.dma("sync", identf[:], ident_in, inc=s_w)
        for eng in ("tensor", "vector", "scalar", "gpsimd"):
            p.wait(eng, s_w)
            p.wait(eng, s_c)

        for u in range(NU):
            if u > 0:
                p.wait("sync", s_v)
            p.dma("sync", qsb[:], qT4[u], inc=s_u)
            p.dma("sync", ksb[:], kT4[u], inc=s_u)
            p.dma("sync", vsb[:], v4[u].rearrange("(a p) m -> p a m", p=128), inc=s_u)
            for eng in ("tensor", "vector", "scalar"):
                p.wait(eng, s_u.h, s_u.v)
            exp_done = {}
            vmm_done = {}
            for qt in range(NQT):
                njb = 4 * qt + 4
                for jb in range(njb):
                    reg = jb % 2
                    if (qt, jb - 2) in exp_done:
                        p.wait("tensor", s_a, exp_done[(qt, jb - 2)])
                    p.run("tensor", (lambda E, qt=qt, jb=jb, reg=reg: E.matmul(
                        ps_sc[:, reg * 512:(reg + 1) * 512],
                        ksb[:, jb * 128:(jb + 1) * 128],
                        qsb[:, qt * 512:(qt + 1) * 512],
                        start=True, stop=True)), inc=s_mm)
                    p.wait("scalar", s_mm)
                    if (qt, jb - 2) in vmm_done:
                        p.wait("scalar", s_mm.h, vmm_done[(qt, jb - 2)])
                    p.run("scalar", (lambda E, reg=reg: E.activation(
                        out=P[reg][:], in_=ps_sc[:, reg * 512:(reg + 1) * 512],
                        func=AF.Exp, scale=0.125, bias=-8.0)), inc=s_a)
                    exp_done[(qt, jb)] = s_a.v
                    diag = jb >= 4 * qt
                    if diag:
                        p.wait("gpsimd", s_a)
                        p.run("gpsimd", (lambda E, qt=qt, jb=jb, reg=reg:
                                         E.affine_select(
                                             P[reg][:], P[reg][:], [[1, 512]],
                                             ALU.is_ge, 0.0,
                                             base=qt * 512 - jb * 128,
                                             channel_multiplier=-1)), inc=s_g)
                        p.wait("tensor", s_g)
                    else:
                        p.wait("tensor", s_a)
                    mms = [
                        (lambda E, qt=qt, jb=jb, reg=reg: E.matmul(
                            ps_acc[:, 0:512], vsb[:, jb, :], P[reg][:],
                            start=(jb == 0), stop=(jb == njb - 1))),
                        (lambda E, qt=qt, jb=jb, reg=reg: E.matmul(
                            ps_sum[:, 0:512], onesb[:], P[reg][:],
                            start=(jb == 0), stop=(jb == njb - 1))),
                    ]
                    p.run("tensor", mms, inc=s_mm)
                    vmm_done[(qt, jb)] = s_mm.v
                jy = qt % 2
                p.wait("vector", s_mm)
                p.run("vector", [lambda E: E.tensor_copy(acc_sb[:], ps_acc[:, 0:512])],
                      deps=(s_v,), inc=s_v)
                p.run("vector", [lambda E: E.tensor_copy(srow[:], ps_sum[:, 0:512])],
                      inc=s_v)
                p.wait("tensor", s_v)
                tps = [(lambda E, c=c: E.transpose(
                    ps_tp[:, 512 + c * 128:512 + c * 128 + 1],
                    srow[0:1, c * 128:(c + 1) * 128], identf[0:1, 0:1]))
                    for c in range(4)]
                p.run("tensor", tps, inc=s_tp)
                p.run("vector", [(lambda E, c=c: E.tensor_copy(
                    scol[:, c:c + 1], ps_tp[:, 512 + c * 128:512 + c * 128 + 1]))
                    for c in range(4)], deps=(s_tp,), inc=s_v)
                p.run("vector", [lambda E: E.reciprocal(scol[:], scol[:])],
                      deps=(s_v,), inc=s_v)
                if qt >= 2:
                    p.wait("vector", s_st[jy].h, 16 * (qt // 2))
                p.wait("tensor", s_v)
                tps = [(lambda E, c=c: E.transpose(
                    ps_tp[:, c * 128:(c + 1) * 128],
                    acc_sb[0:128, c * 128:(c + 1) * 128], identf[:]))
                    for c in range(4)]
                p.run("tensor", tps, inc=s_tp)
                p.run("vector", [(lambda E, c=c, jy=jy: E.tensor_scalar(
                    ystg[jy][:, c, :], ps_tp[:, c * 128:(c + 1) * 128],
                    scol[:, c:c + 1], None, ALU.mult)) for c in range(4)],
                    deps=(s_tp,), inc=s_v)
                p.dma("sync", y4[u][qt * 512:(qt + 1) * 512, :]
                      .rearrange("(a p) m -> p a m", p=128), ystg[jy][:],
                      deps=(s_v,), inc=s_st[jy])
        for j in range(2):
            p.wait("sync", s_st[j].h, s_st[j].v)
        p.build(nc)
    return nc


def build_l1a_post(ntok=TPC):
    nt = ntok // 128
    nc = bass.Bass("TRN2", target_bir_lowering=False, debug=False)
    x_in = nc.dram_tensor("x", [ntok, D], F32, kind="ExternalInput").ap()
    y1_in = nc.dram_tensor("y1", [ntok, D], F32, kind="ExternalInput").ap()
    y2_in = nc.dram_tensor("y2", [ntok, D], F32, kind="ExternalInput").ap()
    wp_in = nc.dram_tensor("wp", [128, 8192], BF16, kind="ExternalInput").ap()
    lv_in = nc.dram_tensor("lv", [1, 128], F32, kind="ExternalInput").ap()
    ident_in = nc.dram_tensor("identin", [128, 128], F32, kind="ExternalInput").ap()
    xo = nc.dram_tensor("xo", [ntok, D], F32, kind="ExternalOutput").ap()
    LAMBDA_INIT = float(0.8 - 0.6 * np.exp(-0.3))

    with ExitStack() as ctx:
        e = ctx.enter_context
        wp = e(nc.sbuf_tensor("wps", [128, 8, 8, 128], BF16))
        identf = e(nc.sbuf_tensor("identf", [128, 128], F32))
        identb = e(nc.sbuf_tensor("identb", [128, 128], BF16))
        lv = e(nc.sbuf_tensor("lvs", [1, 128], F32))
        lt = e(nc.sbuf_tensor("lts", [1, 4], F32))
        lneg = e(nc.sbuf_tensor("lneg", [1, 1], F32))
        onesb = e(nc.sbuf_tensor("onesb", [1, 128], F32))
        lcol = e(nc.sbuf_tensor("lcol", [128, 1], F32))
        xt = [e(nc.sbuf_tensor(f"xt{i}", [128, D], F32)) for i in range(2)]
        y1t = [e(nc.sbuf_tensor(f"y1t{i}", [128, D], F32)) for i in range(2)]
        y2t = [e(nc.sbuf_tensor(f"y2t{i}", [128, D], F32)) for i in range(2)]
        yd = e(nc.sbuf_tensor("yd", [128, D], F32))
        ydb = e(nc.sbuf_tensor("ydb", [128, D], BF16))
        ydT = e(nc.sbuf_tensor("ydT", [128, 8, 128], BF16))
        fin = [e(nc.sbuf_tensor(f"fin{i}", [128, D], F32)) for i in range(2)]
        ps_tp = e(nc.psum_tensor("ps_tp", [128, 1024], F32))
        ps_mm = e(nc.psum_tensor("ps_mm", [128, 1024], F32))
        ps_l = e(nc.psum_tensor("ps_l", [128, 1], F32))

        def tpb(k):
            return ps_tp[:].bitcast(BF16)[:, k * 128:(k + 1) * 128]

        s_w = Sem(ctx, nc, "s_w")
        s_x = [Sem(ctx, nc, f"s_x{j}") for j in range(2)]
        s_a = Sem(ctx, nc, "s_a")
        s_v = Sem(ctx, nc, "s_v")
        s_tp = Sem(ctx, nc, "s_tp")
        s_ev = Sem(ctx, nc, "s_ev")
        s_mm = Sem(ctx, nc, "s_mm")
        s_ep = Sem(ctx, nc, "s_ep")
        s_st = [Sem(ctx, nc, f"s_st{j}") for j in range(2)]
        cst = Consts(ctx, nc, [LAMBDA_INIT])
        s_c = Sem(ctx, nc, "s_c")

        p = Prog()
        cst.emit(p, s_c)
        p.run("gpsimd", [lambda E: E.memset(onesb[:], 1.0)], inc=s_c)
        p.dma("sync", wp[:], wp_in.rearrange("p (a b m) -> p a b m", a=8, b=8), inc=s_w)
        p.dma("sync", identf[:], ident_in, inc=s_w)
        p.dma("sync", lv[:], lv_in, inc=s_w)
        p.run("scalar", lambda E: E.activation(out=identb[:], in_=identf[:], func=AF.Copy),
              deps=(s_w, s_c), inc=s_a)
        p.run("vector", [
            lambda E: E.tensor_mul(lv[:, 0:32], lv[:, 0:32], lv[:, 32:64]),
            lambda E: E.tensor_mul(lv[:, 64:96], lv[:, 64:96], lv[:, 96:128]),
        ], deps=(s_w,), inc=s_v)
        p.run("vector", [
            lambda E: E.reduce_sum(lt[:, 0:1], lv[:, 0:32], axis=AX.X),
            lambda E: E.reduce_sum(lt[:, 1:2], lv[:, 64:96], axis=AX.X),
        ], deps=(s_v,), inc=s_v)
        p.run("scalar", lambda E: E.activation(out=lt[:, 2:3], in_=lt[:, 0:1],
                                               func=AF.Exp), deps=(s_v,), inc=s_a)
        p.run("scalar", lambda E: E.activation(out=lt[:, 3:4], in_=lt[:, 1:2],
                                               func=AF.Exp), deps=(s_a,), inc=s_a)
        p.run("vector", [lambda E: E.tensor_sub(lt[:, 3:4], lt[:, 3:4], lt[:, 2:3])],
              deps=(s_a,), inc=s_v)
        p.run("vector", [lambda E: E.tensor_scalar(lneg[:], lt[:, 3:4],
                                                   LAMBDA_INIT, None, ALU.subtract)],
              deps=(s_v,), inc=s_v)
        p.wait("tensor", s_v)
        p.wait("tensor", s_c)
        p.run("tensor", lambda E: E.matmul(ps_l[:, 0:1], onesb[:], lneg[:],
                                           start=True, stop=True), inc=s_mm)
        p.run("vector", [lambda E: E.tensor_copy(lcol[:], ps_l[:, 0:1])],
              deps=(s_mm,), inc=s_v)
        for eng in ("tensor", "vector", "scalar"):
            p.wait(eng, s_w)
            p.wait(eng, s_a)
            p.wait(eng, s_v)

        for i in range(nt):
            j = i % 2
            if i >= 2:
                p.wait("sync", s_st[j].h, 16 * (i // 2))
            p.dma("sync", xt[j][:], x_in[i * 128:(i + 1) * 128, :], inc=s_x[j])
            p.dma("sync", y1t[j][:], y1_in[i * 128:(i + 1) * 128, :], inc=s_x[j])
            p.dma("sync", y2t[j][:], y2_in[i * 128:(i + 1) * 128, :], inc=s_x[j])
            p.wait("vector", s_x[j].h, 48 * (i // 2 + 1))
            p.wait("vector", s_tp)
            p.run("vector", [(lambda E, j=j: E.scalar_tensor_tensor(
                yd[:], y2t[j][:], lcol[:], y1t[j][:], ALU.mult, ALU.add))],
                deps=(s_v,), inc=s_v)
            p.run("vector", [lambda E: E.tensor_copy(ydb[:], yd[:])],
                  deps=(s_v,), inc=s_v)
            p.wait("vector", s_mm)
            emit_tp_in(p, ydb, (lambda k: ydT[:, k, :]), identb, tpb, s_tp, s_ev,
                       deps=(s_v,))
            p.wait("tensor", s_ev)

            def epi(m, reg, j=j):
                p.run("vector", [(lambda E, m=m, reg=reg: E.tensor_copy(
                    yd[:, m * 128:(m + 1) * 128], reg))], deps=(s_ep,), inc=s_ep)

            emit_dense8(p, nc, (lambda m, k: wp[:, k, m, :]),
                        (lambda k: ydT[:, k, :]), ps_mm, epi, s_mm, s_ep,
                        epi_eng="vector")
            for half in range(2):
                ks = range(half * 4, half * 4 + 4)
                p.wait("tensor", s_v)
                p.wait("tensor", s_ep)
                tps = [(lambda E, k=k, half=half: E.transpose(
                    ps_tp[:, half * 512 + (k % 4) * 128: half * 512 + (k % 4) * 128 + 128],
                    yd[0:128, k * 128:(k + 1) * 128], identf[:])) for k in ks]
                p.run("tensor", tps, inc=s_tp)
                p.run("vector",
                      [(lambda E, k=k, half=half, j=j:
                        E.scalar_tensor_tensor(
                            fin[j][:, k * 128:(k + 1) * 128],
                            ps_tp[:, half * 512 + (k % 4) * 128:
                                  half * 512 + (k % 4) * 128 + 128], 1.0,
                            xt[j][:, k * 128:(k + 1) * 128], ALU.mult, ALU.add))
                       for k in ks], deps=(s_tp,), inc=s_v)
            p.dma("sync", xo[i * 128:(i + 1) * 128, :], fin[j][:], deps=(s_v,),
                  inc=s_st[j])
        for j in range(2):
            p.wait("sync", s_st[j].h, s_st[j].v)
        p.build(nc)
    return nc


def l1a_tables():
    Tn, hd = 8192, 64
    inv_freq = 1.0 / 10000.0 ** (np.arange(0, hd, 2, dtype=np.float32) / hd)
    freqs = np.arange(Tn, dtype=np.float32)[:, None] * inv_freq[None, :]
    return np.ascontiguousarray(
        np.stack([np.cos(freqs), np.sin(freqs)], axis=1).astype(np.float32))


def run_l1attn(x, Wb, launch=_launch):
    ident = _ident()
    nc_ln = _cached(("ln",), lambda: build_ln())
    res = launch(nc_ln, [{"x": s} for s in _shard_tok(x)])
    z = np.concatenate([r["xo"] for r in res], axis=0)

    tabs = l1a_tables()
    nc_pre = _cached(("l1a_pre",), lambda: build_l1a_pre())
    ins = []
    for c in range(NCORES):
        t0 = c * TPC
        blo = (c // 4) * 8192
        ins.append({"z": _shard_tok(z)[c], "wq": Wb["WqT"], "wk": Wb["WkT"],
                    "wv": Wb["WvT"],
                    "tabs": np.ascontiguousarray(tabs[t0 - blo:t0 - blo + TPC]),
                    "identin": ident})
    res = launch(nc_pre, ins)
    qT = [np.concatenate([res[b * 4 + i]["qTo"] for i in range(4)], axis=2)
          for b in range(2)]
    kT = [np.concatenate([res[b * 4 + i]["kTo"] for i in range(4)], axis=2)
          for b in range(2)]
    vv = [np.concatenate([res[b * 4 + i]["vo"] for i in range(4)], axis=0)
          for b in range(2)]

    def unit_q(qTb, s, h):
        c = s * 4 + h // 2
        part = 64 * (h % 2)
        return qTb[part:part + 64, c, :]

    units = [(b, s, h) for b in range(2) for s in range(2) for h in range(8)]
    ins = []
    for core in range(NCORES):
        us = units[core * 4:(core + 1) * 4]
        q4 = np.stack([unit_q(qT[b], s, h) for (b, s, h) in us])
        k4 = np.stack([unit_q(kT[b], s, h) for (b, s, h) in us])
        v4 = np.stack([vv[b][:, h * 128:(h + 1) * 128] for (b, s, h) in us])
        ins.append({"qT4": np.ascontiguousarray(q4),
                    "kT4": np.ascontiguousarray(k4),
                    "v4": np.ascontiguousarray(v4), "identin": ident})
    nc_fl = _cached(("l1a_flash",), lambda: build_l1a_flash())
    res = launch(nc_fl, ins)
    y_half = [[np.zeros((8192, 1024), np.float32) for b in range(2)] for s in range(2)]
    for core in range(NCORES):
        for k4i, (b, s, h) in enumerate(units[core * 4:(core + 1) * 4]):
            y_half[s][b][:, h * 128:(h + 1) * 128] = res[core]["y4"][k4i]
    y1 = np.concatenate([y_half[0][0], y_half[0][1]], axis=0)
    y2 = np.concatenate([y_half[1][0], y_half[1][1]], axis=0)

    lv = np.concatenate([Wb["lq1"], Wb["lk1"], Wb["lq2"], Wb["lk2"]])[None, :]
    nc_post = _cached(("l1a_post",), lambda: build_l1a_post())
    ins = [{"x": _shard_tok(x)[c], "y1": _shard_tok(y1)[c], "y2": _shard_tok(y2)[c],
            "wp": Wb["WprojT"], "lv": np.ascontiguousarray(lv), "identin": ident}
           for c in range(NCORES)]
    res = launch(nc_post, ins)
    return np.concatenate([r["xo"] for r in res], axis=0)


def host_swapT(W):
    return np.ascontiguousarray(W.T.reshape(8, 128, 1024).transpose(1, 0, 2)
                                .reshape(128, 8 * 1024))



from contextlib import contextmanager
from concourse.ordered_engine_set import OrderedEngineSet

_FORI_REGS = {}


@contextmanager
def cached_fori(nc, E, start, end, step=1):
    engines = OrderedEngineSet([E.engine])
    key = (id(nc), E.engine)
    if key not in _FORI_REGS:
        _FORI_REGS[key] = nc.alloc_registers(f"cfori_{E.engine.value}", engines=engines)
    registers = _FORI_REGS[key]
    loop_id = nc.next_id()
    loop_start = f"cfori_{loop_id}_loop"
    loop_end = f"cfori_{loop_id}_end"
    nc.regs_mov(registers, start)
    nc.br(loop_start, engines=engines)
    with nc.body(loop_start, valid_engines=engines):
        yield nc.snap(registers, min_val=start, max_val=end - 1)
        nc.regs_alu(registers, registers, step, op=mybir.AluOpType.add)
        nc.br_lt(registers, end, on_true=loop_start, on_false=loop_end, engines=engines)
    nc.switch_bb(loop_end)


class ScanUnit:

    def __init__(self, ctx, nc, tag, CH, psum_base_ok=True):
        e = ctx.enter_context
        self.nc = nc
        self.CH = CH
        self.WTs = e(nc.sbuf_tensor(f"WTs{tag}", [128, 8, 1024], BF16))
        self.hTf = e(nc.sbuf_tensor(f"hTf{tag}", [128, 16], F32))
        self.hTb = e(nc.sbuf_tensor(f"hTb{tag}", [128, 16], BF16))
        self.u_sb = e(nc.sbuf_tensor(f"u_sb{tag}", [2, 1024], BF16))
        self.s_sb = e(nc.sbuf_tensor(f"s_sb{tag}", [128, 16], F32))
        self.g_sb = e(nc.sbuf_tensor(f"g_sb{tag}", [128, 16], F32))
        self.d_sb = e(nc.sbuf_tensor(f"d_sb{tag}", [128, 16], F32))
        self.m_sb = e(nc.sbuf_tensor(f"m_sb{tag}", [128, 16], F32))
        self.aring = [[e(nc.sbuf_tensor(f"ar{tag}{b}_{j}", [128, CH, 8], F32))
                       for j in range(2)] for b in range(2)]
        self.cring = [[e(nc.sbuf_tensor(f"cr{tag}{b}_{j}", [128, CH, 8], F32))
                       for j in range(2)] for b in range(2)]
        self.stg = [[e(nc.sbuf_tensor(f"st{tag}{b}_{j}", [128, CH, 8], F32))
                     for j in range(2)] for b in range(2)]
        self.u_ps = e(nc.psum_tensor(f"u_ps{tag}", [128, 256], F32))
        self.uT_ps = e(nc.psum_tensor(f"uT_ps{tag}", [128, 16], BF16))
        self.s_mm = Sem(ctx, nc, f"mm{tag}")
        self.s_ev = Sem(ctx, nc, f"ev{tag}")
        self.s_tp = Sem(ctx, nc, f"tp{tag}")
        self.s_sg = Sem(ctx, nc, f"sg{tag}")
        self.s_sig = Sem(ctx, nc, f"sig{tag}")
        self.s_d = Sem(ctx, nc, f"d{tag}")
        self.s_m = Sem(ctx, nc, f"m{tag}")
        self.s_h = Sem(ctx, nc, f"h{tag}")
        self.s_cast = Sem(ctx, nc, f"cast{tag}")
        self.s_stg = Sem(ctx, nc, f"stg{tag}")


def emit_scan_chunk_bodies(st, sl, identb):
    CH = st.CH

    def rAP(tile, t, inner=8):
        return bass.AP(tile, t * inner, [[CH * inner, 128], [1, inner]])

    def tensor_mms(E, t, B):
        E.wait_ge(st.s_cast.h, t + B["cast"])
        E.wait_ge(st.s_ev.h, t * 4 + B["ev"])
        for k in range(8):
            for c4 in range(4):
                mm = E.matmul(st.u_ps[32 * c4:32 * c4 + 2, 0:256],
                              st.hTb[:, 2 * k:2 * k + 2],
                              st.WTs[:, k, 256 * c4:256 * c4 + 256],
                              start=(k == 0), stop=(k == 7),
                              tile_position=(0, 32 * c4))
        mm.then_inc(st.s_mm.h, 1)

    def tensor_tps(E, t, B):
        E.wait_ge(st.s_ev.h, t * 4 + (B["ev"] + 4))
        E.wait_ge(st.s_sg.h, t * 2 + B["sg"])
        for c in range(8):
            tp = E.transpose(st.uT_ps[:, 2 * c:2 * c + 2],
                             st.u_sb[0:2, 128 * c:128 * c + 128],
                             identb[0:2, 0:2])
        tp.then_inc(st.s_tp.h, 1)

    def tensor_body(E, t, B):
        tensor_mms(E, t, B)
        tensor_tps(E, t, B)

    def vector_body(E, t, B):
        E.wait_ge(st.s_h.h, t + B["h"])
        E.tensor_sub(st.d_sb[:, 0:16:2], rAP(st.aring[0][sl], t), st.hTf[:, 0:16:2])
        E.tensor_sub(st.d_sb[:, 1:16:2], rAP(st.aring[1][sl], t),
                     st.hTf[:, 1:16:2]).then_inc(st.s_d.h, 2)
        E.wait_ge(st.s_mm.h, t + (B["mm"] + 1))
        E.tensor_copy(st.u_sb[0:2, 0:256], st.u_ps[0:2, 0:256]).then_inc(st.s_ev.h, 1)
        E.tensor_copy(st.u_sb[0:2, 256:512], st.u_ps[32:34, 0:256]).then_inc(st.s_ev.h, 1)
        E.wait_ge(st.s_tp.h, t + (B["tp"] + 1))
        E.tensor_add(st.s_sb[:, 0:16:2], st.uT_ps[:, 0:16:2], rAP(st.cring[0][sl], t))
        E.tensor_add(st.s_sb[:, 1:16:2], st.uT_ps[:, 1:16:2],
                     rAP(st.cring[1][sl], t)).then_inc(st.s_sg.h, 2)
        E.wait_ge(st.s_sig.h, t + (B["sig"] + 1))
        E.wait_ge(st.s_d.h, t * 2 + (B["d"] + 2))
        E.tensor_mul(st.m_sb[:], st.g_sb[:], st.d_sb[:]).then_inc(st.s_m.h, 1)
        E.wait_ge(st.s_m.h, t + (B["m"] + 1))
        E.wait_ge(st.s_stg.h, t * 2 + B["stg"])
        E.wait_ge(st.s_cast.h, t + (B["cast"] + 1))
        E.tensor_add(st.hTf[:], st.hTf[:], st.m_sb[:]).then_inc(st.s_h.h, 1)

    def scalar_body(E, t, B):
        E.wait_ge(st.s_mm.h, t + (B["mm"] + 1))
        E.copy(st.u_sb[0:2, 512:768], st.u_ps[64:66, 0:256]).then_inc(st.s_ev.h, 1)
        E.copy(st.u_sb[0:2, 768:1024], st.u_ps[96:98, 0:256]).then_inc(st.s_ev.h, 1)
        E.wait_ge(st.s_sg.h, t * 2 + (B["sg"] + 2))
        E.activation(out=st.g_sb[:], in_=st.s_sb[:],
                     func=AF.Sigmoid).then_inc(st.s_sig.h, 1)

    def gpsimd_body(E, t, B):
        E.wait_ge(st.s_m.h, t + (B["m"] + 1))
        E.tensor_add(st.hTb[:], st.hTf[:], st.m_sb[:]).then_inc(st.s_cast.h, 1)
        E.wait_ge(st.s_h.h, t + (B["h"] + 1))
        E.tensor_copy(rAP(st.stg[0][sl], t), st.hTf[:, 0:16:2]).then_inc(st.s_stg.h, 1)
        E.tensor_copy(rAP(st.stg[1][sl], t), st.hTf[:, 1:16:2]).then_inc(st.s_stg.h, 1)

    return dict(tensor=tensor_body, vector=vector_body, scalar=scalar_body,
                gpsimd=gpsimd_body, tensor_mms=tensor_mms, tensor_tps=tensor_tps)


def emit_scan_fori_chunk(p, st, sl, identb, nsteps_in_chunk=None):
    CH = st.CH if nsteps_in_chunk is None else nsteps_in_chunk
    bodies = emit_scan_chunk_bodies(st, sl, identb)
    B = {"cast": st.s_cast.v, "ev": st.s_ev.v, "tp": st.s_tp.v, "mm": st.s_mm.v,
         "sg": st.s_sg.v, "sig": st.s_sig.v, "d": st.s_d.v, "m": st.s_m.v,
         "h": st.s_h.v, "stg": st.s_stg.v}
    nc_ref = st.nc
    for eng in ("tensor", "vector", "scalar", "gpsimd"):
        def make(E, body=bodies[eng], B=B, CH=CH, nc=nc_ref):
            with cached_fori(nc, E, 0, CH) as t:
                body(E, t, B)
        p.emit(eng, make)
    st.s_mm.v += CH
    st.s_tp.v += CH
    st.s_ev.v += 4 * CH
    st.s_sg.v += 2 * CH
    st.s_sig.v += CH
    st.s_d.v += 2 * CH
    st.s_m.v += CH
    st.s_h.v += CH
    st.s_cast.v += CH
    st.s_stg.v += 2 * CH


def build_scan_fori(nsteps=8192, CH=128):
    import concourse.bacc as bacc
    nc = bacc.Bacc("TRN2", target_bir_lowering=False, debug=False)
    WT_in = nc.dram_tensor("WT", [128, 8 * 1024], BF16, kind="ExternalInput").ap()
    h0_in = nc.dram_tensor("h0", [128, 16], F32, kind="ExternalInput").ap()
    a_pl = [nc.dram_tensor(f"a{b}", [128, nsteps, 8], F32, kind="ExternalInput").ap()
            for b in range(2)]
    c_pl = [nc.dram_tensor(f"c{b}", [128, nsteps, 8], F32, kind="ExternalInput").ap()
            for b in range(2)]
    ident_in = nc.dram_tensor("identin", [128, 128], F32, kind="ExternalInput").ap()
    hs_pl = [nc.dram_tensor(f"hs{b}", [128, nsteps, 8], F32, kind="ExternalOutput").ap()
             for b in range(2)]
    hf_out = nc.dram_tensor("hf", [128, 16], F32, kind="ExternalOutput").ap()

    nch = nsteps // CH
    with ExitStack() as ctx:
        e = ctx.enter_context
        st = ScanUnit(ctx, nc, "A", CH)
        identf = e(nc.sbuf_tensor("identf", [128, 128], F32))
        identb = e(nc.sbuf_tensor("identb", [128, 128], BF16))
        s_w = Sem(ctx, nc, "s_w")
        s_i = Sem(ctx, nc, "s_i")
        s_ra = [[Sem(ctx, nc, f"s_ra{b}_{j}") for j in range(2)] for b in range(2)]
        s_rc = [[Sem(ctx, nc, f"s_rc{b}_{j}") for j in range(2)] for b in range(2)]
        s_so = [[Sem(ctx, nc, f"s_so{b}_{j}") for j in range(2)] for b in range(2)]

        p = Prog()
        p.dma("sync", st.WTs[:], WT_in.rearrange("p (k n) -> p k n", k=8), inc=s_w)
        p.dma("sync", st.hTf[:], h0_in, inc=s_w)
        p.dma("sync", identf[:], ident_in, inc=s_w)
        p.run("scalar", lambda E: E.activation(out=identb[:], in_=identf[:], func=AF.Copy),
              deps=(s_w,), inc=s_i)
        p.run("scalar", lambda E: E.activation(out=st.hTb[:], in_=st.hTf[:], func=AF.Copy),
              inc=st.s_cast)
        p.wait("tensor", s_i)
        p.wait("vector", s_i)
        p.wait("gpsimd", s_i)
        for ch in range(2):
            for b in range(2):
                p.dma("sync", st.aring[b][ch][:], a_pl[b][:, ch * CH:(ch + 1) * CH, :],
                      inc=s_ra[b][ch])
                p.dma("sync", st.cring[b][ch][:], c_pl[b][:, ch * CH:(ch + 1) * CH, :],
                      inc=s_rc[b][ch])

        for ch in range(nch):
            sl = ch % 2
            ldv = 16 * (ch // 2 + 1)
            for b in range(2):
                p.wait("tensor", s_ra[b][sl].h, ldv)
                p.wait("tensor", s_rc[b][sl].h, ldv)
                p.wait("vector", s_ra[b][sl].h, ldv)
                p.wait("vector", s_rc[b][sl].h, ldv)
            if ch >= 2:
                sov = 16 * (ch // 2)
                p.wait("gpsimd", s_so[0][sl].h, sov)
                p.wait("gpsimd", s_so[1][sl].h, sov)
            nch_pre = ch + 2
            if nch_pre < nch:
                p.wait("sync", st.s_h, (nch_pre - 1) * CH)
                for b in range(2):
                    p.dma("sync", st.aring[b][nch_pre % 2][:],
                          a_pl[b][:, nch_pre * CH:(nch_pre + 1) * CH, :],
                          inc=s_ra[b][nch_pre % 2])
                    p.dma("sync", st.cring[b][nch_pre % 2][:],
                          c_pl[b][:, nch_pre * CH:(nch_pre + 1) * CH, :],
                          inc=s_rc[b][nch_pre % 2])
            emit_scan_fori_chunk(p, st, sl, identb)
            p.wait("sync", st.s_stg, st.s_stg.v)
            for b in range(2):
                p.dma("sync", hs_pl[b][:, ch * CH:(ch + 1) * CH, :],
                      st.stg[b][sl][:], inc=s_so[b][sl])

        p.wait("sync", st.s_h, nsteps)
        p.dma("sync", hf_out, st.hTf[:], inc=s_w)
        for b in range(2):
            for j in range(2):
                p.wait("sync", s_so[b][j].h, s_so[b][j].v)
        p.wait("sync", s_w, s_w.v)
        p.build(nc)
    nc.compile()
    return nc



def build_a0mega(nsteps=8192, CH=128):
    import concourse.bacc as bacc
    nc = bacc.Bacc("TRN2", target_bir_lowering=False, debug=False)
    WT_A = nc.dram_tensor("WTA", [128, 8 * 1024], BF16, kind="ExternalInput").ap()
    WT_B = nc.dram_tensor("WTB", [128, 8 * 1024], BF16, kind="ExternalInput").ap()
    h0A_in = nc.dram_tensor("h0A", [128, 16], F32, kind="ExternalInput").ap()
    h0B_in = nc.dram_tensor("h0B", [128, 16], F32, kind="ExternalInput").ap()
    a_pl = [nc.dram_tensor(f"a{b}", [128, nsteps, 8], F32, kind="ExternalInput").ap()
            for b in range(2)]
    c_pl = [nc.dram_tensor(f"c{b}", [128, nsteps, 8], F32, kind="ExternalInput").ap()
            for b in range(2)]
    wiT = nc.dram_tensor("wiT", [128, 8192], BF16, kind="ExternalInput").ap()
    wifT = nc.dram_tensor("wifT", [128, 8192], BF16, kind="ExternalInput").ap()
    lb_in = nc.dram_tensor("lb", [128, 8], F32, kind="ExternalInput").ap()
    ident_in = nc.dram_tensor("identin", [128, 128], F32, kind="ExternalInput").ap()
    hs_pl = [nc.dram_tensor(f"hs{b}", [128, nsteps, 8], F32, kind="ExternalOutput").ap()
             for b in range(2)]

    nch = nsteps // CH
    LAG = 2
    with ExitStack() as ctx:
        e = ctx.enter_context
        A = ScanUnit(ctx, nc, "A", CH)
        Bu = ScanUnit(ctx, nc, "B", CH)
        wi = e(nc.sbuf_tensor("wi", [128, 8, 8, 128], BF16))
        wif = e(nc.sbuf_tensor("wif", [128, 8, 8, 128], BF16))
        lb = e(nc.sbuf_tensor("lbs", [128, 8], F32))
        identf = e(nc.sbuf_tensor("identf", [128, 128], F32))
        identb = e(nc.sbuf_tensor("identb", [128, 128], BF16))
        hTm = e(nc.sbuf_tensor("hTm", [128, 8, 128], BF16))
        tmp_a = e(nc.sbuf_tensor("tmpa", [128, 128], F32))
        ps_m = e(nc.psum_tensor("ps_m", [128, 1024], F32))
        s_w = Sem(ctx, nc, "s_w")
        s_i = Sem(ctx, nc, "s_i")
        s_ra = [[Sem(ctx, nc, f"s_ra{b}_{j}") for j in range(2)] for b in range(2)]
        s_rc = [[Sem(ctx, nc, f"s_rc{b}_{j}") for j in range(2)] for b in range(2)]
        s_so = [[Sem(ctx, nc, f"s_so{b}_{j}") for j in range(2)] for b in range(2)]
        s_gm = Sem(ctx, nc, "s_gm")
        s_mmm = Sem(ctx, nc, "s_mmm")
        s_epm = Sem(ctx, nc, "s_epm")

        p = Prog()
        p.dma("sync", A.WTs[:], WT_A.rearrange("p (k n) -> p k n", k=8), inc=s_w)
        p.dma("sync", Bu.WTs[:], WT_B.rearrange("p (k n) -> p k n", k=8), inc=s_w)
        p.dma("sync", A.hTf[:], h0A_in, inc=s_w)
        p.dma("sync", Bu.hTf[:], h0B_in, inc=s_w)
        p.dma("sync", wi[:], wiT.rearrange("p (a b m) -> p a b m", a=8, b=8), inc=s_w)
        p.dma("sync", wif[:], wifT.rearrange("p (a b m) -> p a b m", a=8, b=8), inc=s_w)
        p.dma("sync", lb[:], lb_in, inc=s_w)
        p.dma("sync", identf[:], ident_in, inc=s_w)
        p.run("scalar", lambda E: E.activation(out=identb[:], in_=identf[:], func=AF.Copy),
              deps=(s_w,), inc=s_i)
        p.run("scalar", lambda E: E.activation(out=A.hTb[:], in_=A.hTf[:], func=AF.Copy),
              inc=A.s_cast)
        p.run("scalar", lambda E: E.activation(out=Bu.hTb[:], in_=Bu.hTf[:], func=AF.Copy),
              inc=Bu.s_cast)
        for eng in ("tensor", "vector", "gpsimd"):
            p.wait(eng, s_i)
        for ch in range(2):
            for b in range(2):
                p.dma("sync", A.aring[b][ch][:], a_pl[b][:, ch * CH:(ch + 1) * CH, :],
                      inc=s_ra[b][ch])
                p.dma("sync", A.cring[b][ch][:], c_pl[b][:, ch * CH:(ch + 1) * CH, :],
                      inc=s_rc[b][ch])

        gather_done = {}
        mid_done = {}
        for ch in range(nch + LAG):
            chA = ch
            chB = ch - LAG
            act_A = chA < nch
            act_B = chB >= 0
            if act_A:
                sl = chA % 2
                ldv = 16 * (chA // 2 + 1)
                for b in range(2):
                    p.wait("tensor", s_ra[b][sl].h, ldv)
                    p.wait("tensor", s_rc[b][sl].h, ldv)
                    p.wait("vector", s_ra[b][sl].h, ldv)
                    p.wait("vector", s_rc[b][sl].h, ldv)
                if chA - 2 in gather_done:
                    p.wait("gpsimd", s_gm, gather_done[chA - 2])
                nch_pre = chA + 2
                if nch_pre < nch:
                    p.wait("sync", A.s_h, (nch_pre - 1) * CH)
                    for b in range(2):
                        p.dma("sync", A.aring[b][nch_pre % 2][:],
                              a_pl[b][:, nch_pre * CH:(nch_pre + 1) * CH, :],
                              inc=s_ra[b][nch_pre % 2])
                        p.dma("sync", A.cring[b][nch_pre % 2][:],
                              c_pl[b][:, nch_pre * CH:(nch_pre + 1) * CH, :],
                              inc=s_rc[b][nch_pre % 2])
            if act_B:
                slb = chB % 2
                p.wait("tensor", s_epm, mid_done[chB])
                p.wait("vector", s_epm, mid_done[chB])
                if chB >= 2:
                    sov = 16 * (chB // 2)
                    p.wait("gpsimd", s_so[0][slb].h, sov)
                    p.wait("gpsimd", s_so[1][slb].h, sov)

            if act_A and act_B:
                bodiesA = emit_scan_chunk_bodies(A, chA % 2, identb)
                bodiesB = emit_scan_chunk_bodies(Bu, chB % 2, identb)
                BA = {k: getattr(A, "s_" + k).v for k in
                      ("cast", "ev", "tp", "mm", "sg", "sig", "d", "m", "h", "stg")}
                BB = {k: getattr(Bu, "s_" + k).v for k in
                      ("cast", "ev", "tp", "mm", "sg", "sig", "d", "m", "h", "stg")}
                def make_t(E, bA=bodiesA, bB=bodiesB, BA=BA, BB=BB, CH=CH):
                    with cached_fori(nc, E, 0, CH) as t:
                        bA["tensor_mms"](E, t, BA)
                        bB["tensor_mms"](E, t, BB)
                        bA["tensor_tps"](E, t, BA)
                        bB["tensor_tps"](E, t, BB)
                p.emit("tensor", make_t)
                for eng in ("vector", "scalar", "gpsimd"):
                    def make(E, bA=bodiesA[eng], bB=bodiesB[eng], BA=BA, BB=BB, CH=CH):
                        with cached_fori(nc, E, 0, CH) as t:
                            bA(E, t, BA)
                            bB(E, t, BB)
                    p.emit(eng, make)
                for u in (A, Bu):
                    u.s_mm.v += CH; u.s_tp.v += CH; u.s_ev.v += 4 * CH
                    u.s_sg.v += 2 * CH; u.s_sig.v += CH; u.s_d.v += 2 * CH
                    u.s_m.v += CH; u.s_h.v += CH; u.s_cast.v += CH; u.s_stg.v += 2 * CH
            elif act_A:
                emit_scan_fori_chunk(p, A, chA % 2, identb)
            elif act_B:
                emit_scan_fori_chunk(p, Bu, chB % 2, identb)

            if act_A:
                sl = chA % 2
                p.wait("vector", A.s_stg, A.s_stg.v)
                for b in range(2):
                    if chA - LAG >= 0:
                        p.wait("scalar", Bu.s_h, (chA - LAG + 1) * CH)
                        p.wait("vector", Bu.s_h, (chA - LAG + 1) * CH)
                    p.wait("vector", s_mmm)
                    p.run("vector", [(lambda E, b=b, sl=sl, k=k:
                                      E.tensor_copy(hTm[:, k, :], A.stg[b][sl][:, :, k]))
                                     for k in range(8)], inc=s_gm)
                    p.wait("tensor", s_gm)
                    for m in range(8):
                        p.wait("tensor", s_epm)
                        mms = [(lambda E, b=b, m=m, k=k, sl=sl: E.matmul(
                            ps_m[:, 0:128], wi[:, k, m, :], hTm[:, k, :],
                            start=(k == 0), stop=(k == 7))) for k in range(8)]
                        p.run("tensor", mms, inc=s_mmm)
                        p.wait("scalar", s_mmm)
                        p.run("scalar", (lambda E: E.activation(
                            out=tmp_a[:], in_=ps_m[:, 0:128], func=AF.Tanh)),
                            deps=(s_epm,), inc=s_epm)
                        p.run("scalar", (lambda E, b=b, m=m, sl=sl: E.activation(
                            out=Bu.aring[b][sl][:, :, m], in_=tmp_a[:], func=AF.Tanh)),
                            deps=(s_epm,), inc=s_epm)
                        mms = [(lambda E, b=b, m=m, k=k, sl=sl: E.matmul(
                            ps_m[:, 512:640], wif[:, k, m, :], hTm[:, k, :],
                            start=(k == 0), stop=(k == 7))) for k in range(8)]
                        p.run("tensor", mms, inc=s_mmm)
                        p.wait("scalar", s_mmm)
                        p.run("scalar", (lambda E, b=b, m=m, sl=sl: E.activation(
                            out=Bu.cring[b][sl][:, :, m], in_=ps_m[:, 512:640],
                            func=AF.Identity, bias=lb[:, m:m + 1])),
                            deps=(s_epm,), inc=s_epm)
                gather_done[chA] = s_gm.v
                mid_done[chA] = s_epm.v

            if act_B:
                slb = chB % 2
                p.wait("sync", Bu.s_stg, Bu.s_stg.v)
                for b in range(2):
                    p.dma("sync", hs_pl[b][:, chB * CH:(chB + 1) * CH, :],
                          Bu.stg[b][slb][:], inc=s_so[b][slb])

        for b in range(2):
            for j in range(2):
                p.wait("sync", s_so[b][j].h, s_so[b][j].v)
        p.build(nc)
    nc.compile()
    return nc


# version 06 (final)
# speedup vs baseline: 1.0089x; 1.0089x over previous
import numpy as np
from contextlib import ExitStack

import concourse.bass as bass
import concourse.mybir as mybir
from concourse.bass_utils import run_bass_kernel_spmd

F32 = mybir.dt.float32
BF16 = mybir.dt.bfloat16
AF = mybir.ActivationFunctionType
ALU = mybir.AluOpType
AX = mybir.AxisListType

NCORES = 8
D = 1024
T_FULL = 16384
TPC = T_FULL // NCORES

ENGINES = ("sync", "tensor", "vector", "scalar", "gpsimd")


class Sem:
    def __init__(self, ctx, nc, name):
        self.h = ctx.enter_context(nc.semaphore(name))
        self.v = 0
        self.name = name


class Consts:

    def __init__(self, ctx, nc, values, name="consts"):
        self.values = list(dict.fromkeys(float(v) for v in values))
        n = max(1, len(self.values))
        self.tile = ctx.enter_context(nc.sbuf_tensor(name, [128, n], F32))
        for j, v in enumerate(self.values):
            nc.const_aps.register(F32, v, self.tile[:, j:j + 1])

    def emit(self, p, sem):
        fns = [(lambda E, j=j, v=v: E.memset(self.tile[:, j:j + 1], v))
               for j, v in enumerate(self.values)]
        p.run("gpsimd", fns, inc=sem)


class Prog:

    def __init__(self):
        self.ev = {e: [] for e in ENGINES}

    def emit(self, eng, fn):
        self.ev[eng].append(fn)

    def wait(self, eng, sem, val=None):
        v = sem.v if val is None else val
        h = sem.h if isinstance(sem, Sem) else sem
        self.ev[eng].append(lambda E, s=h, v=v: E.wait_ge(s, v))

    def run(self, eng, fns, deps=(), inc=None, inc_n=1):
        for s in deps:
            self.wait(eng, s)
        if not isinstance(fns, (list, tuple)):
            fns = [fns]
        for i, fn in enumerate(fns):
            if inc is not None and i == len(fns) - 1:
                sem = inc
                n = inc_n
                sem.v += n
                self.ev[eng].append(lambda E, f=fn, s=sem.h, n=n: f(E).then_inc(s, n))
            else:
                self.ev[eng].append(fn)

    def dma(self, eng, out, in_, deps=(), inc=None):
        def f(E, o=out, i=in_):
            return E.dma_start(out=o, in_=i)
        self.run(eng, f, deps=deps, inc=inc, inc_n=16)

    def build(self, nc):
        blk_ctx = nc.Block()
        blk = blk_ctx.__enter__()
        for eng in ENGINES:
            fns = self.ev[eng]
            if not fns:
                continue

            def make(fns):
                def body(E):
                    for f in fns:
                        f(E)
                return body
            getattr(blk, eng)(make(fns))
        blk_ctx.__exit__(None, None, None)



def build_prep(rows, cols):
    nc = bass.Bass("TRN2", target_bir_lowering=False, debug=False)
    src = nc.dram_tensor("src", [rows, cols], F32, kind="ExternalInput").ap()
    dst = nc.dram_tensor("dst", [rows, cols], BF16, kind="ExternalOutput").ap()
    ntiles = (rows + 127) // 128
    CB = min(cols, 16384)
    ncb = (cols + CB - 1) // CB
    with ExitStack() as ctx:
        e = ctx.enter_context
        a = [e(nc.sbuf_tensor(f"a{i}", [128, CB], F32)) for i in range(2)]
        b = [e(nc.sbuf_tensor(f"b{i}", [128, CB], BF16)) for i in range(2)]
        ld = [Sem(ctx, nc, f"ld{k}") for k in range(2)]
        cv = Sem(ctx, nc, "cv")
        st = [Sem(ctx, nc, f"st{k}") for k in range(2)]
        p = Prog()
        idx = 0
        for i in range(ntiles):
            r0, r1 = i * 128, min((i + 1) * 128, rows)
            for j in range(ncb):
                c0, c1 = j * CB, min((j + 1) * CB, cols)
                k = idx % 2
                if idx >= 2:
                    p.wait("sync", st[k].h, 16 * ((idx - 2) // 2 + 1))
                p.dma("sync", a[k][0:r1 - r0, 0:c1 - c0], src[r0:r1, c0:c1], inc=ld[k])
                p.wait("scalar", ld[k].h, 16 * (idx // 2 + 1))
                p.run("scalar",
                      lambda E, k=k, nr=r1 - r0, ncl=c1 - c0:
                      E.activation(out=b[k][0:nr, 0:ncl], in_=a[k][0:nr, 0:ncl], func=AF.Copy),
                      inc=cv)
                p.run("gpsimd",
                      lambda E, k=k, r0=r0, r1=r1, c0=c0, c1=c1:
                      E.dma_start(out=dst[r0:r1, c0:c1], in_=b[k][0:r1 - r0, 0:c1 - c0]),
                      deps=(cv,), inc=st[k], inc_n=16)
                idx += 1
        p.wait("gpsimd", st[0].h, st[0].v)
        p.wait("gpsimd", st[1].h, st[1].v)
        p.build(nc)
    return nc


_NC_CACHE = {}


def _cached(key, fn):
    if key not in _NC_CACHE:
        _NC_CACHE[key] = fn()
    return _NC_CACHE[key]


def cast_bf16_dev(arr, launches):
    a = np.ascontiguousarray(arr, dtype=np.float32)
    shape = a.shape
    n = a.size
    cols = shape[-1]
    rows = n // cols
    a2 = a.reshape(rows, cols)
    rpc = (rows + NCORES - 1) // NCORES
    pad = rpc * NCORES - rows
    if pad:
        a2 = np.concatenate([a2, np.zeros((pad, cols), np.float32)], axis=0)
    nc = _cached(("prep", rpc, cols), lambda: build_prep(rpc, cols))
    in_maps = [{"src": np.ascontiguousarray(a2[i * rpc:(i + 1) * rpc])} for i in range(NCORES)]
    res = launches(nc, in_maps)
    out = np.concatenate([res[i]["dst"] for i in range(NCORES)], axis=0)
    if pad:
        out = out[:rows]
    return out.reshape(shape)



def emit_ln_tile(p, t, x_f32, out_bf, sq, sums, mu, rstd, s_a, s_v,
                 eps=1e-5, out_f32=None):
    p.run("scalar", lambda E: E.activation(out=sq[:], in_=x_f32, func=AF.Square),
          inc=s_a)
    p.run("vector", [
        lambda E: E.reduce_sum(sums[:, 0:1], x_f32, axis=AX.X),
    ])
    p.run("vector", [
        lambda E: E.reduce_sum(sums[:, 1:2], sq[:], axis=AX.X),
    ], deps=(s_a,), inc=s_v)
    p.run("vector", [
        lambda E: E.tensor_scalar(mu[:], sums[:, 0:1], 1.0 / D, None, ALU.mult),
    ], deps=(s_v,), inc=s_v)
    p.run("vector", [
        lambda E: E.tensor_tensor(sums[:, 2:3], mu[:], mu[:], ALU.mult),
    ], deps=(s_v,), inc=s_v)
    p.run("vector", [
        lambda E: E.tensor_scalar(sums[:, 3:4], sums[:, 1:2], 1.0 / D, None, ALU.mult),
    ], deps=(s_v,), inc=s_v)
    p.run("vector", [
        lambda E: E.tensor_sub(sums[:, 3:4], sums[:, 3:4], sums[:, 2:3]),
    ], deps=(s_v,), inc=s_v)
    p.run("scalar", lambda E: E.activation(out=rstd[:], in_=sums[:, 3:4],
                                           func=AF.Sqrt, bias=float(eps)),
          deps=(s_v,), inc=s_a)
    p.run("vector", [lambda E: E.reciprocal(rstd[:], rstd[:])], deps=(s_a,), inc=s_v)
    out_aps = [out_bf] if out_f32 is None else (
        [out_f32] if out_bf is None else [out_bf, out_f32])
    outs = [(lambda E, o=o: E.tensor_scalar(o, x_f32, mu[:], rstd[:],
                                            ALU.subtract, ALU.mult))
            for o in out_aps]
    p.run("vector", outs, deps=(s_v,), inc=s_v)


def emit_tp_in(p, src_bf, dst, ident, ps_view, s_tp, s_ev, nsrc_part=128,
               kchunks=8, deps=(), evac_eng="vector"):
    tps = [
        (lambda E, k=k: E.transpose(
            ps_view(k),
            src_bf[0:nsrc_part, k * 128:(k + 1) * 128],
            ident[0:nsrc_part, 0:nsrc_part]))
        for k in range(kchunks)
    ]
    p.run("tensor", tps, deps=(s_ev,) + tuple(deps), inc=s_tp)
    p.run(evac_eng, [lambda E, k=k: E.tensor_copy(dst(k), ps_view(k))
                     for k in range(kchunks)], deps=(s_tp,), inc=s_ev)






def build_f0(ntok=TPC, tg=256):
    nc = bass.Bass("TRN2", target_bir_lowering=False, debug=False)
    x_in = nc.dram_tensor("x", [ntok, D], F32, kind="ExternalInput").ap()
    w1bd = nc.dram_tensor("w1bd", [128, 32 * 128], BF16, kind="ExternalInput").ap()
    w2bd = nc.dram_tensor("w2bd", [128, 32 * 512], BF16, kind="ExternalInput").ap()
    wcT = nc.dram_tensor("wcT", [16384, 1024], BF16, kind="ExternalInput").ap()
    ident_in = nc.dram_tensor("identin", [128, 128], F32, kind="ExternalInput").ap()
    x_out = nc.dram_tensor("xo", [ntok, D], F32, kind="ExternalOutput").ap()

    ngrp = ntok // tg
    nsub = tg // 128

    with ExitStack() as ctx:
        e = ctx.enter_context
        w1 = e(nc.sbuf_tensor("w1", [128, 32, 128], BF16))
        w2 = e(nc.sbuf_tensor("w2", [128, 32, 512], BF16))
        identf = e(nc.sbuf_tensor("identf", [128, 128], F32))
        identb = e(nc.sbuf_tensor("identb", [128, 128], BF16))
        wc = [e(nc.sbuf_tensor(f"wc{i}", [128, 1024], BF16)) for i in range(2)]
        xt = [e(nc.sbuf_tensor(f"xt{i}", [128, D], F32)) for i in range(2 * nsub)]
        sq = e(nc.sbuf_tensor("sq", [128, D], F32))
        sums = e(nc.sbuf_tensor("sums", [128, 4], F32))
        mu = e(nc.sbuf_tensor("mu", [128, 1], F32))
        rstd = e(nc.sbuf_tensor("rstd", [128, 1], F32))
        xln = e(nc.sbuf_tensor("xln", [128, D], BF16))
        xlnT = e(nc.sbuf_tensor("xlnT", [128, 8, tg], BF16))
        yT = e(nc.sbuf_tensor("yT", [128, 8, tg], BF16))
        hg = e(nc.sbuf_tensor("hg", [128, 32, tg], BF16))
        outT = e(nc.sbuf_tensor("outT", [128, 8, tg], F32))
        fin = [e(nc.sbuf_tensor(f"fin{i}", [128, D], F32)) for i in range(2)]
        ps_tp = e(nc.psum_tensor("ps_tp", [128, 1024], F32))
        ps_s = e(nc.psum_tensor("ps_s", [128, 1024], F32))
        ps_out = e(nc.psum_tensor("ps_out", [128, 8, tg], F32))

        def tpb(k):
            return ps_tp[:].bitcast(BF16)[:, k * 128:(k + 1) * 128]

        s_w = Sem(ctx, nc, "s_w")
        s_x = [Sem(ctx, nc, f"s_x{j}") for j in range(2 * nsub)]
        s_wc = [Sem(ctx, nc, f"s_wc{j}") for j in range(2)]
        s_a = Sem(ctx, nc, "s_a")
        s_v = Sem(ctx, nc, "s_v")
        s_tp = Sem(ctx, nc, "s_tp")
        s_ev = Sem(ctx, nc, "s_ev")
        s_mm = Sem(ctx, nc, "s_mm")
        s_gel = Sem(ctx, nc, "s_gel")
        s_st = [Sem(ctx, nc, f"s_st{j}") for j in range(2)]
        s_fin = Sem(ctx, nc, "s_fin")
        s_c = Sem(ctx, nc, "s_c")
        cst = Consts(ctx, nc, [1e-5])

        p = Prog()
        cst.emit(p, s_c)
        p.dma("sync", w1[:], w1bd.rearrange("p (a m) -> p a m", a=32), inc=s_w)
        p.dma("sync", w2[:], w2bd.rearrange("p (a m) -> p a m", a=32), inc=s_w)
        p.dma("sync", identf[:], ident_in, inc=s_w)
        p.run("scalar", lambda E: E.activation(out=identb[:], in_=identf[:], func=AF.Copy),
              deps=(s_w, s_c), inc=s_a)
        p.wait("vector", s_c)
        for eng in ("tensor", "vector"):
            p.wait(eng, s_w)
            p.wait(eng, s_a)

        wc_idx = 0
        wc_mm_done = []
        for g in range(ngrp):
            for s in range(nsub):
                i = g * nsub + s
                j = i % (2 * nsub)
                if i >= 2 * nsub:
                    p.wait("sync", s_fin, i - 2 * nsub + 1)
                p.dma("sync", xt[j][:],
                      x_in[g * tg + s * 128: g * tg + (s + 1) * 128, :], inc=s_x[j])

            for s in range(nsub):
                i = g * nsub + s
                j = i % (2 * nsub)
                buf = xt[j]
                p.wait("scalar", s_x[j], 16 * (i // (2 * nsub) + 1))
                p.wait("vector", s_x[j], 16 * (i // (2 * nsub) + 1))
                p.wait("scalar", s_v)
                p.wait("vector", s_tp)
                emit_ln_tile(p, i, buf[:], xln[:], sq, sums, mu, rstd, s_a, s_v)
                emit_tp_in(p, xln, (lambda k, s=s: xlnT[:, k, s * 128:(s + 1) * 128]),
                           identb, tpb, s_tp, s_ev, deps=(s_v,))

            for h in range(4):
                ev0 = s_ev.v
                for k in range(8):
                    off = (k % 2) * 512
                    if k >= 2:
                        p.wait("tensor", s_ev, ev0 + (k - 2) + 1)
                    elif k == 0:
                        p.wait("tensor", s_ev)
                        p.wait("tensor", s_gel)
                    p.run("tensor",
                          lambda E, h=h, k=k, off=off:
                          E.matmul(ps_s[:, off:off + tg],
                                   w1[:, h * 8 + k, :], xlnT[:, k, 0:tg],
                                   start=True, stop=True),
                          inc=s_mm)
                    p.run("vector",
                          lambda E, k=k, off=off:
                          E.tensor_copy(yT[:, k, 0:tg], ps_s[:, off:off + tg]),
                          deps=(s_mm,), inc=s_ev)
                for ky in range(8):
                    p.wait("tensor", s_gel)
                    if ky == 0:
                        p.wait("tensor", s_ev)
                    mms = [
                        (lambda E, h=h, ky=ky, m=m:
                         E.matmul(ps_s[:, m * tg:(m + 1) * tg],
                                  w2[:, h * 8 + ky, m * 128:(m + 1) * 128],
                                  yT[:, ky, 0:tg], start=True, stop=True))
                        for m in range(4)
                    ]
                    p.run("tensor", mms, inc=s_mm)
                    p.run("scalar",
                          lambda E, ky=ky:
                          E.activation(
                              out=hg[:, ky * 4:(ky + 1) * 4, 0:tg].rearrange("p a t -> p (a t)"),
                              in_=ps_s[:, 0:4 * tg], func=AF.Gelu),
                          deps=(s_mm,), inc=s_gel)
                p.wait("tensor", s_gel)
                for kcl in range(32):
                    kc = h * 32 + kcl
                    b = wc_idx % 2
                    if wc_idx >= 2:
                        p.wait("gpsimd", s_mm, wc_mm_done[wc_idx - 2])
                    p.dma("gpsimd", wc[b][:],
                          wcT[kc * 128:(kc + 1) * 128, :], inc=s_wc[b])
                    p.wait("tensor", s_wc[b], 16 * (wc_idx // 2 + 1))
                    first = (h == 0 and kcl == 0)
                    last = (kc == 127)
                    mms = [
                        (lambda E, m=m, kcl=kcl, first=first, last=last, b=wc_idx % 2:
                         E.matmul(ps_out[:, m, 0:tg], wc[b][:, m * 128:(m + 1) * 128],
                                  hg[:, kcl, 0:tg],
                                  start=(first and m % 2 == 0),
                                  stop=(last and m % 2 == 1)))
                        for m in range(8)
                    ]
                    p.run("tensor", mms, inc=s_mm)
                    wc_mm_done.append(s_mm.v)
                    wc_idx += 1

            p.run("vector", [lambda E: E.tensor_copy(outT[:, 0:4, :], ps_out[:, 0:4, :])],
                  deps=(s_mm,), inc=s_ev)
            p.run("scalar", [lambda E: E.activation(
                out=outT[:, 4:8, :].rearrange("p a t -> p (a t)"),
                in_=ps_out[:, 4:8, :].rearrange("p a t -> p (a t)"), func=AF.Copy)],
                deps=(s_mm,), inc=s_a)
            for s in range(nsub):
                i = g * nsub + s
                for half in range(2):
                    ks = range(half * 4, half * 4 + 4)
                    if not (g == 0 and s == 0 and half == 0):
                        p.wait("tensor", s_v)
                    if s == 0 and half == 0:
                        p.wait("tensor", s_ev)
                        p.wait("tensor", s_a)
                    p.run("tensor",
                          [(lambda E, k=k, s=s, half=half:
                            E.transpose(ps_tp[:, half * 512 + (k % 4) * 128:
                                              half * 512 + (k % 4) * 128 + 128],
                                        outT[:, k, s * 128:(s + 1) * 128],
                                        identf[:]))
                           for k in ks], inc=s_tp)
                    if half == 0 and i >= 2:
                        p.wait("vector", s_st[i % 2], 16 * ((i - 2) // 2 + 1))
                    p.run("vector",
                          [(lambda E, i=i, k=k, half=half:
                            E.scalar_tensor_tensor(
                                fin[i % 2][:, k * 128:(k + 1) * 128],
                                ps_tp[:, half * 512 + (k % 4) * 128:
                                      half * 512 + (k % 4) * 128 + 128], 0.5,
                                xt[i % (2 * nsub)][:, k * 128:(k + 1) * 128],
                                ALU.mult, ALU.add))
                           for k in ks], deps=(s_tp,), inc=s_v)
                p.dma("sync", x_out[g * tg + s * 128: g * tg + (s + 1) * 128, :],
                      fin[i % 2][:], deps=(s_v,), inc=s_st[i % 2])
                p.run("gpsimd", lambda E: E.engine_nop(), deps=(s_v,), inc=s_fin)
        p.wait("sync", s_st[0])
        p.wait("sync", s_st[1])
        p.build(nc)
    return nc


def f0_weights_host(rail_maker, rail_up_proj, w_cproj):
    w1 = np.zeros((128, 32, 128), np.float32)
    w2 = np.zeros((128, 32, 512), np.float32)
    for h in range(4):
        for kc in range(8):
            for r in range(8):
                w1[16 * r:16 * r + 16, h * 8 + kc, 16 * r:16 * r + 16] = \
                    rail_maker[h, 8 * kc + r]
                w2[16 * r:16 * r + 16, h * 8 + kc, 64 * r:64 * r + 64] = \
                    rail_up_proj[h, 8 * kc + r].T
    wcT = np.ascontiguousarray(w_cproj.T)
    return (w1.reshape(128, 32 * 128), w2.reshape(128, 32 * 512), wcT)



def build_scan(nsteps=2048, CH=128):
    nc = bass.Bass("TRN2", target_bir_lowering=False, debug=False)
    WT_in = nc.dram_tensor("WT", [128, 8 * 1024], BF16, kind="ExternalInput").ap()
    h0_in = nc.dram_tensor("h0", [128, 16], F32, kind="ExternalInput").ap()
    a_pl = [nc.dram_tensor(f"a{b}", [128, nsteps, 8], F32, kind="ExternalInput").ap()
            for b in range(2)]
    c_pl = [nc.dram_tensor(f"c{b}", [128, nsteps, 8], F32, kind="ExternalInput").ap()
            for b in range(2)]
    ident_in = nc.dram_tensor("identin", [128, 128], F32, kind="ExternalInput").ap()
    hs_pl = [nc.dram_tensor(f"hs{b}", [128, nsteps, 8], F32, kind="ExternalOutput").ap()
             for b in range(2)]
    hf_out = nc.dram_tensor("hf", [128, 16], F32, kind="ExternalOutput").ap()

    nch = nsteps // CH
    with ExitStack() as ctx:
        e = ctx.enter_context
        WTs = e(nc.sbuf_tensor("WTs", [128, 8, 1024], BF16))
        identf = e(nc.sbuf_tensor("identf", [128, 128], F32))
        identb = e(nc.sbuf_tensor("identb", [128, 128], BF16))
        hTf = e(nc.sbuf_tensor("hTf", [128, 16], F32))
        hTb = e(nc.sbuf_tensor("hTb", [128, 16], BF16))
        u_sb = e(nc.sbuf_tensor("u_sb", [2, 1024], BF16))
        s_sb = e(nc.sbuf_tensor("s_sb", [128, 16], F32))
        g_sb = e(nc.sbuf_tensor("g_sb", [128, 16], F32))
        d_sb = e(nc.sbuf_tensor("d_sb", [128, 16], F32))
        m_sb = e(nc.sbuf_tensor("m_sb", [128, 16], F32))
        aring = [[e(nc.sbuf_tensor(f"ar{b}_{j}", [128, CH, 8], F32)) for j in range(2)]
                 for b in range(2)]
        cring = [[e(nc.sbuf_tensor(f"cr{b}_{j}", [128, CH, 8], F32)) for j in range(2)]
                 for b in range(2)]
        stg = [[e(nc.sbuf_tensor(f"st{b}_{j}", [128, CH, 8], F32)) for j in range(2)]
               for b in range(2)]
        u_ps = e(nc.psum_tensor("u_ps", [128, 256], F32))
        uT_ps = e(nc.psum_tensor("uT_ps", [128, 16], BF16))

        s_w = Sem(ctx, nc, "s_w")
        s_ra = [[Sem(ctx, nc, f"s_ra{b}_{j}") for j in range(2)] for b in range(2)]
        s_rc = [[Sem(ctx, nc, f"s_rc{b}_{j}") for j in range(2)] for b in range(2)]
        s_so = [[Sem(ctx, nc, f"s_so{b}_{j}") for j in range(2)] for b in range(2)]
        s_mm = Sem(ctx, nc, "s_mm")
        s_ev = Sem(ctx, nc, "s_ev")
        s_tp = Sem(ctx, nc, "s_tp")
        s_sg = Sem(ctx, nc, "s_sg")
        s_sig = Sem(ctx, nc, "s_sig")
        s_d = Sem(ctx, nc, "s_d")
        s_m = Sem(ctx, nc, "s_m")
        s_h = Sem(ctx, nc, "s_h")
        s_cast = Sem(ctx, nc, "s_cast")
        s_stg = Sem(ctx, nc, "s_stg")

        s_i = Sem(ctx, nc, "s_i")
        p = Prog()
        p.dma("sync", WTs[:], WT_in.rearrange("p (k n) -> p k n", k=8), inc=s_w)
        p.dma("sync", hTf[:], h0_in, inc=s_w)
        p.dma("sync", identf[:], ident_in, inc=s_w)
        p.run("scalar", lambda E: E.activation(out=identb[:], in_=identf[:], func=AF.Copy),
              deps=(s_w,), inc=s_i)
        p.run("scalar", lambda E: E.activation(out=hTb[:], in_=hTf[:], func=AF.Copy),
              inc=s_cast)
        p.wait("tensor", s_i)
        p.wait("vector", s_i)

        for ch in range(2):
            for b in range(2):
                p.dma("sync", aring[b][ch][:],
                      a_pl[b][:, ch * CH:(ch + 1) * CH, :], inc=s_ra[b][ch])
                p.dma("sync", cring[b][ch][:],
                      c_pl[b][:, ch * CH:(ch + 1) * CH, :], inc=s_rc[b][ch])

        for t in range(nsteps):
            ch = t // CH
            sl = ch % 2
            tt = t % CH

            if tt == 0:
                ldv = 16 * (ch // 2 + 1)
                for b in range(2):
                    p.wait("tensor", s_ra[b][sl].h, ldv)
                    p.wait("tensor", s_rc[b][sl].h, ldv)
                    p.wait("vector", s_ra[b][sl].h, ldv)
                    p.wait("vector", s_rc[b][sl].h, ldv)
                if ch >= 2:
                    sov = 16 * (ch // 2)
                    p.wait("gpsimd", s_so[0][sl].h, sov)
                    p.wait("gpsimd", s_so[1][sl].h, sov)
                nch_pre = ch + 2
                if nch_pre < nch:
                    p.wait("sync", s_h, nch_pre * CH - CH)
                    for b in range(2):
                        p.dma("sync", aring[b][nch_pre % 2][:],
                              a_pl[b][:, nch_pre * CH:(nch_pre + 1) * CH, :],
                              inc=s_ra[b][nch_pre % 2])
                        p.dma("sync", cring[b][nch_pre % 2][:],
                              c_pl[b][:, nch_pre * CH:(nch_pre + 1) * CH, :],
                              inc=s_rc[b][nch_pre % 2])

            p.wait("tensor", s_cast, t + 1)
            if t > 0:
                p.wait("tensor", s_ev, 4 * t)
            mms = []
            for k in range(8):
                for c4 in range(4):
                    mms.append(lambda E, k=k, c4=c4:
                               E.matmul(u_ps[32 * c4:32 * c4 + 2, 0:256],
                                        hTb[:, 2 * k:2 * k + 2],
                                        WTs[:, k, 256 * c4:256 * c4 + 256],
                                        start=(k == 0), stop=(k == 7),
                                        tile_position=(0, 32 * c4)))
            p.run("tensor", mms, inc=s_mm)

            p.wait("vector", s_cast, t + 1)
            p.run("vector",
                  [(lambda E, b=b, sl=sl, tt=tt:
                    E.tensor_sub(d_sb[:, b:16:2], aring[b][sl][:, tt, :],
                                 hTf[:, b:16:2]))
                   for b in range(2)], inc=s_d, inc_n=2)

            p.wait("vector", s_mm, t + 1)
            p.run("vector", lambda E: E.tensor_copy(u_sb[0:2, 0:256], u_ps[0:2, 0:256]),
                  inc=s_ev)
            p.run("vector", lambda E: E.tensor_copy(u_sb[0:2, 256:512], u_ps[32:34, 0:256]),
                  inc=s_ev)
            p.wait("scalar", s_mm, t + 1)
            p.run("scalar", lambda E: E.copy(u_sb[0:2, 512:768], u_ps[64:66, 0:256]),
                  inc=s_ev)
            p.run("scalar", lambda E: E.copy(u_sb[0:2, 768:1024], u_ps[96:98, 0:256]),
                  inc=s_ev)

            p.wait("tensor", s_ev, 4 * (t + 1))
            if t > 0:
                p.wait("tensor", s_sg, 2 * t)
            tps = [(lambda E, c=c:
                    E.transpose(uT_ps[:, 2 * c:2 * c + 2],
                                u_sb[0:2, 128 * c:128 * c + 128],
                                identb[0:2, 0:2]))
                   for c in range(8)]
            p.run("tensor", tps, inc=s_tp)

            p.wait("vector", s_tp, t + 1)
            p.run("vector",
                  [(lambda E, b=b, sl=sl, tt=tt:
                    E.tensor_add(s_sb[:, b:16:2], uT_ps[:, b:16:2],
                                 cring[b][sl][:, tt, :]))
                   for b in range(2)], inc=s_sg, inc_n=2)

            p.wait("scalar", s_sg, 2 * (t + 1))
            p.run("scalar", lambda E: E.activation(out=g_sb[:], in_=s_sb[:],
                                                   func=AF.Sigmoid), inc=s_sig)

            p.wait("vector", s_sig, t + 1)
            p.wait("vector", s_d, 2 * (t + 1))
            p.run("vector", lambda E: E.tensor_mul(m_sb[:], g_sb[:], d_sb[:]), inc=s_m)
            p.wait("vector", s_m, t + 1)
            if t > 0:
                p.wait("vector", s_stg, 2 * t)
            p.run("vector", lambda E: E.tensor_add(hTf[:], hTf[:], m_sb[:]), inc=s_h)

            p.wait("scalar", s_h, t + 1)
            p.run("scalar", lambda E: E.activation(out=hTb[:], in_=hTf[:], func=AF.Copy),
                  inc=s_cast)
            p.wait("gpsimd", s_h, t + 1)
            p.run("gpsimd",
                  [(lambda E, b=b, sl=sl, tt=tt:
                    E.tensor_copy(stg[b][sl][:, tt, :], hTf[:, b:16:2]))
                   for b in range(2)], inc=s_stg, inc_n=2)

            if tt == CH - 1:
                p.wait("sync", s_stg, 2 * (t + 1))
                for b in range(2):
                    p.dma("sync", hs_pl[b][:, ch * CH:(ch + 1) * CH, :],
                          stg[b][sl][:], inc=s_so[b][sl])

        p.wait("sync", s_h, nsteps)
        p.dma("sync", hf_out, hTf[:], inc=s_w)
        for b in range(2):
            for j in range(2):
                p.wait("sync", s_so[b][j].h, s_so[b][j].v)
        p.wait("sync", s_w, s_w.v)
        p.build(nc)
    return nc



def host_wT(W):
    return np.ascontiguousarray(
        W.T.reshape(8, 128, 8, 128).transpose(1, 0, 2, 3).reshape(128, 8 * 8 * 128))


def host_colvec(v):
    return np.ascontiguousarray(v.reshape(8, 128).T)


def emit_dense8(p, nc, lhsT_of, rhs_of, ps, epi, s_mm, s_ep, nk=8,
                epi_eng="scalar", tg=128, extra=None):
    eps_after = []
    for m in range(8):
        off = (m % 2) * 512
        if m >= 2:
            p.wait("tensor", s_ep, eps_after[m - 2])
        else:
            p.wait("tensor", s_ep)
        mms = [(lambda E, m=m, k=k, off=off:
                E.matmul(ps[:, off:off + tg], lhsT_of(m, k), rhs_of(k),
                         start=(k == 0), stop=(k == nk - 1 and extra is None)))
               for k in range(nk)]
        if extra is not None:
            mms += [(lambda E, m=m, k=k, off=off:
                     E.matmul(ps[:, off:off + tg], *extra(m, k),
                              start=False, stop=(k == nk - 1)))
                    for k in range(nk)]
        p.run("tensor", mms, inc=s_mm)
        p.wait(epi_eng, s_mm)
        epi(m, ps[:, off:off + tg])
        eps_after.append(s_ep.v)


def build_a0pre(ntok=TPC):
    nc = bass.Bass("TRN2", target_bir_lowering=False, debug=False)
    x_in = nc.dram_tensor("x", [ntok, D], F32, kind="ExternalInput").ap()
    wiT = nc.dram_tensor("wiT", [128, 8192], BF16, kind="ExternalInput").ap()
    wifT = nc.dram_tensor("wifT", [128, 8192], BF16, kind="ExternalInput").ap()
    gam_in = nc.dram_tensor("gam", [1024], F32, kind="ExternalInput")
    lb_in = nc.dram_tensor("lb", [128, 8], F32, kind="ExternalInput").ap()
    ident_in = nc.dram_tensor("identin", [128, 128], F32, kind="ExternalInput").ap()
    xn_out = nc.dram_tensor("xn", [ntok, D], F32, kind="ExternalOutput").ap()
    a_out = nc.dram_tensor("aop", [128, ntok, 8], F32, kind="ExternalOutput").ap()
    c_out = nc.dram_tensor("cop", [128, ntok, 8], F32, kind="ExternalOutput").ap()

    nt = ntok // 128
    with ExitStack() as ctx:
        e = ctx.enter_context
        wi = e(nc.sbuf_tensor("wi", [128, 8, 8, 128], BF16))
        wif = e(nc.sbuf_tensor("wif", [128, 8, 8, 128], BF16))
        identf = e(nc.sbuf_tensor("identf", [128, 128], F32))
        identb = e(nc.sbuf_tensor("identb", [128, 128], BF16))
        gamr = e(nc.sbuf_tensor("gamr", [128, 1024], F32))
        lb = e(nc.sbuf_tensor("lbs", [128, 8], F32))
        xt = [e(nc.sbuf_tensor(f"xt{i}", [128, D], F32)) for i in range(2)]
        sq = e(nc.sbuf_tensor("sq", [128, D], F32))
        sums = e(nc.sbuf_tensor("sums", [128, 4], F32))
        mu = e(nc.sbuf_tensor("mu", [128, 1], F32))
        rstd = e(nc.sbuf_tensor("rstd", [128, 1], F32))
        lno = e(nc.sbuf_tensor("lno", [128, D], F32))
        xn_f = [e(nc.sbuf_tensor(f"xnf{i}", [128, D], F32)) for i in range(2)]
        xn_b = e(nc.sbuf_tensor("xnb", [128, D], BF16))
        xnT = e(nc.sbuf_tensor("xnT", [128, 8, 128], BF16))
        tmp_a = e(nc.sbuf_tensor("tmpa", [128, 128], F32))
        astg = [e(nc.sbuf_tensor(f"astg{i}", [128, 128, 8], F32)) for i in range(2)]
        cstg = [e(nc.sbuf_tensor(f"cstg{i}", [128, 128, 8], F32)) for i in range(2)]
        ps_tp = e(nc.psum_tensor("ps_tp", [128, 1024], F32))
        ps_mm = e(nc.psum_tensor("ps_mm", [128, 1024], F32))

        def tpb(k):
            return ps_tp[:].bitcast(BF16)[:, k * 128:(k + 1) * 128]

        s_w = Sem(ctx, nc, "s_w")
        s_x = [Sem(ctx, nc, f"s_x{j}") for j in range(2)]
        s_a = Sem(ctx, nc, "s_a")
        s_v = Sem(ctx, nc, "s_v")
        s_tp = Sem(ctx, nc, "s_tp")
        s_ev = Sem(ctx, nc, "s_ev")
        s_mm = Sem(ctx, nc, "s_mm")
        s_ep = Sem(ctx, nc, "s_ep")
        s_st = [Sem(ctx, nc, f"s_st{j}") for j in range(2)]
        s_stn = [Sem(ctx, nc, f"s_stn{j}") for j in range(2)]
        cst = Consts(ctx, nc, [1e-5])
        s_c = Sem(ctx, nc, "s_c")

        p = Prog()
        cst.emit(p, s_c)
        p.dma("sync", wi[:], wiT.rearrange("p (a b m) -> p a b m", a=8, b=8), inc=s_w)
        p.dma("sync", wif[:], wifT.rearrange("p (a b m) -> p a b m", a=8, b=8), inc=s_w)
        p.dma("sync", identf[:], ident_in, inc=s_w)
        p.dma("sync", gamr[:], bass.AP(gam_in, 0, [[0, 128], [1, 1024]]), inc=s_w)
        p.dma("sync", lb[:], lb_in, inc=s_w)
        p.run("scalar", lambda E: E.activation(out=identb[:], in_=identf[:], func=AF.Copy),
              deps=(s_w, s_c), inc=s_a)
        p.run("vector", lambda E: E.tensor_scalar(gamr[:], gamr[:], 1.0, float(np.sqrt(D)),
                                                  ALU.add, ALU.mult),
              deps=(s_w, s_c), inc=s_v)
        for eng in ("tensor", "vector", "scalar"):
            p.wait(eng, s_w)
            p.wait(eng, s_a)
            p.wait(eng, s_v)

        for i in range(nt):
            j = i % 2
            if i >= 2:
                p.wait("sync", s_stn[j].h, 16 * (i // 2))
            p.dma("sync", xt[j][:], x_in[i * 128:(i + 1) * 128, :], inc=s_x[j])
            p.wait("scalar", s_x[j].h, 16 * (i // 2 + 1))
            p.wait("vector", s_x[j].h, 16 * (i // 2 + 1))
            p.wait("scalar", s_v)
            emit_ln_tile(p, i, xt[j][:], None, sq, sums, mu, rstd, s_a, s_v,
                         out_f32=lno[:])
            p.run("scalar", lambda E: E.activation(out=sq[:], in_=lno[:], func=AF.Square),
                  deps=(s_v,), inc=s_a)
            p.run("vector", [lambda E: E.reduce_sum(sums[:, 0:1], sq[:], axis=AX.X)],
                  deps=(s_a,), inc=s_v)
            p.run("scalar", lambda E: E.activation(out=sums[:, 1:2], in_=sums[:, 0:1],
                                                   func=AF.Sqrt), deps=(s_v,), inc=s_a)
            p.run("vector", [lambda E: E.tensor_single_scalar(sums[:, 1:2], sums[:, 1:2],
                                                              1e-12, ALU.max)],
                  deps=(s_a,), inc=s_v)
            p.run("vector", [lambda E: E.reciprocal(rstd[:], sums[:, 1:2])],
                  deps=(s_v,), inc=s_v)
            p.run("vector", [lambda E, j=j: E.tensor_scalar(xn_f[j][:], lno[:], rstd[:],
                                                            None, ALU.mult)],
                  deps=(s_v,), inc=s_v)
            if i >= 2:
                p.wait("vector", s_stn[j].h, 16 * (i // 2))
            p.run("vector", [lambda E, j=j: E.tensor_mul(xn_f[j][:], xn_f[j][:], gamr[:])],
                  deps=(s_v,), inc=s_v)
            p.run("vector", [lambda E, j=j: E.tensor_copy(xn_b[:], xn_f[j][:])],
                  deps=(s_v,), inc=s_v)
            p.dma("sync", xn_out[i * 128:(i + 1) * 128, :], xn_f[j][:],
                  deps=(s_v,), inc=s_stn[j])
            p.wait("vector", s_mm)
            emit_tp_in(p, xn_b, (lambda k: xnT[:, k, :]), identb, tpb, s_tp, s_ev,
                       deps=(s_v,))
            p.wait("tensor", s_ev)
            if i >= 2:
                p.wait("scalar", s_st[j].h, 16 * 2 * (i // 2))

            def epi_a(m, reg, j=j):
                p.run("scalar", (lambda E, reg=reg: E.activation(out=tmp_a[:], in_=reg,
                                                                 func=AF.Tanh)),
                      deps=(s_ep,), inc=s_ep)
                p.run("scalar", (lambda E, m=m, j=j: E.activation(
                    out=astg[j][:, :, m], in_=tmp_a[:], func=AF.Tanh)),
                    deps=(s_ep,), inc=s_ep)

            emit_dense8(p, nc, (lambda m, k: wi[:, k, m, :]),
                        (lambda k: xnT[:, k, :]), ps_mm, epi_a, s_mm, s_ep)

            def epi_c(m, reg, j=j):
                p.run("scalar", (lambda E, m=m, reg=reg, j=j: E.activation(
                    out=cstg[j][:, :, m], in_=reg, func=AF.Identity, bias=lb[:, m:m + 1])),
                    inc=s_ep)

            emit_dense8(p, nc, (lambda m, k: wif[:, k, m, :]),
                        (lambda k: xnT[:, k, :]), ps_mm, epi_c, s_mm, s_ep)
            p.dma("sync", a_out[:, i * 128:(i + 1) * 128, :], astg[j][:],
                  deps=(s_ep,), inc=s_st[j])
            p.dma("sync", c_out[:, i * 128:(i + 1) * 128, :], cstg[j][:],
                  deps=(s_ep,), inc=s_st[j])
        for j in range(2):
            p.wait("sync", s_st[j].h, s_st[j].v)
            p.wait("sync", s_stn[j].h, s_stn[j].v)
        p.build(nc)
    return nc


def build_a0mid(ntok=TPC):
    nc = bass.Bass("TRN2", target_bir_lowering=False, debug=False)
    hs_in = nc.dram_tensor("hsp", [128, ntok, 8], F32, kind="ExternalInput").ap()
    wiT = nc.dram_tensor("wiT", [128, 8192], BF16, kind="ExternalInput").ap()
    wifT = nc.dram_tensor("wifT", [128, 8192], BF16, kind="ExternalInput").ap()
    lb_in = nc.dram_tensor("lb", [128, 8], F32, kind="ExternalInput").ap()
    a_out = nc.dram_tensor("aop", [128, ntok, 8], F32, kind="ExternalOutput").ap()
    c_out = nc.dram_tensor("cop", [128, ntok, 8], F32, kind="ExternalOutput").ap()

    nt = ntok // 128
    with ExitStack() as ctx:
        e = ctx.enter_context
        wi = e(nc.sbuf_tensor("wi", [128, 8, 8, 128], BF16))
        wif = e(nc.sbuf_tensor("wif", [128, 8, 8, 128], BF16))
        lb = e(nc.sbuf_tensor("lbs", [128, 8], F32))
        ht = [e(nc.sbuf_tensor(f"ht{i}", [128, 128, 8], F32)) for i in range(2)]
        hT = e(nc.sbuf_tensor("hT", [128, 8, 128], BF16))
        tmp_a = e(nc.sbuf_tensor("tmpa", [128, 128], F32))
        astg = [e(nc.sbuf_tensor(f"astg{i}", [128, 128, 8], F32)) for i in range(2)]
        cstg = [e(nc.sbuf_tensor(f"cstg{i}", [128, 128, 8], F32)) for i in range(2)]
        ps_mm = e(nc.psum_tensor("ps_mm", [128, 1024], F32))

        s_w = Sem(ctx, nc, "s_w")
        s_x = [Sem(ctx, nc, f"s_x{j}") for j in range(2)]
        s_v = Sem(ctx, nc, "s_v")
        s_mm = Sem(ctx, nc, "s_mm")
        s_ep = Sem(ctx, nc, "s_ep")
        s_st = [Sem(ctx, nc, f"s_st{j}") for j in range(2)]

        p = Prog()
        p.dma("sync", wi[:], wiT.rearrange("p (a b m) -> p a b m", a=8, b=8), inc=s_w)
        p.dma("sync", wif[:], wifT.rearrange("p (a b m) -> p a b m", a=8, b=8), inc=s_w)
        p.dma("sync", lb[:], lb_in, inc=s_w)
        for eng in ("tensor", "vector", "scalar"):
            p.wait(eng, s_w)

        for i in range(nt):
            j = i % 2
            p.dma("sync", ht[j][:], hs_in[:, i * 128:(i + 1) * 128, :], inc=s_x[j])
            p.wait("vector", s_x[j].h, 16 * (i // 2 + 1))
            p.wait("vector", s_mm)
            p.run("vector", [(lambda E, k=k, j=j: E.tensor_copy(hT[:, k, :], ht[j][:, :, k]))
                             for k in range(8)], inc=s_v)
            p.wait("tensor", s_v)
            if i >= 2:
                p.wait("scalar", s_st[j].h, 16 * 2 * (i // 2))

            def epi_a(m, reg, j=j):
                p.run("scalar", (lambda E, reg=reg: E.activation(out=tmp_a[:], in_=reg,
                                                                 func=AF.Tanh)),
                      deps=(s_ep,), inc=s_ep)
                p.run("scalar", (lambda E, m=m, j=j: E.activation(
                    out=astg[j][:, :, m], in_=tmp_a[:], func=AF.Tanh)),
                    deps=(s_ep,), inc=s_ep)

            emit_dense8(p, nc, (lambda m, k: wi[:, k, m, :]),
                        (lambda k: hT[:, k, :]), ps_mm, epi_a, s_mm, s_ep)

            def epi_c(m, reg, j=j):
                p.run("scalar", (lambda E, m=m, reg=reg, j=j: E.activation(
                    out=cstg[j][:, :, m], in_=reg, func=AF.Identity, bias=lb[:, m:m + 1])),
                    deps=(s_ep,), inc=s_ep)

            emit_dense8(p, nc, (lambda m, k: wif[:, k, m, :]),
                        (lambda k: hT[:, k, :]), ps_mm, epi_c, s_mm, s_ep)
            p.dma("sync", a_out[:, i * 128:(i + 1) * 128, :], astg[j][:],
                  deps=(s_ep,), inc=s_st[j])
            p.dma("sync", c_out[:, i * 128:(i + 1) * 128, :], cstg[j][:],
                  deps=(s_ep,), inc=s_st[j])
        for j in range(2):
            p.wait("sync", s_st[j].h, s_st[j].v)
        p.build(nc)
    return nc


def build_a0post(ntok=TPC):
    nc = bass.Bass("TRN2", target_bir_lowering=False, debug=False)
    x_in = nc.dram_tensor("x", [ntok, D], F32, kind="ExternalInput").ap()
    xn_in = nc.dram_tensor("xnin", [ntok, D], F32, kind="ExternalInput").ap()
    hs_in = nc.dram_tensor("hsp", [128, ntok, 8], F32, kind="ExternalInput").ap()
    wiT = nc.dram_tensor("wiT", [128, 8192], BF16, kind="ExternalInput").ap()
    wifT = nc.dram_tensor("wifT", [128, 8192], BF16, kind="ExternalInput").ap()
    whfT = nc.dram_tensor("whfT", [128, 8192], BF16, kind="ExternalInput").ap()
    gb_in = nc.dram_tensor("gb", [128, 8], F32, kind="ExternalInput").ap()
    ident_in = nc.dram_tensor("identin", [128, 128], F32, kind="ExternalInput").ap()
    x_out = nc.dram_tensor("xo", [ntok, D], F32, kind="ExternalOutput").ap()

    nt = ntok // 128
    with ExitStack() as ctx:
        e = ctx.enter_context
        wi = e(nc.sbuf_tensor("wi", [128, 8, 8, 128], BF16))
        wif = e(nc.sbuf_tensor("wif", [128, 8, 8, 128], BF16))
        whf = e(nc.sbuf_tensor("whf", [128, 8, 8, 128], BF16))
        identf = e(nc.sbuf_tensor("identf", [128, 128], F32))
        identb = e(nc.sbuf_tensor("identb", [128, 128], BF16))
        gb = e(nc.sbuf_tensor("gbs", [128, 8], F32))
        xt = [e(nc.sbuf_tensor(f"xt{i}", [128, D], F32)) for i in range(2)]
        xnt = [e(nc.sbuf_tensor(f"xnt{i}", [128, D], F32)) for i in range(2)]
        ht = [e(nc.sbuf_tensor(f"ht{i}", [128, 128, 8], F32)) for i in range(2)]
        hT = e(nc.sbuf_tensor("hT", [128, 8, 128], BF16))
        sq = e(nc.sbuf_tensor("sq", [128, D], F32))
        sums = e(nc.sbuf_tensor("sums", [128, 4], F32))
        mu = e(nc.sbuf_tensor("mu", [128, 1], F32))
        rstd = e(nc.sbuf_tensor("rstd", [128, 1], F32))
        lno = e(nc.sbuf_tensor("lno", [128, D], F32))
        xnb = e(nc.sbuf_tensor("xnb", [128, D], BF16))
        xnT_b = e(nc.sbuf_tensor("xnTb", [128, 8, 128], BF16))
        xnT_f = e(nc.sbuf_tensor("xnTf", [128, 8, 128], F32))
        tni = e(nc.sbuf_tensor("tni", [128, 8, 128], F32))
        gsb = e(nc.sbuf_tensor("gsb", [128, 8, 128], F32))
        yT = e(nc.sbuf_tensor("yT", [128, 8, 128], F32))
        fin = [e(nc.sbuf_tensor(f"fin{i}", [128, D], F32)) for i in range(2)]
        ps_tp = e(nc.psum_tensor("ps_tp", [128, 1024], F32))
        ps_mm = e(nc.psum_tensor("ps_mm", [128, 1024], F32))

        def tpb(k):
            return ps_tp[:].bitcast(BF16)[:, k * 128:(k + 1) * 128]

        s_w = Sem(ctx, nc, "s_w")
        s_x = [Sem(ctx, nc, f"s_x{j}") for j in range(2)]
        s_a = Sem(ctx, nc, "s_a")
        s_v = Sem(ctx, nc, "s_v")
        s_tp = Sem(ctx, nc, "s_tp")
        s_ev = Sem(ctx, nc, "s_ev")
        s_mm = Sem(ctx, nc, "s_mm")
        s_ep = Sem(ctx, nc, "s_ep")
        s_st = [Sem(ctx, nc, f"s_st{j}") for j in range(2)]
        cst = Consts(ctx, nc, [1e-5])
        s_c = Sem(ctx, nc, "s_c")

        p = Prog()
        cst.emit(p, s_c)
        p.dma("sync", wi[:], wiT.rearrange("p (a b m) -> p a b m", a=8, b=8), inc=s_w)
        p.dma("sync", wif[:], wifT.rearrange("p (a b m) -> p a b m", a=8, b=8), inc=s_w)
        p.dma("sync", whf[:], whfT.rearrange("p (a b m) -> p a b m", a=8, b=8), inc=s_w)
        p.dma("sync", identf[:], ident_in, inc=s_w)
        p.dma("sync", gb[:], gb_in, inc=s_w)
        p.run("scalar", lambda E: E.activation(out=identb[:], in_=identf[:], func=AF.Copy),
              deps=(s_w, s_c), inc=s_a)
        for eng in ("tensor", "vector", "scalar"):
            p.wait(eng, s_w)
            p.wait(eng, s_a)

        for i in range(nt):
            j = i % 2
            if i >= 2:
                p.wait("sync", s_st[j].h, 16 * (i // 2))
            p.dma("sync", xt[j][:], x_in[i * 128:(i + 1) * 128, :], inc=s_x[j])
            p.dma("sync", xnt[j][:], xn_in[i * 128:(i + 1) * 128, :], inc=s_x[j])
            p.dma("sync", ht[j][:], hs_in[:, i * 128:(i + 1) * 128, :], inc=s_x[j])
            p.wait("scalar", s_x[j].h, 48 * (i // 2 + 1))
            p.wait("vector", s_x[j].h, 48 * (i // 2 + 1))
            p.wait("vector", s_mm)
            p.run("vector", [(lambda E, k=k, j=j: E.tensor_copy(hT[:, k, :], ht[j][:, :, k]))
                             for k in range(8)], inc=s_v)
            p.run("vector", [(lambda E, j=j: E.tensor_copy(xnb[:], xnt[j][:]))],
                  deps=(s_v,), inc=s_v)
            emit_tp_in(p, xnb, (lambda k: xnT_b[:, k, :]), identb, tpb, s_tp, s_ev,
                       deps=(s_v,))
            tps = [(lambda E, k=k, j=j: E.transpose(
                ps_tp[:, (k % 4) * 128 + (k // 4) * 512:(k % 4) * 128 + (k // 4) * 512 + 128],
                xnt[j][0:128, k * 128:(k + 1) * 128], identf[:]))
                   for k in range(8)]
            p.run("tensor", tps, deps=(s_ev,), inc=s_tp)
            p.run("vector", [(lambda E, k=k: E.tensor_copy(
                xnT_f[:, k, :],
                ps_tp[:, (k % 4) * 128 + (k // 4) * 512:(k % 4) * 128 + (k // 4) * 512 + 128]))
                for k in range(8)], deps=(s_tp,), inc=s_ev)
            p.wait("tensor", s_ev)

            p.wait("scalar", s_v)

            def epi_ni(m, reg):
                p.run("scalar", (lambda E, m=m, reg=reg: E.activation(
                    out=tni[:, m, :], in_=reg, func=AF.Tanh)), deps=(s_ep,), inc=s_ep)
                p.run("scalar", (lambda E, m=m: E.activation(
                    out=tni[:, m, :], in_=tni[:, m, :], func=AF.Tanh)),
                    deps=(s_ep,), inc=s_ep)

            emit_dense8(p, nc, (lambda m, k: wi[:, k, m, :]),
                        (lambda k: hT[:, k, :]), ps_mm, epi_ni, s_mm, s_ep)

            def epi_g(m, reg):
                p.run("scalar", (lambda E, m=m, reg=reg: E.activation(
                    out=gsb[:, m, :], in_=reg, func=AF.Sigmoid, bias=gb[:, m:m + 1])),
                    deps=(s_ep,), inc=s_ep)

            emit_dense8(p, nc, (lambda m, k: whf[:, k, m, :]),
                        (lambda k: xnT_b[:, k, :]), ps_mm, epi_g, s_mm, s_ep,
                        extra=(lambda m, k: (wif[:, k, m, :], hT[:, k, :])))

            p.run("vector", [(lambda E: E.tensor_sub(tni[:], tni[:], xnT_f[:]))],
                  deps=(s_ep,), inc=s_v)
            p.run("vector", [(lambda E: E.tensor_mul(tni[:], gsb[:], tni[:]))],
                  deps=(s_v,), inc=s_v)
            p.run("vector", [(lambda E: E.tensor_add(yT[:], xnT_f[:], tni[:]))],
                  deps=(s_v,), inc=s_v)
            p.wait("scalar", s_v)
            emit_ln_tile(p, i, xt[j][:], None, sq, sums, mu, rstd, s_a, s_v,
                         out_f32=lno[:])
            p.run("vector", [(lambda E, j=j: E.tensor_add(lno[:], lno[:], xt[j][:]))],
                  deps=(s_v,), inc=s_v)
            for half in range(2):
                ks = range(half * 4, half * 4 + 4)
                p.wait("tensor", s_v)
                tps = [(lambda E, k=k, half=half: E.transpose(
                    ps_tp[:, half * 512 + (k % 4) * 128: half * 512 + (k % 4) * 128 + 128],
                    yT[:, k, :], identf[:])) for k in ks]
                p.run("tensor", tps, inc=s_tp)
                p.run("vector",
                      [(lambda E, k=k, half=half, j=j:
                        E.scalar_tensor_tensor(
                            fin[j][:, k * 128:(k + 1) * 128],
                            ps_tp[:, half * 512 + (k % 4) * 128:
                                  half * 512 + (k % 4) * 128 + 128], 1.0,
                            lno[:, k * 128:(k + 1) * 128], ALU.mult, ALU.add))
                       for k in ks], deps=(s_tp,), inc=s_v)
            p.dma("sync", x_out[i * 128:(i + 1) * 128, :], fin[j][:],
                  deps=(s_v,), inc=s_st[j])
        for j in range(2):
            p.wait("sync", s_st[j].h, s_st[j].v)
        p.build(nc)
    return nc



def _launch(nc, in_maps):
    return run_bass_kernel_spmd(nc, in_maps, core_ids=list(range(NCORES))).results


def _shard_tok(x):
    return [np.ascontiguousarray(x[i * TPC:(i + 1) * TPC]) for i in range(NCORES)]


def _ident():
    return np.eye(128, dtype=np.float32)


def h0_layout(v):
    out = np.zeros((128, 16), np.float32)
    for k in range(8):
        out[:, 2 * k] = v[k * 128:(k + 1) * 128]
        out[:, 2 * k + 1] = v[k * 128:(k + 1) * 128]
    return out


def run_f0(x, w1b, w2b, wcb, launch=_launch):
    nc = _cached(("f0",), lambda: build_f0())
    ins = [{"x": s, "w1bd": w1b, "w2bd": w2b, "wcT": wcb, "identin": _ident()}
           for s in _shard_tok(x)]
    res = launch(nc, ins)
    return np.concatenate([r["xo"] for r in res], axis=0)


def run_a0(x, W, launch=_launch):
    ident = _ident()
    nc = _cached(("a0pre",), lambda: build_a0pre())
    ins = [{"x": s, "wiT": W["lWiT0"], "wifT": W["lWifT0"], "gam": W["gamma"],
            "lb": W["lbhf0"], "identin": ident} for s in _shard_tok(x)]
    res = launch(nc, ins)
    xn = np.concatenate([r["xn"] for r in res], axis=0)
    a_pl = [np.concatenate([res[b * 4 + i]["aop"] for i in range(4)], axis=1)
            for b in range(2)]
    c_pl = [np.concatenate([res[b * 4 + i]["cop"] for i in range(4)], axis=1)
            for b in range(2)]

    nc = _cached(("a0mega",), lambda: build_a0mega(nsteps=8192, CH=128))
    im = {"WTA": W["WhfS0"], "WTB": W["WhfS1"],
          "h0A": h0_layout(W["l_init0"]), "h0B": h0_layout(W["l_init1"]),
          "wiT": W["lWiT1"], "wifT": W["lWifT1"], "lb": W["lbhf1"],
          "identin": ident}
    for b in range(2):
        im[f"a{b}"] = np.ascontiguousarray(a_pl[b])
        im[f"c{b}"] = np.ascontiguousarray(c_pl[b])
    res = launch(nc, [im] * NCORES)
    hs2 = [res[0][f"hs{b}"] for b in range(2)]

    nc = _cached(("a0post",), lambda: build_a0post())
    ins = [{"x": _shard_tok(x)[c], "xnin": _shard_tok(xn)[c],
            "hsp": np.ascontiguousarray(hs2[c // 4][:, (c % 4) * TPC:(c % 4 + 1) * TPC, :]),
            "wiT": W["gWiT"], "wifT": W["gWifT"], "whfT": W["gWhfT"],
            "gb": W["gbhf"], "identin": ident} for c in range(NCORES)]
    res = launch(nc, ins)
    return np.concatenate([r["xo"] for r in res], axis=0)


def host_scanWT(Whf):
    return np.ascontiguousarray(
        Whf.T.reshape(8, 128, 1024).transpose(1, 0, 2).reshape(128, 8192))


def prep_weights(inp, launch=_launch):
    W = {}

    def cast(name, arr):
        W[name] = cast_bf16_dev(arr, launch)

    w1, w2, wcT = f0_weights_host(inp["L0_ffn_rail_maker"],
                                  inp["L0_ffn_rail_up_proj"], inp["L0_ffn_w_cproj"])
    cast("f0w1", w1); cast("f0w2", w2); cast("f0wcT", wcT)
    cast("lWiT0", host_wT(inp["L0_attn_lWi"][0]))
    cast("lWifT0", host_wT(inp["L0_attn_lWif"][0]))
    cast("lWiT1", host_wT(inp["L0_attn_lWi"][1]))
    cast("lWifT1", host_wT(inp["L0_attn_lWif"][1]))
    cast("gWiT", host_wT(inp["L0_attn_gWi"]))
    cast("gWifT", host_wT(inp["L0_attn_gWif"]))
    cast("gWhfT", host_wT(inp["L0_attn_gWhf"]))
    cast("WhfS0", host_scanWT(inp["L0_attn_lWhf"][0]))
    cast("WhfS1", host_scanWT(inp["L0_attn_lWhf"][1]))
    W["gamma"] = np.ascontiguousarray(inp["L0_attn_gamma"], np.float32)
    W["lbhf0"] = host_colvec(inp["L0_attn_lbhf"][0])
    W["lbhf1"] = host_colvec(inp["L0_attn_lbhf"][1])
    W["gbhf"] = host_colvec(inp["L0_attn_gbhf"])
    W["l_init0"] = np.ascontiguousarray(inp["L0_attn_l_init_hidden"][0], np.float32)
    W["l_init1"] = np.ascontiguousarray(inp["L0_attn_l_init_hidden"][1], np.float32)
    return W


def cast_bf16_many(arrs, launch):
    flats = {k: np.ascontiguousarray(v, np.float32).reshape(-1) for k, v in arrs.items()}
    total = sum(v.size for v in flats.values())
    COLS = 8192
    rows = (total + COLS - 1) // COLS
    big = np.zeros((rows * COLS,), np.float32)
    offs = {}
    off = 0
    for k, v in flats.items():
        big[off:off + v.size] = v
        offs[k] = off
        off += v.size
    out = np.asarray(cast_bf16_dev(big.reshape(rows, COLS), launch)).reshape(-1)
    return {k: np.ascontiguousarray(
        out[offs[k]:offs[k] + flats[k].size].reshape(arrs[k].shape))
        for k in arrs}


def prep_weights2(inp, launch=_launch):
    todo = {}
    w1, w2, wcT = f0_weights_host(inp["L0_ffn_rail_maker"],
                                  inp["L0_ffn_rail_up_proj"], inp["L0_ffn_w_cproj"])
    todo["f0w1"] = w1
    todo["f0w2"] = w2
    todo["f0wcT"] = wcT
    todo["lWiT0"] = host_wT(inp["L0_attn_lWi"][0])
    todo["lWifT0"] = host_wT(inp["L0_attn_lWif"][0])
    todo["lWiT1"] = host_wT(inp["L0_attn_lWi"][1])
    todo["lWifT1"] = host_wT(inp["L0_attn_lWif"][1])
    todo["gWiT"] = host_wT(inp["L0_attn_gWi"])
    todo["gWifT"] = host_wT(inp["L0_attn_gWif"])
    todo["gWhfT"] = host_wT(inp["L0_attn_gWhf"])
    todo["WhfS0"] = host_scanWT(inp["L0_attn_lWhf"][0])
    todo["WhfS1"] = host_scanWT(inp["L0_attn_lWhf"][1])
    Wt = [None, None]
    for l in range(2):
        Wl = t1l_weights_host(inp, l)
        for k in ("wqkv", "wo", "w1", "w2"):
            todo[f"t1l{l}_{k}"] = Wl[k]
        Wt[l] = Wl
    todo["WqT"] = host_swapT(inp["L1_attn_Wq"])
    todo["WkT"] = host_swapT(inp["L1_attn_Wk"])
    todo["WvT"] = host_swapT(inp["L1_attn_Wv"])
    todo["WprojT"] = host_wT(inp["L1_attn_Wproj"])
    casted = cast_bf16_many(todo, launch)
    W = {k: casted[k] for k in ("f0w1", "f0w2", "f0wcT", "lWiT0", "lWifT0",
                                "lWiT1", "lWifT1", "gWiT", "gWifT", "gWhfT",
                                "WhfS0", "WhfS1", "WqT", "WkT", "WvT", "WprojT")}
    for l in range(2):
        for k in ("wqkv", "wo", "w1", "w2"):
            Wt[l][k] = casted[f"t1l{l}_{k}"]
    W["t1l"] = Wt
    W["gamma"] = np.ascontiguousarray(inp["L0_attn_gamma"], np.float32)
    W["lbhf0"] = host_colvec(inp["L0_attn_lbhf"][0])
    W["lbhf1"] = host_colvec(inp["L0_attn_lbhf"][1])
    W["gbhf"] = host_colvec(inp["L0_attn_gbhf"])
    W["l_init0"] = np.ascontiguousarray(inp["L0_attn_l_init_hidden"][0], np.float32)
    W["l_init1"] = np.ascontiguousarray(inp["L0_attn_l_init_hidden"][1], np.float32)
    W["lq1"] = np.asarray(inp["L1_attn_lambda_q1"], np.float32)
    W["lk1"] = np.asarray(inp["L1_attn_lambda_k1"], np.float32)
    W["lq2"] = np.asarray(inp["L1_attn_lambda_q2"], np.float32)
    W["lk2"] = np.asarray(inp["L1_attn_lambda_k2"], np.float32)
    return W


def kernel(**inp):
    x = np.ascontiguousarray(np.asarray(inp["L0_attn_x"], np.float32).reshape(T_FULL, D))
    W = prep_weights2(inp)
    x = run_f0(x, W["f0w1"], W["f0w2"], W["f0wcT"])
    x = run_a0(x, W)
    x = run_f0(x, W["f0w1"], W["f0w2"], W["f0wcT"])
    x = run_l1ffn_macaron(x, W["t1l"][0], W["t1l"][1])
    x = run_l1attn(x, W)
    x = run_l1ffn_macaron(x, W["t1l"][0], W["t1l"][1])
    return x.reshape(2, 8192, 1024)



HALO = 640


def build_t1l_attn(ntok=TPC):
    N = ntok + HALO
    nt = N // 128
    NQB = ntok // 512
    nc = bass.Bass("TRN2", target_bir_lowering=False, debug=False)
    z2 = nc.dram_tensor("z2", [N + 1, D], F32, kind="ExternalInput").ap()
    wq_in = nc.dram_tensor("wqkv", [128, 8 * 1536], BF16, kind="ExternalInput").ap()
    wo_in = nc.dram_tensor("wo", [128, 4 * 1024], BF16, kind="ExternalInput").ap()
    tab_in = nc.dram_tensor("tabs", [N, 4, 64], F32, kind="ExternalInput").ap()
    qs_in = nc.dram_tensor("qs", [64], F32, kind="ExternalInput")
    ks_in = nc.dram_tensor("ks", [64], F32, kind="ExternalInput")
    mask_in = nc.dram_tensor("mask4", [NQB, 512, 1024], F32, kind="ExternalInput").ap()
    ident_in = nc.dram_tensor("identin", [128, 128], F32, kind="ExternalInput").ap()
    xo = nc.dram_tensor("xo", [ntok, D], F32, kind="ExternalOutput").ap()

    with ExitStack() as ctx:
        e = ctx.enter_context
        wq = e(nc.sbuf_tensor("wq", [128, 8, 1536], BF16))
        wo = e(nc.sbuf_tensor("wos", [128, 4, 1024], BF16))
        identf = e(nc.sbuf_tensor("identf", [128, 128], F32))
        identb = e(nc.sbuf_tensor("identb", [128, 128], BF16))
        qs_r = e(nc.sbuf_tensor("qsr", [128, 64], F32))
        ks_r = e(nc.sbuf_tensor("ksr", [128, 64], F32))
        zt = [e(nc.sbuf_tensor(f"zt{i}", [128, D], F32)) for i in range(2)]
        zs = [e(nc.sbuf_tensor(f"zs{i}", [128, D], F32)) for i in range(2)]
        zb = e(nc.sbuf_tensor("zb", [128, D], BF16))
        zsb = e(nc.sbuf_tensor("zsb", [128, D], BF16))
        hT = e(nc.sbuf_tensor("hTt", [128, 8, 128], BF16))
        tab = [e(nc.sbuf_tensor(f"tab{i}", [128, 4, 64], F32)) for i in range(2)]
        qf = e(nc.sbuf_tensor("qf", [128, 8, 64], F32))
        kf = e(nc.sbuf_tensor("kf", [128, 8, 64], F32))
        rt = e(nc.sbuf_tensor("rt", [128, 8, 64], F32))
        nrm = e(nc.sbuf_tensor("nrm", [128, 8], F32))
        sqs = e(nc.sbuf_tensor("sqs", [128, 8, 64], F32))
        qT = e(nc.sbuf_tensor("qT", [128, 4, N], BF16))
        kT = e(nc.sbuf_tensor("kT", [128, 4, N], BF16))
        vtok = e(nc.sbuf_tensor("vtok", [128, nt, 512], BF16))
        mask = e(nc.sbuf_tensor("maskt", [128, 4, 1024], F32))
        negmax = e(nc.sbuf_tensor("negmax", [128, 1], F32))
        ssum = e(nc.sbuf_tensor("ssum", [128, 1], F32))
        rs = e(nc.sbuf_tensor("rs", [128, 1], F32))
        P = e(nc.sbuf_tensor("P", [128, 1024], BF16))
        PT = e(nc.sbuf_tensor("PT", [128, 8, 128], BF16))
        osb = e(nc.sbuf_tensor("osb", [128, 4, 512], F32))
        ob = e(nc.sbuf_tensor("ob", [128, 512], BF16))
        oT = e(nc.sbuf_tensor("oT", [128, 4, 128], BF16))
        xq = [e(nc.sbuf_tensor(f"xq{i}", [128, D], F32)) for i in range(2)]
        fin = [e(nc.sbuf_tensor(f"fin{i}", [128, D], F32)) for i in range(2)]
        ps_qkv = e(nc.psum_tensor("ps_qkv", [128, 1536], F32))
        ps_s = e(nc.psum_tensor("ps_s", [128, 1024], F32))
        ps_tp = e(nc.psum_tensor("ps_tp", [128, 1024], F32))
        ps_o = e(nc.psum_tensor("ps_o", [128, 128], F32))

        def tpb(k):
            return ps_tp[:].bitcast(BF16)[:, k * 128:(k + 1) * 128]

        s_w = Sem(ctx, nc, "s_w")
        s_z = [Sem(ctx, nc, f"s_z{j}") for j in range(2)]
        s_tb = [Sem(ctx, nc, f"s_tb{j}") for j in range(2)]
        s_m = Sem(ctx, nc, "s_m")
        s_xq = [Sem(ctx, nc, f"s_xq{j}") for j in range(2)]
        s_a = Sem(ctx, nc, "s_a")
        s_v = Sem(ctx, nc, "s_v")
        s_tp = Sem(ctx, nc, "s_tp")
        s_ev = Sem(ctx, nc, "s_ev")
        s_mm = Sem(ctx, nc, "s_mm")
        s_st = [Sem(ctx, nc, f"s_st{j}") for j in range(2)]
        s_zc = Sem(ctx, nc, "s_zc")
        cst = Consts(ctx, nc, [1e-12])
        s_c = Sem(ctx, nc, "s_c")

        p = Prog()
        cst.emit(p, s_c)
        p.dma("sync", wq[:], wq_in.rearrange("p (a m) -> p a m", a=8), inc=s_w)
        p.dma("sync", wo[:], wo_in.rearrange("p (a m) -> p a m", a=4), inc=s_w)
        p.dma("sync", identf[:], ident_in, inc=s_w)
        p.dma("sync", qs_r[:], bass.AP(qs_in, 0, [[0, 128], [1, 64]]), inc=s_w)
        p.dma("sync", ks_r[:], bass.AP(ks_in, 0, [[0, 128], [1, 64]]), inc=s_w)
        p.run("scalar", lambda E: E.activation(out=identb[:], in_=identf[:], func=AF.Copy),
              deps=(s_w, s_c), inc=s_a)
        p.run("vector", [lambda E: E.tensor_single_scalar(qs_r[:], qs_r[:], 8.0, ALU.mult)],
              deps=(s_w,), inc=s_v)
        for eng in ("tensor", "vector", "scalar"):
            p.wait(eng, s_w)
            p.wait(eng, s_a)
            p.wait(eng, s_v)

        def rotary_norm(src_f, scale_rep, tabc_col, tabs_col, dstT, i, sl):
            p.run("scalar", lambda E: E.activation(out=sqs[:], in_=src_f[:], func=AF.Square),
                  deps=(s_v,), inc=s_a)
            p.run("vector", [lambda E: E.tensor_reduce(nrm[:], sqs[:], axis=AX.X, op=ALU.add)],
                  deps=(s_a,), inc=s_v)
            p.run("scalar", lambda E: E.activation(out=nrm[:], in_=nrm[:], func=AF.Sqrt),
                  deps=(s_v,), inc=s_a)
            p.run("vector", [lambda E: E.tensor_single_scalar(nrm[:], nrm[:], 1e-12, ALU.max)],
                  deps=(s_a,), inc=s_v)
            p.run("vector", [lambda E: E.reciprocal(nrm[:], nrm[:])], deps=(s_v,), inc=s_v)
            p.run("vector", [lambda E: E.tensor_tensor(
                src_f[:], src_f[:],
                bass.AP(nrm.tensor if hasattr(nrm, 'tensor') else nrm, 0,
                        [[8, 128], [1, 8], [0, 64]]), ALU.mult)], deps=(s_v,), inc=s_v)
            p.run("vector", [lambda E: E.tensor_tensor(
                src_f[:], src_f[:],
                bass.AP(scale_rep, 0, [[64, 128], [0, 8], [1, 64]]), ALU.mult)],
                deps=(s_v,), inc=s_v)
            p.run("vector", [
                lambda E: E.tensor_single_scalar(rt[:, :, 0:32], src_f[:, :, 32:64],
                                                 -1.0, ALU.mult),
                lambda E: E.tensor_copy(rt[:, :, 32:64], src_f[:, :, 0:32]),
            ], deps=(s_v,), inc=s_v)
            p.run("vector", [
                lambda E: E.tensor_tensor(
                    src_f[:], src_f[:],
                    bass.AP(tab[sl].tensor if hasattr(tab[sl], 'tensor') else tab[sl],
                            tabc_col * 64, [[256, 128], [0, 8], [1, 64]]), ALU.mult),
                lambda E: E.tensor_tensor(
                    rt[:], rt[:],
                    bass.AP(tab[sl].tensor if hasattr(tab[sl], 'tensor') else tab[sl],
                            tabs_col * 64, [[256, 128], [0, 8], [1, 64]]), ALU.mult),
            ], deps=(s_v,), inc=s_v)
            p.run("vector", [lambda E: E.tensor_add(src_f[:], src_f[:], rt[:])],
                  deps=(s_v,), inc=s_v)
            p.wait("vector", s_tp)
            p.run("vector", [lambda E: E.tensor_copy(zb[:, 0:512],
                                                     src_f[:].rearrange("p a b -> p (a b)"))],
                  deps=(s_v,), inc=s_v)
            p.wait("tensor", s_ev)
            tps = [(lambda E, c=c: E.transpose(tpb(c), zb[0:128, c * 128:(c + 1) * 128],
                                               identb[:])) for c in range(4)]
            p.run("tensor", tps, deps=(s_v,), inc=s_tp)
            p.run("vector", [(lambda E, c=c, i=i: E.tensor_copy(
                dstT[:, c, i * 128:(i + 1) * 128], tpb(c))) for c in range(4)],
                deps=(s_tp,), inc=s_ev)

        zc_vals = []
        for i in range(nt):
            j = i % 2
            if i >= 2:
                p.wait("sync", s_v, zc_vals[i - 2])
            p.dma("sync", zt[j][:], z2[1 + i * 128: 1 + (i + 1) * 128, :], inc=s_z[j])
            p.dma("sync", zs[j][:], z2[i * 128: (i + 1) * 128, :], inc=s_z[j])
            p.dma("sync", tab[j][:], tab_in[i * 128:(i + 1) * 128, :, :], inc=s_tb[j])
            p.wait("vector", s_z[j].h, 32 * (i // 2 + 1))
            p.wait("vector", s_tb[j].h, 16 * (i // 2 + 1))
            p.wait("vector", s_tp)
            p.run("vector", [
                (lambda E, j=j: E.tensor_copy(zb[:, 0:512], zt[j][:, 0:512])),
                (lambda E, j=j: E.tensor_copy(zsb[:, 512:1024], zs[j][:, 512:1024])),
            ], deps=(s_v,), inc=s_v)
            zc_vals.append(s_v.v)
            p.wait("vector", s_mm)
            srcs = [zb, zb, zb, zb, zsb, zsb, zsb, zsb]
            p.wait("tensor", s_ev)
            tps = [(lambda E, k=k: E.transpose(tpb(k),
                                               srcs[k][0:128, k * 128:(k + 1) * 128],
                                               identb[:])) for k in range(8)]
            p.run("tensor", tps, deps=(s_v,), inc=s_tp)
            p.run("vector", [(lambda E, k=k: E.tensor_copy(hT[:, k, :], tpb(k)))
                             for k in range(8)], deps=(s_tp,), inc=s_ev)
            p.wait("tensor", s_ev)
            p.wait("tensor", s_a)
            p.wait("tensor", s_v)
            mms = [(lambda E, k=k, n=n: E.matmul(
                ps_qkv[:, n * 512:(n + 1) * 512], hT[:, k, :],
                wq[:, k, n * 512:(n + 1) * 512], start=(k == 0), stop=(k == 7)))
                for n in range(3) for k in range(8)]
            p.run("tensor", mms, inc=s_mm)
            p.wait("vector", s_mm)
            p.run("vector", [lambda E: E.tensor_copy(qf[:].rearrange("p a b -> p (a b)"),
                                                     ps_qkv[:, 0:512])], deps=(s_v,), inc=s_v)
            p.run("vector", [lambda E: E.tensor_copy(kf[:].rearrange("p a b -> p (a b)"),
                                                     ps_qkv[:, 512:1024])], inc=s_v)
            p.wait("scalar", s_mm)
            p.run("scalar", [(lambda E, i=i: E.activation(out=vtok[:, i, :],
                                                          in_=ps_qkv[:, 1024:1536],
                                                          func=AF.Copy))], inc=s_a)
            rotary_norm(qf, qs_r.tensor if hasattr(qs_r, 'tensor') else qs_r, 0, 1, qT, i, j)
            rotary_norm(kf, ks_r.tensor if hasattr(ks_r, 'tensor') else ks_r, 2, 3, kT, i, j)
            zc_vals[i] = s_v.v

        p.wait("tensor", s_ev)
        p.wait("tensor", s_a)
        p.wait("tensor", s_v)
        QK_SCALE = 8.0
        for wl in range(NQB):
            p.dma("sync", mask[:].rearrange("p a m -> p (a m)") if False else mask[:],
                  mask_in[wl, :, :].rearrange("(a p) m -> p a m", p=128), inc=s_m)
            kt0 = (128 + 512 * wl) // 128
            for it in range(4):
                qtile = 5 + 4 * wl + it
                for h in range(8):
                    qpart = 64 * (h % 2)
                    p.wait("tensor", s_v)
                    mms = [(lambda E, half=half, h=h, qpart=qpart, qtile=qtile, kt0=kt0:
                            E.matmul(ps_s[:, half * 512:(half + 1) * 512],
                                     qT[qpart:qpart + 64, h // 2,
                                        qtile * 128:(qtile + 1) * 128],
                                     kT[qpart:qpart + 64, h // 2,
                                        kt0 * 128 + half * 512:kt0 * 128 + (half + 1) * 512],
                                     start=True, stop=True))
                           for half in range(2)]
                    p.run("tensor", mms, inc=s_mm)
                    p.wait("vector", s_mm)
                    p.wait("vector", s_m[0].h if False else s_m.h, 16 * (wl + 1))
                    p.run("vector", [(lambda E, it=it: E.tensor_tensor(
                        ps_s[:, 0:1024], ps_s[:, 0:1024], mask[:, it, :], ALU.add))],
                        inc=s_v)
                    p.run("vector", [lambda E: E.reduce_max(negmax[:], ps_s[:, 0:1024],
                                                            axis=AX.X, negate=True)],
                          deps=(s_v,), inc=s_v)
                    p.run("scalar", (lambda E: E.activation(
                        out=P[:], in_=ps_s[:, 0:1024], func=AF.Exp,
                        bias=negmax[:], accum_out=ssum[:])), deps=(s_v,), inc=s_a)
                    p.run("vector", [lambda E: E.reciprocal(rs[:], ssum[:])],
                          deps=(s_a,), inc=s_v)
                    p.wait("tensor", s_ev)
                    p.wait("tensor", s_a)
                    tps = [(lambda E, c=c: E.transpose(tpb(c),
                                                      P[0:128, c * 128:(c + 1) * 128],
                                                      identb[:])) for c in range(8)]
                    p.run("tensor", tps, inc=s_tp)
                    p.run("vector", [(lambda E, c=c: E.tensor_copy(PT[:, c, :], tpb(c)))
                                     for c in range(8)], deps=(s_tp,), inc=s_ev)
                    p.wait("tensor", s_ev)
                    mms = [(lambda E, c=c, h=h, kt0=kt0:
                            E.matmul(ps_o[:, 0:64], PT[:, c, :],
                                     vtok[:, kt0 + c, h * 64:(h + 1) * 64],
                                     start=(c == 0), stop=(c == 7)))
                           for c in range(8)]
                    p.run("tensor", mms, inc=s_mm)
                    p.wait("vector", s_mm)
                    p.run("vector", [(lambda E, it=it, h=h: E.tensor_scalar(
                        osb[:, it, h * 64:(h + 1) * 64], ps_o[:, 0:64], rs[:],
                        float(QK_SCALE ** 2) if False else None, ALU.mult))], inc=s_v)
            for it in range(4):
                qtile = 5 + 4 * wl + it
                j = it % 2
                row0 = wl * 512 + it * 128
                if wl * 4 + it >= 2:
                    p.wait("sync", s_st[j].h, 16 * ((wl * 4 + it) // 2))
                p.dma("sync", xq[j][:], z2[1 + qtile * 128:1 + (qtile + 1) * 128, :],
                      inc=s_xq[j])
                p.run("vector", [(lambda E, it=it: E.tensor_copy(
                    ob[:], osb[:, it, :]))], deps=(s_v,), inc=s_v)
                p.wait("tensor", s_ev)
                p.wait("tensor", s_v)
                tps = [(lambda E, c=c: E.transpose(tpb(c), ob[0:128, c * 128:(c + 1) * 128],
                                                   identb[:])) for c in range(4)]
                p.run("tensor", tps, inc=s_tp)
                p.run("vector", [(lambda E, c=c: E.tensor_copy(oT[:, c, :], tpb(c)))
                                 for c in range(4)], deps=(s_tp,), inc=s_ev)
                p.wait("tensor", s_ev)
                mms = [(lambda E, c=c, n=n: E.matmul(
                    ps_s[:, n * 512:(n + 1) * 512], oT[:, c, :],
                    wo[:, c, n * 512:(n + 1) * 512], start=(c == 0), stop=(c == 3)))
                    for n in range(2) for c in range(4)]
                p.run("tensor", mms, inc=s_mm)
                p.wait("vector", s_mm)
                p.wait("vector", s_xq[j].h, 16 * ((wl * 4 + it) // 2 + 1))
                p.run("vector", [(lambda E, j=j: E.tensor_add(
                    fin[j][:], ps_s[:, 0:1024], xq[j][:]))], inc=s_v)
                p.dma("sync", xo[row0:row0 + 128, :], fin[j][:], deps=(s_v,),
                      inc=s_st[j])
        for j in range(2):
            p.wait("sync", s_st[j].h, s_st[j].v)
        p.build(nc)
    return nc


def build_t1l_ffn(ntok=TPC):
    nt = ntok // 128
    nc = bass.Bass("TRN2", target_bir_lowering=False, debug=False)
    z2 = nc.dram_tensor("z2", [ntok + 1, D], F32, kind="ExternalInput").ap()
    w1_in = nc.dram_tensor("w1", [128, 8 * 44 * 128], BF16, kind="ExternalInput").ap()
    w2_in = nc.dram_tensor("w2", [128, 22 * 8 * 128], BF16, kind="ExternalInput").ap()
    b1_in = nc.dram_tensor("b1", [1, 5632], F32, kind="ExternalInput").ap()
    b2_in = nc.dram_tensor("b2", [128, 8], F32, kind="ExternalInput").ap()
    lng_in = nc.dram_tensor("lng", [128, 8], F32, kind="ExternalInput").ap()
    lnb_in = nc.dram_tensor("lnb", [128, 8], F32, kind="ExternalInput").ap()
    ident_in = nc.dram_tensor("identin", [128, 128], F32, kind="ExternalInput").ap()
    xo = nc.dram_tensor("xo", [ntok, D], F32, kind="ExternalOutput").ap()

    with ExitStack() as ctx:
        e = ctx.enter_context
        w1 = e(nc.sbuf_tensor("w1s", [128, 8, 44, 128], BF16))
        w2 = [e(nc.sbuf_tensor(f"w2s{i}", [128, 8, 128], BF16)) for i in range(2)]
        b2 = e(nc.sbuf_tensor("b2s", [128, 8], F32))
        lng = e(nc.sbuf_tensor("lngs", [128, 8], F32))
        lnb = e(nc.sbuf_tensor("lnbs", [128, 8], F32))
        lnb_b = e(nc.sbuf_tensor("lnbb", [128, 8], BF16))
        btot = e(nc.sbuf_tensor("btot", [1, 5632], F32))
        btot_b = e(nc.sbuf_tensor("btotb", [1, 5632], BF16))
        ones = e(nc.sbuf_tensor("ones", [1, 128], BF16))
        identf = e(nc.sbuf_tensor("identf", [128, 128], F32))
        identb = e(nc.sbuf_tensor("identb", [128, 128], BF16))
        zt = [e(nc.sbuf_tensor(f"zt{i}", [128, D], F32)) for i in range(2)]
        zs = [e(nc.sbuf_tensor(f"zs{i}", [128, D], F32)) for i in range(2)]
        sq = e(nc.sbuf_tensor("sq", [128, D], F32))
        sums = e(nc.sbuf_tensor("sums", [128, 4], F32))
        mu = e(nc.sbuf_tensor("mu", [128, 1], F32))
        rstd = e(nc.sbuf_tensor("rstd", [128, 1], F32))
        nh = e(nc.sbuf_tensor("nh", [128, D], F32))
        nhb = e(nc.sbuf_tensor("nhb", [128, D], BF16))
        nT = e(nc.sbuf_tensor("nT", [128, 8, 128], BF16))
        gl = e(nc.sbuf_tensor("gl", [128, 128], F32))
        hgT = e(nc.sbuf_tensor("hgT", [128, 22, 128], BF16))
        outT = e(nc.sbuf_tensor("outT", [128, 8, 128], F32))
        fin = [e(nc.sbuf_tensor(f"fin{i}", [128, D], F32)) for i in range(2)]
        ps_tp = e(nc.psum_tensor("ps_tp", [128, 1024], F32))
        ps_w1 = e(nc.psum_tensor("ps_w1", [128, 1024], F32))
        ps_w2 = e(nc.psum_tensor("ps_w2", [128, 1024], F32))
        ps_bs = e(nc.psum_tensor("ps_bs", [1, 512], F32))

        def tpb(k):
            return ps_tp[:].bitcast(BF16)[:, k * 128:(k + 1) * 128]

        s_w = Sem(ctx, nc, "s_w")
        s_z = [Sem(ctx, nc, f"s_z{j}") for j in range(2)]
        s_a = Sem(ctx, nc, "s_a")
        s_v = Sem(ctx, nc, "s_v")
        s_tp = Sem(ctx, nc, "s_tp")
        s_ev = Sem(ctx, nc, "s_ev")
        s_mm = Sem(ctx, nc, "s_mm")
        s_ep = Sem(ctx, nc, "s_ep")
        s_st = [Sem(ctx, nc, f"s_st{j}") for j in range(2)]
        s_w2 = [Sem(ctx, nc, f"s_w2{j}") for j in range(2)]
        cst = Consts(ctx, nc, [1e-5])
        s_c = Sem(ctx, nc, "s_c")

        p = Prog()
        cst.emit(p, s_c)
        p.run("gpsimd", [lambda E: E.memset(ones[:], 1.0)], inc=s_c)
        p.dma("sync", w1[:], w1_in.rearrange("p (a b m) -> p a b m", a=8, b=44), inc=s_w)
        p.dma("sync", b2[:], b2_in, inc=s_w)
        p.dma("sync", lng[:], lng_in, inc=s_w)
        p.dma("sync", lnb[:], lnb_in, inc=s_w)
        p.dma("sync", identf[:], ident_in, inc=s_w)
        p.dma("sync", btot[:], b1_in, inc=s_w)
        p.run("scalar", lambda E: E.activation(out=identb[:], in_=identf[:], func=AF.Copy),
              deps=(s_w, s_c), inc=s_a)
        for eng in ("tensor", "vector", "scalar"):
            p.wait(eng, s_w)
            p.wait(eng, s_a)
            p.wait(eng, s_c)
        p.run("vector", [lambda E: E.tensor_copy(lnb_b[:], lnb[:])], inc=s_v)
        p.wait("tensor", s_v)
        for n in range(11):
            mms = [(lambda E, k=k, n=n: E.matmul(
                ps_bs[:, 0:512], lnb_b[:, k:k + 1],
                w1[:, k, :, :].rearrange("p a m -> p (a m)")[:, n * 512:(n + 1) * 512],
                start=(k == 0), stop=(k == 7))) for k in range(8)]
            p.run("tensor", mms, deps=(s_ep,), inc=s_mm)
            p.run("vector", [(lambda E, n=n: E.tensor_add(
                btot[:, n * 512:(n + 1) * 512], btot[:, n * 512:(n + 1) * 512],
                ps_bs[:, 0:512]))], deps=(s_mm,), inc=s_ep)
        p.run("vector", [lambda E: E.tensor_copy(btot_b[:], btot[:])],
              deps=(s_ep,), inc=s_v)
        p.wait("tensor", s_v)

        w2mm_done = []
        for i in range(nt):
            j = i % 2
            if i >= 2:
                p.wait("sync", s_st[j].h, 16 * (i // 2))
            p.dma("sync", zt[j][:], z2[1 + i * 128: 1 + (i + 1) * 128, :], inc=s_z[j])
            p.dma("sync", zs[j][:], z2[i * 128: (i + 1) * 128, :], inc=s_z[j])
            p.wait("scalar", s_z[j].h, 32 * (i // 2 + 1))
            p.wait("vector", s_z[j].h, 32 * (i // 2 + 1))
            p.wait("scalar", s_v)
            p.run("scalar", lambda E, j=j: E.activation(out=sq[:, 0:512], in_=zt[j][:, 0:512],
                                                        func=AF.Square), deps=(s_ep,), inc=s_a)
            p.run("scalar", lambda E, j=j: E.activation(out=sq[:, 512:1024], in_=zs[j][:, 512:1024],
                                                        func=AF.Square), deps=(s_a,), inc=s_a)
            p.run("vector", [
                lambda E, j=j: E.reduce_sum(sums[:, 0:1], zt[j][:, 0:512], axis=AX.X),
                lambda E, j=j: E.reduce_sum(sums[:, 1:2], zs[j][:, 512:1024], axis=AX.X),
            ], deps=(s_v,), inc=s_v)
            p.run("vector", [lambda E: E.reduce_sum(sums[:, 2:3], sq[:], axis=AX.X)],
                  deps=(s_a,), inc=s_v)
            p.run("vector", [lambda E: E.tensor_add(sums[:, 0:1], sums[:, 0:1], sums[:, 1:2])],
                  deps=(s_v,), inc=s_v)
            p.run("vector", [lambda E: E.tensor_scalar(mu[:], sums[:, 0:1], 1.0 / D, None,
                                                       ALU.mult)], deps=(s_v,), inc=s_v)
            p.run("vector", [lambda E: E.tensor_tensor(sums[:, 1:2], mu[:], mu[:], ALU.mult)],
                  deps=(s_v,), inc=s_v)
            p.run("vector", [lambda E: E.tensor_scalar(sums[:, 3:4], sums[:, 2:3], 1.0 / D,
                                                       None, ALU.mult)], deps=(s_v,), inc=s_v)
            p.run("vector", [lambda E: E.tensor_sub(sums[:, 3:4], sums[:, 3:4], sums[:, 1:2])],
                  deps=(s_v,), inc=s_v)
            p.run("scalar", lambda E: E.activation(out=rstd[:], in_=sums[:, 3:4],
                                                   func=AF.Sqrt, bias=1e-5),
                  deps=(s_v,), inc=s_a)
            p.run("vector", [lambda E: E.reciprocal(rstd[:], rstd[:])], deps=(s_a,), inc=s_v)
            p.run("vector", [
                lambda E, j=j: E.tensor_scalar(nh[:, 0:512], zt[j][:, 0:512], mu[:], rstd[:],
                                               ALU.subtract, ALU.mult),
                lambda E, j=j: E.tensor_scalar(nh[:, 512:1024], zs[j][:, 512:1024], mu[:],
                                               rstd[:], ALU.subtract, ALU.mult),
            ], deps=(s_v,), inc=s_v)
            p.wait("vector", s_tp)
            p.run("vector", [lambda E: E.tensor_copy(nhb[:], nh[:])], deps=(s_v,), inc=s_v)
            p.wait("vector", s_mm)
            p.wait("tensor", s_ev)
            tps = [(lambda E, k=k: E.transpose(tpb(k), nhb[0:128, k * 128:(k + 1) * 128],
                                               identb[:])) for k in range(8)]
            p.run("tensor", tps, deps=(s_v,), inc=s_tp)
            p.run("vector", [(lambda E, k=k: E.tensor_scalar(nT[:, k, :], tpb(k),
                                                             lng[:, k:k + 1], None, ALU.mult))
                             for k in range(8)], deps=(s_tp,), inc=s_ev)
            p.wait("tensor", s_ev)
            ep0v = []
            for mp in range(22):
                bk = mp % 2
                aoff = bk * 512
                goff = bk * 512 + 128
                if mp >= 2:
                    p.wait("tensor", s_ep, ep0v[mp - 2])
                else:
                    p.wait("tensor", s_ep)
                mms = []
                for k in range(8):
                    mms.append(lambda E, k=k, mp=mp, aoff=aoff: E.matmul(
                        ps_w1[:, aoff:aoff + 128], w1[:, k, mp, :], nT[:, k, :],
                        start=(k == 0), stop=False))
                mms.append(lambda E, mp=mp, aoff=aoff: E.matmul(
                    ps_w1[:, aoff:aoff + 128], btot_b[:, mp * 128:(mp + 1) * 128],
                    ones[:], start=False, stop=True))
                for k in range(8):
                    mms.append(lambda E, k=k, mp=mp, goff=goff: E.matmul(
                        ps_w1[:, goff:goff + 128], w1[:, k, 22 + mp, :], nT[:, k, :],
                        start=(k == 0), stop=False))
                mms.append(lambda E, mp=mp, goff=goff: E.matmul(
                    ps_w1[:, goff:goff + 128], btot_b[:, (22 + mp) * 128:(23 + mp) * 128],
                    ones[:], start=False, stop=True))
                p.run("tensor", mms, inc=s_mm)
                p.wait("scalar", s_mm)
                p.run("scalar", (lambda E, goff=goff: E.activation(
                    out=gl[:], in_=ps_w1[:, goff:goff + 128], func=AF.Gelu)),
                    deps=(s_ep,), inc=s_ep)
                p.run("vector", [(lambda E, mp=mp, aoff=aoff: E.tensor_tensor(
                    hgT[:, mp, :], ps_w1[:, aoff:aoff + 128], gl[:], ALU.mult))],
                    deps=(s_ep,), inc=s_ep)
                ep0v.append(s_ep.v)
            p.wait("tensor", s_ep)
            for k in range(22):
                w2i = i * 22 + k
                sl2 = w2i % 2
                if w2i >= 2:
                    p.wait("gpsimd", s_mm, w2mm_done[w2i - 2])
                p.dma("gpsimd", w2[sl2][:],
                      w2_in.rearrange("p (a b m) -> p a b m", a=22, b=8)[:, k, :, :],
                      inc=s_w2[sl2])
                p.wait("tensor", s_w2[sl2].h, 16 * (w2i // 2 + 1))
                mms = [(lambda E, k=k, m2=m2, sl2=sl2: E.matmul(
                    ps_w2[:, m2 * 128:(m2 + 1) * 128], w2[sl2][:, m2, :], hgT[:, k, :],
                    start=(k == 0 and m2 % 4 == 0), stop=(k == 21 and m2 % 4 == 3)))
                    for m2 in range(8)]
                p.run("tensor", mms, inc=s_mm)
                w2mm_done.append(s_mm.v)
            for m2 in range(8):
                p.wait("scalar", s_mm)
                p.run("scalar", (lambda E, m2=m2: E.activation(
                    out=outT[:, m2, :], in_=ps_w2[:, m2 * 128:(m2 + 1) * 128],
                    func=AF.Identity, bias=b2[:, m2:m2 + 1])),
                    deps=(s_ep,) if m2 == 0 else (), inc=s_ep)
            for half in range(2):
                ks = range(half * 4, half * 4 + 4)
                p.wait("tensor", s_v)
                p.wait("tensor", s_ep)
                tps = [(lambda E, k=k, half=half: E.transpose(
                    ps_tp[:, half * 512 + (k % 4) * 128: half * 512 + (k % 4) * 128 + 128],
                    outT[:, k, :], identf[:])) for k in ks]
                p.run("tensor", tps, inc=s_tp)
                p.run("vector",
                      [(lambda E, k=k, half=half, j=j:
                        E.scalar_tensor_tensor(
                            fin[j][:, k * 128:(k + 1) * 128],
                            ps_tp[:, half * 512 + (k % 4) * 128:
                                  half * 512 + (k % 4) * 128 + 128], 1.0,
                            zt[j][:, k * 128:(k + 1) * 128], ALU.mult, ALU.add))
                       for k in ks], deps=(s_tp,), inc=s_v)
            p.dma("sync", xo[i * 128:(i + 1) * 128, :], fin[j][:], deps=(s_v,),
                  inc=s_st[j])
        for j in range(2):
            p.wait("sync", s_st[j].h, s_st[j].v)
        p.build(nc)
    return nc


def build_ln(ntok=TPC):
    nc = bass.Bass("TRN2", target_bir_lowering=False, debug=False)
    x_in = nc.dram_tensor("x", [ntok, D], F32, kind="ExternalInput").ap()
    xo = nc.dram_tensor("xo", [ntok, D], F32, kind="ExternalOutput").ap()
    nt = ntok // 128
    with ExitStack() as ctx:
        e = ctx.enter_context
        xt = [e(nc.sbuf_tensor(f"xt{i}", [128, D], F32)) for i in range(2)]
        out = [e(nc.sbuf_tensor(f"ot{i}", [128, D], F32)) for i in range(2)]
        sq = e(nc.sbuf_tensor("sq", [128, D], F32))
        sums = e(nc.sbuf_tensor("sums", [128, 4], F32))
        mu = e(nc.sbuf_tensor("mu", [128, 1], F32))
        rstd = e(nc.sbuf_tensor("rstd", [128, 1], F32))
        s_x = [Sem(ctx, nc, f"s_x{j}") for j in range(2)]
        s_a = Sem(ctx, nc, "s_a")
        s_v = Sem(ctx, nc, "s_v")
        s_st = [Sem(ctx, nc, f"s_st{j}") for j in range(2)]
        cst = Consts(ctx, nc, [1e-5])
        s_c = Sem(ctx, nc, "s_c")
        p = Prog()
        cst.emit(p, s_c)
        p.wait("scalar", s_c)
        p.wait("vector", s_c)
        for i in range(nt):
            j = i % 2
            if i >= 2:
                p.wait("sync", s_st[j].h, 16 * (i // 2))
            p.dma("sync", xt[j][:], x_in[i * 128:(i + 1) * 128, :], inc=s_x[j])
            p.wait("scalar", s_x[j].h, 16 * (i // 2 + 1))
            p.wait("vector", s_x[j].h, 16 * (i // 2 + 1))
            p.wait("scalar", s_v)
            emit_ln_tile(p, i, xt[j][:], None, sq, sums, mu, rstd, s_a, s_v,
                         out_f32=out[j][:])
            p.dma("sync", xo[i * 128:(i + 1) * 128, :], out[j][:], deps=(s_v,),
                  inc=s_st[j])
        for j in range(2):
            p.wait("sync", s_st[j].h, s_st[j].v)
        p.build(nc)
    return nc


def build_axpy(ntok=TPC, alpha=0.5):
    nc = bass.Bass("TRN2", target_bir_lowering=False, debug=False)
    a_in = nc.dram_tensor("a", [ntok, D], F32, kind="ExternalInput").ap()
    b_in = nc.dram_tensor("b", [ntok, D], F32, kind="ExternalInput").ap()
    xo = nc.dram_tensor("xo", [ntok, D], F32, kind="ExternalOutput").ap()
    nt = ntok // 128
    with ExitStack() as ctx:
        e = ctx.enter_context
        at = [e(nc.sbuf_tensor(f"at{i}", [128, D], F32)) for i in range(2)]
        bt = [e(nc.sbuf_tensor(f"bt{i}", [128, D], F32)) for i in range(2)]
        ot = [e(nc.sbuf_tensor(f"ot{i}", [128, D], F32)) for i in range(2)]
        s_x = [Sem(ctx, nc, f"s_x{j}") for j in range(2)]
        s_v = Sem(ctx, nc, "s_v")
        s_st = [Sem(ctx, nc, f"s_st{j}") for j in range(2)]
        p = Prog()
        for i in range(nt):
            j = i % 2
            if i >= 2:
                p.wait("sync", s_st[j].h, 16 * (i // 2))
            p.dma("sync", at[j][:], a_in[i * 128:(i + 1) * 128, :], inc=s_x[j])
            p.dma("sync", bt[j][:], b_in[i * 128:(i + 1) * 128, :], inc=s_x[j])
            p.wait("vector", s_x[j].h, 32 * (i // 2 + 1))
            p.run("vector", [(lambda E, j=j: E.scalar_tensor_tensor(
                ot[j][:], bt[j][:], float(alpha), at[j][:], ALU.mult, ALU.add))],
                deps=(s_v,), inc=s_v)
            p.dma("sync", xo[i * 128:(i + 1) * 128, :], ot[j][:], deps=(s_v,),
                  inc=s_st[j])
        for j in range(2):
            p.wait("sync", s_st[j].h, s_st[j].v)
        p.build(nc)
    return nc



def t1l_tables():
    n, DH, W = 8192, 64, 512
    half = np.arange(0, DH, 2, dtype=np.float32)
    inv_freq = 1.0 / 10000.0 ** (half / DH)
    t = np.arange(n, dtype=np.float32)
    freqs = t[:, None] * inv_freq[None, :]
    freqs = np.concatenate([freqs, freqs], axis=-1)
    base = (half + 0.4 * DH) / (1.4 * DH)
    power = (t - n // 2) / float(W // 2)
    scale = base[None, :] ** power[:, None]
    scale = np.concatenate([scale, scale], axis=-1)
    cos, sin = np.cos(freqs), np.sin(freqs)
    tabs = np.stack([cos * scale, sin * scale, cos / scale, sin / scale], axis=1)
    return np.ascontiguousarray(tabs.astype(np.float32))


def t1l_masks():
    i = np.arange(512)[:, None]
    jj = np.arange(1024)[None, :]
    j_abs = jj - 512
    validB = (i >= j_abs) & (i <= j_abs + 512)
    maskB = np.where(validB, 0.0, -1e30).astype(np.float32)
    validA = validB & (jj >= 512)
    maskA = np.where(validA, 0.0, -1e30).astype(np.float32)
    return maskA, maskB


def t1l_weights_host(inp, l):
    wqkv = inp["L1_ffn_wqkv"][l]
    wo = inp["L1_ffn_wo"][l]
    w1 = inp["L1_ffn_w1"][l]
    b1 = inp["L1_ffn_b1"][l]
    w2 = inp["L1_ffn_w2"][l]
    w1p = np.zeros((1024, 5632), np.float32)
    w1p[:, 0:2730] = w1[:, 0:2730]
    w1p[:, 2816:2816 + 2730] = w1[:, 2730:5460]
    b1p = np.zeros((1, 5632), np.float32)
    b1p[0, 0:2730] = b1[0:2730]
    b1p[0, 2816:2816 + 2730] = b1[2730:5460]
    w2p = np.zeros((2816, 1024), np.float32)
    w2p[0:2730] = w2
    return {
        "wqkv": np.ascontiguousarray(wqkv.reshape(8, 128, 1536).transpose(1, 0, 2)
                                     .reshape(128, 8 * 1536)),
        "wo": np.ascontiguousarray(wo.reshape(4, 128, 1024).transpose(1, 0, 2)
                                   .reshape(128, 4 * 1024)),
        "w1": np.ascontiguousarray(w1p.reshape(8, 128, 44, 128).transpose(1, 0, 2, 3)
                                   .reshape(128, 8 * 44 * 128)),
        "w2": np.ascontiguousarray(w2p.reshape(22, 128, 8, 128).transpose(1, 0, 2, 3)
                                   .reshape(128, 22 * 8 * 128)),
        "b1p": b1p,
        "b2": host_colvec(np.asarray(inp["L1_ffn_b2"][l], np.float32)),
        "lng": host_colvec(np.asarray(inp["L1_ffn_ln_g"][l], np.float32)),
        "lnb": host_colvec(np.asarray(inp["L1_ffn_ln_b"][l], np.float32)),
        "qs": np.ascontiguousarray(inp["L1_ffn_q_scale"][l], np.float32),
        "ks": np.ascontiguousarray(inp["L1_ffn_k_scale"][l], np.float32),
    }


def _z2_slices(z, halo):
    out = []
    for c in range(NCORES):
        b = c // 4
        t0 = c * TPC
        lo = t0 - halo - 1
        blo = b * 8192
        sl = np.zeros((TPC + halo + 1, D), np.float32)
        src_lo = max(lo, blo)
        sl[src_lo - lo:] = z[src_lo:t0 + TPC]
        out.append(sl)
    return out


def run_t1l(z, Wb, launch=_launch):
    ident = _ident()
    tabs = t1l_tables()
    maskA, maskB = t1l_masks()
    nc_a = _cached(("t1l_attn",), lambda: build_t1l_attn())
    ins = []
    z2s = _z2_slices(z, HALO)
    for c in range(NCORES):
        t0 = c * TPC
        blo = (c // 4) * 8192
        pos = np.clip(np.arange(t0 - HALO, t0 + TPC) - blo, 0, 8191)
        m4 = np.stack([maskA if (t0 + 512 * wl) % 8192 == 0 else maskB
                       for wl in range(4)])
        ins.append({"z2": z2s[c], "wqkv": Wb["wqkv"], "wo": Wb["wo"],
                    "tabs": np.ascontiguousarray(tabs[pos]),
                    "qs": Wb["qs"], "ks": Wb["ks"],
                    "mask4": m4, "identin": ident})
    res = launch(nc_a, ins)
    zmid = np.concatenate([r["xo"] for r in res], axis=0)

    nc_f = _cached(("t1l_ffn",), lambda: build_t1l_ffn())
    z2s = _z2_slices(zmid, 0)
    ins = [{"z2": z2s[c], "w1": Wb["w1"], "w2": Wb["w2"], "b1": Wb["b1p"],
            "b2": Wb["b2"], "lng": Wb["lng"], "lnb": Wb["lnb"],
            "identin": ident} for c in range(NCORES)]
    res = launch(nc_f, ins)
    return np.concatenate([r["xo"] for r in res], axis=0)


def run_l1ffn_macaron(x, Wb0, Wb1, launch=_launch):
    nc_ln = _cached(("ln",), lambda: build_ln())
    res = launch(nc_ln, [{"x": s} for s in _shard_tok(x)])
    z = np.concatenate([r["xo"] for r in res], axis=0)
    z = run_t1l(z, Wb0, launch)
    z = run_t1l(z, Wb1, launch)
    nc_ax = _cached(("axpy",), lambda: build_axpy())
    res = launch(nc_ax, [{"a": a, "b": b} for a, b in
                         zip(_shard_tok(x), _shard_tok(z))])
    return np.concatenate([r["xo"] for r in res], axis=0)



def build_l1a_pre(ntok=TPC):
    nt = ntok // 128
    nc = bass.Bass("TRN2", target_bir_lowering=False, debug=False)
    z_in = nc.dram_tensor("z", [ntok, D], F32, kind="ExternalInput").ap()
    wq_in = nc.dram_tensor("wq", [128, 8 * 1024], BF16, kind="ExternalInput").ap()
    wk_in = nc.dram_tensor("wk", [128, 8 * 1024], BF16, kind="ExternalInput").ap()
    wv_in = nc.dram_tensor("wv", [128, 8 * 1024], BF16, kind="ExternalInput").ap()
    tab_in = nc.dram_tensor("tabs", [ntok, 2, 32], F32, kind="ExternalInput").ap()
    ident_in = nc.dram_tensor("identin", [128, 128], F32, kind="ExternalInput").ap()
    qT_out = nc.dram_tensor("qTo", [128, 8, ntok], BF16, kind="ExternalOutput").ap()
    kT_out = nc.dram_tensor("kTo", [128, 8, ntok], BF16, kind="ExternalOutput").ap()
    v_out = nc.dram_tensor("vo", [ntok, D], BF16, kind="ExternalOutput").ap()

    with ExitStack() as ctx:
        e = ctx.enter_context
        wq = e(nc.sbuf_tensor("wqs", [128, 8, 1024], BF16))
        wk = e(nc.sbuf_tensor("wks", [128, 8, 1024], BF16))
        wv = e(nc.sbuf_tensor("wvs", [128, 8, 1024], BF16))
        identf = e(nc.sbuf_tensor("identf", [128, 128], F32))
        identb = e(nc.sbuf_tensor("identb", [128, 128], BF16))
        zt = [e(nc.sbuf_tensor(f"zt{i}", [128, D], F32)) for i in range(2)]
        zb = e(nc.sbuf_tensor("zb", [128, D], BF16))
        hT = e(nc.sbuf_tensor("hTt", [128, 8, 128], BF16))
        qf = e(nc.sbuf_tensor("qf", [128, 16, 64], F32))
        kf = e(nc.sbuf_tensor("kf", [128, 16, 64], F32))
        sqs = e(nc.sbuf_tensor("sqs", [128, 16, 64], F32))
        nrm = e(nc.sbuf_tensor("nrm", [128, 16], F32))
        rot = e(nc.sbuf_tensor("rot", [128, 16, 64], F32))
        tab = [e(nc.sbuf_tensor(f"tab{i}", [128, 2, 32], F32)) for i in range(2)]
        qb = e(nc.sbuf_tensor("qb", [128, D], BF16))
        vb = [e(nc.sbuf_tensor(f"vb{i}", [128, D], BF16)) for i in range(2)]
        qstg = [e(nc.sbuf_tensor(f"qstg{i}", [128, 8, 128], BF16)) for i in range(2)]
        kstg = [e(nc.sbuf_tensor(f"kstg{i}", [128, 8, 128], BF16)) for i in range(2)]
        ps_q = e(nc.psum_tensor("ps_q", [128, 1024], F32))
        ps_k = e(nc.psum_tensor("ps_k", [128, 1024], F32))
        ps_v = e(nc.psum_tensor("ps_v", [128, 1024], F32))
        ps_tp = e(nc.psum_tensor("ps_tp", [128, 1024], F32))

        def tpb(k):
            return ps_tp[:].bitcast(BF16)[:, k * 128:(k + 1) * 128]

        s_w = Sem(ctx, nc, "s_w")
        s_z = [Sem(ctx, nc, f"s_z{j}") for j in range(2)]
        s_tb = [Sem(ctx, nc, f"s_tb{j}") for j in range(2)]
        s_a = Sem(ctx, nc, "s_a")
        s_v = Sem(ctx, nc, "s_v")
        s_tp = Sem(ctx, nc, "s_tp")
        s_ev = Sem(ctx, nc, "s_ev")
        s_mm = Sem(ctx, nc, "s_mm")
        s_stv = [Sem(ctx, nc, f"s_stv{j}") for j in range(2)]
        s_stq = [Sem(ctx, nc, f"s_stq{j}") for j in range(2)]
        s_stk = [Sem(ctx, nc, f"s_stk{j}") for j in range(2)]
        cst = Consts(ctx, nc, [float(np.finfo(np.float32).eps)])
        s_c = Sem(ctx, nc, "s_c")

        p = Prog()
        cst.emit(p, s_c)
        p.dma("sync", wq[:], wq_in.rearrange("p (a m) -> p a m", a=8), inc=s_w)
        p.dma("sync", wk[:], wk_in.rearrange("p (a m) -> p a m", a=8), inc=s_w)
        p.dma("sync", wv[:], wv_in.rearrange("p (a m) -> p a m", a=8), inc=s_w)
        p.dma("sync", identf[:], ident_in, inc=s_w)
        p.run("scalar", lambda E: E.activation(out=identb[:], in_=identf[:], func=AF.Copy),
              deps=(s_w, s_c), inc=s_a)
        for eng in ("tensor", "vector", "scalar"):
            p.wait(eng, s_w)
            p.wait(eng, s_a)
            p.wait(eng, s_c)

        def rms_rot(src, dstT, i, sl):
            p.run("scalar", lambda E, src=src: E.activation(out=sqs[:], in_=src[:],
                                                            func=AF.Square),
                  deps=(s_v,), inc=s_a)
            p.run("vector", [lambda E: E.tensor_reduce(nrm[:], sqs[:], axis=AX.X,
                                                       op=ALU.add)],
                  deps=(s_a,), inc=s_v)
            p.run("scalar", lambda E: E.activation(
                out=nrm[:], in_=nrm[:], func=AF.Sqrt, scale=float(1.0 / 64.0),
                bias=float(np.finfo(np.float32).eps)), deps=(s_v,), inc=s_a)
            p.run("vector", [lambda E: E.reciprocal(nrm[:], nrm[:])],
                  deps=(s_a,), inc=s_v)
            p.run("vector", [lambda E, src=src: E.tensor_tensor(
                src[:], src[:],
                bass.AP(nrm.tensor if hasattr(nrm, 'tensor') else nrm, 0,
                        [[16, 128], [1, 16], [0, 64]]), ALU.mult)],
                deps=(s_v,), inc=s_v)
            tabh = tab[sl].tensor if hasattr(tab[sl], 'tensor') else tab[sl]
            cAP = bass.AP(tabh, 0, [[64, 128], [0, 16], [1, 32]])
            sAP = bass.AP(tabh, 32, [[64, 128], [0, 16], [1, 32]])
            p.run("vector", [
                (lambda E, src=src, cAP=cAP: E.tensor_tensor(
                    rot[:, :, 0:32], src[:, :, 0:32], cAP, ALU.mult)),
                (lambda E, src=src, cAP=cAP: E.tensor_tensor(
                    rot[:, :, 32:64], src[:, :, 32:64], cAP, ALU.mult)),
            ], deps=(s_v,), inc=s_v)
            p.run("vector", [
                (lambda E, src=src, sAP=sAP: E.tensor_tensor(
                    src[:, :, 32:64], src[:, :, 32:64], sAP, ALU.mult)),
                (lambda E, src=src, sAP=sAP: E.tensor_tensor(
                    src[:, :, 0:32], src[:, :, 0:32], sAP, ALU.mult)),
            ], deps=(s_v,), inc=s_v)
            p.run("vector", [
                (lambda E, src=src: E.tensor_add(rot[:, :, 0:32], rot[:, :, 0:32],
                                                 src[:, :, 32:64])),
                (lambda E, src=src: E.tensor_sub(rot[:, :, 32:64], rot[:, :, 32:64],
                                                 src[:, :, 0:32])),
            ], deps=(s_v,), inc=s_v)
            p.wait("vector", s_tp)
            p.run("vector", [lambda E: E.tensor_copy(qb[:],
                                                     rot[:].rearrange("p a b -> p (a b)"))],
                  deps=(s_v,), inc=s_v)
            p.wait("tensor", s_ev)
            tps = [(lambda E, c=c: E.transpose(tpb(c), qb[0:128, c * 128:(c + 1) * 128],
                                               identb[:])) for c in range(8)]
            p.run("tensor", tps, deps=(s_v,), inc=s_tp)
            p.run("vector", [(lambda E, c=c, dstT=dstT: E.tensor_copy(
                dstT[:, c, :], tpb(c)))
                for c in range(8)], deps=(s_tp,), inc=s_ev)

        zc_vals = []
        for i in range(nt):
            j = i % 2
            if i >= 2:
                p.wait("sync", s_v, zc_vals[i - 2])
            p.dma("sync", zt[j][:], z_in[i * 128:(i + 1) * 128, :], inc=s_z[j])
            p.dma("sync", tab[j][:], tab_in[i * 128:(i + 1) * 128, :, :], inc=s_tb[j])
            p.wait("vector", s_z[j].h, 16 * (i // 2 + 1))
            p.wait("vector", s_tb[j].h, 16 * (i // 2 + 1))
            p.wait("vector", s_tp)
            p.run("vector", [(lambda E, j=j: E.tensor_copy(zb[:], zt[j][:]))],
                  deps=(s_v,), inc=s_v)
            p.wait("vector", s_mm)
            p.wait("tensor", s_ev)
            tps = [(lambda E, k=k: E.transpose(tpb(k), zb[0:128, k * 128:(k + 1) * 128],
                                               identb[:])) for k in range(8)]
            p.run("tensor", tps, deps=(s_v,), inc=s_tp)
            p.run("vector", [(lambda E, k=k: E.tensor_copy(hT[:, k, :], tpb(k)))
                             for k in range(8)], deps=(s_tp,), inc=s_ev)
            p.wait("tensor", s_ev)
            p.wait("tensor", s_v)
            p.wait("tensor", s_a)
            mms = []
            for (ps, w) in ((ps_q, wq), (ps_k, wk), (ps_v, wv)):
                for n in range(2):
                    for k in range(8):
                        mms.append(lambda E, ps=ps, w=w, n=n, k=k: E.matmul(
                            ps[:, n * 512:(n + 1) * 512], hT[:, k, :],
                            w[:, k, n * 512:(n + 1) * 512],
                            start=(k == 0), stop=(k == 7)))
            p.run("tensor", mms, inc=s_mm)
            p.wait("vector", s_mm)
            p.run("vector", [lambda E: E.tensor_copy(qf[:].rearrange("p a b -> p (a b)"),
                                                     ps_q[:, 0:1024])], deps=(s_v,), inc=s_v)
            p.run("vector", [lambda E: E.tensor_copy(kf[:].rearrange("p a b -> p (a b)"),
                                                     ps_k[:, 0:1024])], inc=s_v)
            p.wait("scalar", s_mm)
            if i >= 2:
                p.wait("scalar", s_stv[j].h, 16 * (i // 2))
            p.run("scalar", [(lambda E, j=j: E.activation(out=vb[j][:], in_=ps_v[:, 0:1024],
                                                          func=AF.Copy))], inc=s_a)
            p.dma("sync", v_out[i * 128:(i + 1) * 128, :], vb[j][:], deps=(s_a,),
                  inc=s_stv[j])
            if i >= 2:
                p.wait("vector", s_stq[j].h, 16 * (i // 2))
                p.wait("vector", s_stk[j].h, 16 * (i // 2))
            rms_rot(qf, qstg[j], i, j)
            p.dma("sync", qT_out[:, :, i * 128:(i + 1) * 128], qstg[j][:],
                  deps=(s_ev,), inc=s_stq[j])
            rms_rot(kf, kstg[j], i, j)
            p.dma("sync", kT_out[:, :, i * 128:(i + 1) * 128], kstg[j][:],
                  deps=(s_ev,), inc=s_stk[j])
            zc_vals.append(s_v.v)
        for j in range(2):
            p.wait("sync", s_stq[j].h, s_stq[j].v)
            p.wait("sync", s_stk[j].h, s_stk[j].v)
            p.wait("sync", s_stv[j].h, s_stv[j].v)
        p.build(nc)
    return nc


def build_l1a_flash(T=8192, NU=4):
    nc = bass.Bass("TRN2", target_bir_lowering=False, debug=False)
    qT4 = nc.dram_tensor("qT4", [NU, 64, T], BF16, kind="ExternalInput").ap()
    kT4 = nc.dram_tensor("kT4", [NU, 64, T], BF16, kind="ExternalInput").ap()
    v4 = nc.dram_tensor("v4", [NU, T, 128], BF16, kind="ExternalInput").ap()
    ident_in = nc.dram_tensor("identin", [128, 128], F32, kind="ExternalInput").ap()
    y4 = nc.dram_tensor("y4", [NU, T, 128], F32, kind="ExternalOutput").ap()

    NQT = T // 512
    with ExitStack() as ctx:
        e = ctx.enter_context
        qsb = e(nc.sbuf_tensor("qsb", [64, T], BF16))
        ksb = e(nc.sbuf_tensor("ksb", [64, T], BF16))
        vsb = e(nc.sbuf_tensor("vsb", [128, T // 128, 128], BF16))
        onesb = e(nc.sbuf_tensor("onesb", [128, 1], BF16))
        identf = e(nc.sbuf_tensor("identf", [128, 128], F32))
        P = [e(nc.sbuf_tensor(f"P{i}", [128, 512], BF16)) for i in range(2)]
        acc_sb = e(nc.sbuf_tensor("acc_sb", [128, 512], F32))
        srow = e(nc.sbuf_tensor("srow", [1, 512], F32))
        scol = e(nc.sbuf_tensor("scol", [128, 4], F32))
        ystg = [e(nc.sbuf_tensor(f"ystg{i}", [128, 4, 128], F32)) for i in range(2)]
        ps_sc = e(nc.psum_tensor("ps_sc", [128, 1024], F32))
        ps_acc = e(nc.psum_tensor("ps_acc", [128, 512], F32))
        ps_sum = e(nc.psum_tensor("ps_sum", [1, 512], F32))
        ps_tp = e(nc.psum_tensor("ps_tp", [128, 1024], F32))

        s_w = Sem(ctx, nc, "s_w")
        s_u = Sem(ctx, nc, "s_u")
        s_a = Sem(ctx, nc, "s_a")
        s_v = Sem(ctx, nc, "s_v")
        s_g = Sem(ctx, nc, "s_g")
        s_tp = Sem(ctx, nc, "s_tp")
        s_mm = Sem(ctx, nc, "s_mm")
        s_st = [Sem(ctx, nc, f"s_st{j}") for j in range(2)]
        cst = Consts(ctx, nc, [0.125, -8.0])
        s_c = Sem(ctx, nc, "s_c")

        p = Prog()
        cst.emit(p, s_c)
        p.run("gpsimd", [lambda E: E.memset(onesb[:], 1.0)], inc=s_c)
        p.dma("sync", identf[:], ident_in, inc=s_w)
        for eng in ("tensor", "vector", "scalar", "gpsimd"):
            p.wait(eng, s_w)
            p.wait(eng, s_c)

        for u in range(NU):
            if u > 0:
                p.wait("sync", s_v)
            p.dma("sync", qsb[:], qT4[u], inc=s_u)
            p.dma("sync", ksb[:], kT4[u], inc=s_u)
            p.dma("sync", vsb[:], v4[u].rearrange("(a p) m -> p a m", p=128), inc=s_u)
            for eng in ("tensor", "vector", "scalar"):
                p.wait(eng, s_u.h, s_u.v)
            exp_done = {}
            vmm_done = {}
            for qt in range(NQT):
                njb = 4 * qt + 4
                for jb in range(njb):
                    reg = jb % 2
                    if (qt, jb - 2) in exp_done:
                        p.wait("tensor", s_a, exp_done[(qt, jb - 2)])
                    p.run("tensor", (lambda E, qt=qt, jb=jb, reg=reg: E.matmul(
                        ps_sc[:, reg * 512:(reg + 1) * 512],
                        ksb[:, jb * 128:(jb + 1) * 128],
                        qsb[:, qt * 512:(qt + 1) * 512],
                        start=True, stop=True)), inc=s_mm)
                    p.wait("scalar", s_mm)
                    if (qt, jb - 2) in vmm_done:
                        p.wait("scalar", s_mm.h, vmm_done[(qt, jb - 2)])
                    p.run("scalar", (lambda E, reg=reg: E.activation(
                        out=P[reg][:], in_=ps_sc[:, reg * 512:(reg + 1) * 512],
                        func=AF.Exp, scale=0.125, bias=-8.0)), inc=s_a)
                    exp_done[(qt, jb)] = s_a.v
                    diag = jb >= 4 * qt
                    if diag:
                        p.wait("gpsimd", s_a)
                        p.run("gpsimd", (lambda E, qt=qt, jb=jb, reg=reg:
                                         E.affine_select(
                                             P[reg][:], P[reg][:], [[1, 512]],
                                             ALU.is_ge, 0.0,
                                             base=qt * 512 - jb * 128,
                                             channel_multiplier=-1)), inc=s_g)
                        p.wait("tensor", s_g)
                    else:
                        p.wait("tensor", s_a)
                    mms = [
                        (lambda E, qt=qt, jb=jb, reg=reg: E.matmul(
                            ps_acc[:, 0:512], vsb[:, jb, :], P[reg][:],
                            start=(jb == 0), stop=(jb == njb - 1))),
                        (lambda E, qt=qt, jb=jb, reg=reg: E.matmul(
                            ps_sum[:, 0:512], onesb[:], P[reg][:],
                            start=(jb == 0), stop=(jb == njb - 1))),
                    ]
                    p.run("tensor", mms, inc=s_mm)
                    vmm_done[(qt, jb)] = s_mm.v
                jy = qt % 2
                p.wait("vector", s_mm)
                p.run("vector", [lambda E: E.tensor_copy(acc_sb[:], ps_acc[:, 0:512])],
                      deps=(s_v,), inc=s_v)
                p.run("vector", [lambda E: E.tensor_copy(srow[:], ps_sum[:, 0:512])],
                      inc=s_v)
                p.wait("tensor", s_v)
                tps = [(lambda E, c=c: E.transpose(
                    ps_tp[:, 512 + c * 128:512 + c * 128 + 1],
                    srow[0:1, c * 128:(c + 1) * 128], identf[0:1, 0:1]))
                    for c in range(4)]
                p.run("tensor", tps, inc=s_tp)
                p.run("vector", [(lambda E, c=c: E.tensor_copy(
                    scol[:, c:c + 1], ps_tp[:, 512 + c * 128:512 + c * 128 + 1]))
                    for c in range(4)], deps=(s_tp,), inc=s_v)
                p.run("vector", [lambda E: E.reciprocal(scol[:], scol[:])],
                      deps=(s_v,), inc=s_v)
                if qt >= 2:
                    p.wait("vector", s_st[jy].h, 16 * (qt // 2))
                p.wait("tensor", s_v)
                tps = [(lambda E, c=c: E.transpose(
                    ps_tp[:, c * 128:(c + 1) * 128],
                    acc_sb[0:128, c * 128:(c + 1) * 128], identf[:]))
                    for c in range(4)]
                p.run("tensor", tps, inc=s_tp)
                p.run("vector", [(lambda E, c=c, jy=jy: E.tensor_scalar(
                    ystg[jy][:, c, :], ps_tp[:, c * 128:(c + 1) * 128],
                    scol[:, c:c + 1], None, ALU.mult)) for c in range(4)],
                    deps=(s_tp,), inc=s_v)
                p.dma("sync", y4[u][qt * 512:(qt + 1) * 512, :]
                      .rearrange("(a p) m -> p a m", p=128), ystg[jy][:],
                      deps=(s_v,), inc=s_st[jy])
        for j in range(2):
            p.wait("sync", s_st[j].h, s_st[j].v)
        p.build(nc)
    return nc


def build_l1a_post(ntok=TPC):
    nt = ntok // 128
    nc = bass.Bass("TRN2", target_bir_lowering=False, debug=False)
    x_in = nc.dram_tensor("x", [ntok, D], F32, kind="ExternalInput").ap()
    y1_in = nc.dram_tensor("y1", [ntok, D], F32, kind="ExternalInput").ap()
    y2_in = nc.dram_tensor("y2", [ntok, D], F32, kind="ExternalInput").ap()
    wp_in = nc.dram_tensor("wp", [128, 8192], BF16, kind="ExternalInput").ap()
    lv_in = nc.dram_tensor("lv", [1, 128], F32, kind="ExternalInput").ap()
    ident_in = nc.dram_tensor("identin", [128, 128], F32, kind="ExternalInput").ap()
    xo = nc.dram_tensor("xo", [ntok, D], F32, kind="ExternalOutput").ap()
    LAMBDA_INIT = float(0.8 - 0.6 * np.exp(-0.3))

    with ExitStack() as ctx:
        e = ctx.enter_context
        wp = e(nc.sbuf_tensor("wps", [128, 8, 8, 128], BF16))
        identf = e(nc.sbuf_tensor("identf", [128, 128], F32))
        identb = e(nc.sbuf_tensor("identb", [128, 128], BF16))
        lv = e(nc.sbuf_tensor("lvs", [1, 128], F32))
        lt = e(nc.sbuf_tensor("lts", [1, 4], F32))
        lneg = e(nc.sbuf_tensor("lneg", [1, 1], F32))
        onesb = e(nc.sbuf_tensor("onesb", [1, 128], F32))
        lcol = e(nc.sbuf_tensor("lcol", [128, 1], F32))
        xt = [e(nc.sbuf_tensor(f"xt{i}", [128, D], F32)) for i in range(2)]
        y1t = [e(nc.sbuf_tensor(f"y1t{i}", [128, D], F32)) for i in range(2)]
        y2t = [e(nc.sbuf_tensor(f"y2t{i}", [128, D], F32)) for i in range(2)]
        yd = e(nc.sbuf_tensor("yd", [128, D], F32))
        ydb = e(nc.sbuf_tensor("ydb", [128, D], BF16))
        ydT = e(nc.sbuf_tensor("ydT", [128, 8, 128], BF16))
        fin = [e(nc.sbuf_tensor(f"fin{i}", [128, D], F32)) for i in range(2)]
        ps_tp = e(nc.psum_tensor("ps_tp", [128, 1024], F32))
        ps_mm = e(nc.psum_tensor("ps_mm", [128, 1024], F32))
        ps_l = e(nc.psum_tensor("ps_l", [128, 1], F32))

        def tpb(k):
            return ps_tp[:].bitcast(BF16)[:, k * 128:(k + 1) * 128]

        s_w = Sem(ctx, nc, "s_w")
        s_x = [Sem(ctx, nc, f"s_x{j}") for j in range(2)]
        s_a = Sem(ctx, nc, "s_a")
        s_v = Sem(ctx, nc, "s_v")
        s_tp = Sem(ctx, nc, "s_tp")
        s_ev = Sem(ctx, nc, "s_ev")
        s_mm = Sem(ctx, nc, "s_mm")
        s_ep = Sem(ctx, nc, "s_ep")
        s_st = [Sem(ctx, nc, f"s_st{j}") for j in range(2)]
        cst = Consts(ctx, nc, [LAMBDA_INIT])
        s_c = Sem(ctx, nc, "s_c")

        p = Prog()
        cst.emit(p, s_c)
        p.run("gpsimd", [lambda E: E.memset(onesb[:], 1.0)], inc=s_c)
        p.dma("sync", wp[:], wp_in.rearrange("p (a b m) -> p a b m", a=8, b=8), inc=s_w)
        p.dma("sync", identf[:], ident_in, inc=s_w)
        p.dma("sync", lv[:], lv_in, inc=s_w)
        p.run("scalar", lambda E: E.activation(out=identb[:], in_=identf[:], func=AF.Copy),
              deps=(s_w, s_c), inc=s_a)
        p.run("vector", [
            lambda E: E.tensor_mul(lv[:, 0:32], lv[:, 0:32], lv[:, 32:64]),
            lambda E: E.tensor_mul(lv[:, 64:96], lv[:, 64:96], lv[:, 96:128]),
        ], deps=(s_w,), inc=s_v)
        p.run("vector", [
            lambda E: E.reduce_sum(lt[:, 0:1], lv[:, 0:32], axis=AX.X),
            lambda E: E.reduce_sum(lt[:, 1:2], lv[:, 64:96], axis=AX.X),
        ], deps=(s_v,), inc=s_v)
        p.run("scalar", lambda E: E.activation(out=lt[:, 2:3], in_=lt[:, 0:1],
                                               func=AF.Exp), deps=(s_v,), inc=s_a)
        p.run("scalar", lambda E: E.activation(out=lt[:, 3:4], in_=lt[:, 1:2],
                                               func=AF.Exp), deps=(s_a,), inc=s_a)
        p.run("vector", [lambda E: E.tensor_sub(lt[:, 3:4], lt[:, 3:4], lt[:, 2:3])],
              deps=(s_a,), inc=s_v)
        p.run("vector", [lambda E: E.tensor_scalar(lneg[:], lt[:, 3:4],
                                                   LAMBDA_INIT, None, ALU.subtract)],
              deps=(s_v,), inc=s_v)
        p.wait("tensor", s_v)
        p.wait("tensor", s_c)
        p.run("tensor", lambda E: E.matmul(ps_l[:, 0:1], onesb[:], lneg[:],
                                           start=True, stop=True), inc=s_mm)
        p.run("vector", [lambda E: E.tensor_copy(lcol[:], ps_l[:, 0:1])],
              deps=(s_mm,), inc=s_v)
        for eng in ("tensor", "vector", "scalar"):
            p.wait(eng, s_w)
            p.wait(eng, s_a)
            p.wait(eng, s_v)

        for i in range(nt):
            j = i % 2
            if i >= 2:
                p.wait("sync", s_st[j].h, 16 * (i // 2))
            p.dma("sync", xt[j][:], x_in[i * 128:(i + 1) * 128, :], inc=s_x[j])
            p.dma("sync", y1t[j][:], y1_in[i * 128:(i + 1) * 128, :], inc=s_x[j])
            p.dma("sync", y2t[j][:], y2_in[i * 128:(i + 1) * 128, :], inc=s_x[j])
            p.wait("vector", s_x[j].h, 48 * (i // 2 + 1))
            p.wait("vector", s_tp)
            p.run("vector", [(lambda E, j=j: E.scalar_tensor_tensor(
                yd[:], y2t[j][:], lcol[:], y1t[j][:], ALU.mult, ALU.add))],
                deps=(s_v,), inc=s_v)
            p.run("vector", [lambda E: E.tensor_copy(ydb[:], yd[:])],
                  deps=(s_v,), inc=s_v)
            p.wait("vector", s_mm)
            emit_tp_in(p, ydb, (lambda k: ydT[:, k, :]), identb, tpb, s_tp, s_ev,
                       deps=(s_v,))
            p.wait("tensor", s_ev)

            def epi(m, reg, j=j):
                p.run("vector", [(lambda E, m=m, reg=reg: E.tensor_copy(
                    yd[:, m * 128:(m + 1) * 128], reg))], deps=(s_ep,), inc=s_ep)

            emit_dense8(p, nc, (lambda m, k: wp[:, k, m, :]),
                        (lambda k: ydT[:, k, :]), ps_mm, epi, s_mm, s_ep,
                        epi_eng="vector")
            for half in range(2):
                ks = range(half * 4, half * 4 + 4)
                p.wait("tensor", s_v)
                p.wait("tensor", s_ep)
                tps = [(lambda E, k=k, half=half: E.transpose(
                    ps_tp[:, half * 512 + (k % 4) * 128: half * 512 + (k % 4) * 128 + 128],
                    yd[0:128, k * 128:(k + 1) * 128], identf[:])) for k in ks]
                p.run("tensor", tps, inc=s_tp)
                p.run("vector",
                      [(lambda E, k=k, half=half, j=j:
                        E.scalar_tensor_tensor(
                            fin[j][:, k * 128:(k + 1) * 128],
                            ps_tp[:, half * 512 + (k % 4) * 128:
                                  half * 512 + (k % 4) * 128 + 128], 1.0,
                            xt[j][:, k * 128:(k + 1) * 128], ALU.mult, ALU.add))
                       for k in ks], deps=(s_tp,), inc=s_v)
            p.dma("sync", xo[i * 128:(i + 1) * 128, :], fin[j][:], deps=(s_v,),
                  inc=s_st[j])
        for j in range(2):
            p.wait("sync", s_st[j].h, s_st[j].v)
        p.build(nc)
    return nc


def l1a_tables():
    Tn, hd = 8192, 64
    inv_freq = 1.0 / 10000.0 ** (np.arange(0, hd, 2, dtype=np.float32) / hd)
    freqs = np.arange(Tn, dtype=np.float32)[:, None] * inv_freq[None, :]
    return np.ascontiguousarray(
        np.stack([np.cos(freqs), np.sin(freqs)], axis=1).astype(np.float32))


def run_l1attn(x, Wb, launch=_launch):
    ident = _ident()
    nc_ln = _cached(("ln",), lambda: build_ln())
    res = launch(nc_ln, [{"x": s} for s in _shard_tok(x)])
    z = np.concatenate([r["xo"] for r in res], axis=0)

    tabs = l1a_tables()
    nc_pre = _cached(("l1a_pre",), lambda: build_l1a_pre())
    ins = []
    for c in range(NCORES):
        t0 = c * TPC
        blo = (c // 4) * 8192
        ins.append({"z": _shard_tok(z)[c], "wq": Wb["WqT"], "wk": Wb["WkT"],
                    "wv": Wb["WvT"],
                    "tabs": np.ascontiguousarray(tabs[t0 - blo:t0 - blo + TPC]),
                    "identin": ident})
    res = launch(nc_pre, ins)
    qT = [np.concatenate([res[b * 4 + i]["qTo"] for i in range(4)], axis=2)
          for b in range(2)]
    kT = [np.concatenate([res[b * 4 + i]["kTo"] for i in range(4)], axis=2)
          for b in range(2)]
    vv = [np.concatenate([res[b * 4 + i]["vo"] for i in range(4)], axis=0)
          for b in range(2)]

    def unit_q(qTb, s, h):
        c = s * 4 + h // 2
        part = 64 * (h % 2)
        return qTb[part:part + 64, c, :]

    units = [(b, s, h) for b in range(2) for s in range(2) for h in range(8)]
    ins = []
    for core in range(NCORES):
        us = units[core * 4:(core + 1) * 4]
        q4 = np.stack([unit_q(qT[b], s, h) for (b, s, h) in us])
        k4 = np.stack([unit_q(kT[b], s, h) for (b, s, h) in us])
        v4 = np.stack([vv[b][:, h * 128:(h + 1) * 128] for (b, s, h) in us])
        ins.append({"qT4": np.ascontiguousarray(q4),
                    "kT4": np.ascontiguousarray(k4),
                    "v4": np.ascontiguousarray(v4), "identin": ident})
    nc_fl = _cached(("l1a_flash",), lambda: build_l1a_flash())
    res = launch(nc_fl, ins)
    y_half = [[np.zeros((8192, 1024), np.float32) for b in range(2)] for s in range(2)]
    for core in range(NCORES):
        for k4i, (b, s, h) in enumerate(units[core * 4:(core + 1) * 4]):
            y_half[s][b][:, h * 128:(h + 1) * 128] = res[core]["y4"][k4i]
    y1 = np.concatenate([y_half[0][0], y_half[0][1]], axis=0)
    y2 = np.concatenate([y_half[1][0], y_half[1][1]], axis=0)

    lv = np.concatenate([Wb["lq1"], Wb["lk1"], Wb["lq2"], Wb["lk2"]])[None, :]
    nc_post = _cached(("l1a_post",), lambda: build_l1a_post())
    ins = [{"x": _shard_tok(x)[c], "y1": _shard_tok(y1)[c], "y2": _shard_tok(y2)[c],
            "wp": Wb["WprojT"], "lv": np.ascontiguousarray(lv), "identin": ident}
           for c in range(NCORES)]
    res = launch(nc_post, ins)
    return np.concatenate([r["xo"] for r in res], axis=0)


def host_swapT(W):
    return np.ascontiguousarray(W.T.reshape(8, 128, 1024).transpose(1, 0, 2)
                                .reshape(128, 8 * 1024))



from contextlib import contextmanager
from concourse.ordered_engine_set import OrderedEngineSet

_FORI_REGS = {}


@contextmanager
def cached_fori(nc, E, start, end, step=1):
    engines = OrderedEngineSet([E.engine])
    key = (id(nc), E.engine)
    if key not in _FORI_REGS:
        _FORI_REGS[key] = nc.alloc_registers(f"cfori_{E.engine.value}", engines=engines)
    registers = _FORI_REGS[key]
    loop_id = nc.next_id()
    loop_start = f"cfori_{loop_id}_loop"
    loop_end = f"cfori_{loop_id}_end"
    nc.regs_mov(registers, start)
    nc.br(loop_start, engines=engines)
    with nc.body(loop_start, valid_engines=engines):
        yield nc.snap(registers, min_val=start, max_val=end - 1)
        nc.regs_alu(registers, registers, step, op=mybir.AluOpType.add)
        nc.br_lt(registers, end, on_true=loop_start, on_false=loop_end, engines=engines)
    nc.switch_bb(loop_end)


class ScanUnit:

    def __init__(self, ctx, nc, tag, CH, psum_base_ok=True):
        e = ctx.enter_context
        self.nc = nc
        self.CH = CH
        self.WTs = e(nc.sbuf_tensor(f"WTs{tag}", [128, 8, 1024], BF16))
        self.hTf = e(nc.sbuf_tensor(f"hTf{tag}", [128, 16], F32))
        self.hTb = e(nc.sbuf_tensor(f"hTb{tag}", [128, 16], BF16))
        self.u_sb = e(nc.sbuf_tensor(f"u_sb{tag}", [2, 1024], BF16))
        self.s_sb = e(nc.sbuf_tensor(f"s_sb{tag}", [128, 16], F32))
        self.g_sb = e(nc.sbuf_tensor(f"g_sb{tag}", [128, 16], F32))
        self.d_sb = e(nc.sbuf_tensor(f"d_sb{tag}", [128, 16], F32))
        self.m_sb = e(nc.sbuf_tensor(f"m_sb{tag}", [128, 16], F32))
        self.aring = [[e(nc.sbuf_tensor(f"ar{tag}{b}_{j}", [128, CH, 8], F32))
                       for j in range(2)] for b in range(2)]
        self.cring = [[e(nc.sbuf_tensor(f"cr{tag}{b}_{j}", [128, CH, 8], F32))
                       for j in range(2)] for b in range(2)]
        self.stg = [[e(nc.sbuf_tensor(f"st{tag}{b}_{j}", [128, CH, 8], F32))
                     for j in range(2)] for b in range(2)]
        self.u_ps = e(nc.psum_tensor(f"u_ps{tag}", [128, 256], F32))
        self.uT_ps = e(nc.psum_tensor(f"uT_ps{tag}", [128, 16], BF16))
        self.s_mm = Sem(ctx, nc, f"mm{tag}")
        self.s_ev = Sem(ctx, nc, f"ev{tag}")
        self.s_tp = Sem(ctx, nc, f"tp{tag}")
        self.s_sg = Sem(ctx, nc, f"sg{tag}")
        self.s_sig = Sem(ctx, nc, f"sig{tag}")
        self.s_d = Sem(ctx, nc, f"d{tag}")
        self.s_m = Sem(ctx, nc, f"m{tag}")
        self.s_h = Sem(ctx, nc, f"h{tag}")
        self.s_cast = Sem(ctx, nc, f"cast{tag}")
        self.s_stg = Sem(ctx, nc, f"stg{tag}")


def emit_scan_chunk_bodies(st, sl, identb):
    CH = st.CH

    def rAP(tile, t, inner=8):
        return bass.AP(tile, t * inner, [[CH * inner, 128], [1, inner]])

    def tensor_mms(E, t, B):
        E.wait_ge(st.s_cast.h, t + B["cast"])
        E.wait_ge(st.s_ev.h, t * 4 + B["ev"])
        for k in range(8):
            for c4 in range(4):
                mm = E.matmul(st.u_ps[32 * c4:32 * c4 + 2, 0:256],
                              st.hTb[:, 2 * k:2 * k + 2],
                              st.WTs[:, k, 256 * c4:256 * c4 + 256],
                              start=(k == 0), stop=(k == 7),
                              tile_position=(0, 32 * c4))
        mm.then_inc(st.s_mm.h, 1)

    def tensor_tps(E, t, B):
        E.wait_ge(st.s_ev.h, t * 4 + (B["ev"] + 4))
        E.wait_ge(st.s_sg.h, t * 2 + B["sg"])
        for c in range(8):
            tp = E.transpose(st.uT_ps[:, 2 * c:2 * c + 2],
                             st.u_sb[0:2, 128 * c:128 * c + 128],
                             identb[0:2, 0:2])
        tp.then_inc(st.s_tp.h, 1)

    def tensor_body(E, t, B):
        tensor_mms(E, t, B)
        tensor_tps(E, t, B)

    def vector_body(E, t, B):
        E.wait_ge(st.s_h.h, t + B["h"])
        E.tensor_sub(st.d_sb[:, 0:16:2], rAP(st.aring[0][sl], t), st.hTf[:, 0:16:2])
        E.tensor_sub(st.d_sb[:, 1:16:2], rAP(st.aring[1][sl], t),
                     st.hTf[:, 1:16:2]).then_inc(st.s_d.h, 2)
        E.wait_ge(st.s_mm.h, t + (B["mm"] + 1))
        E.tensor_copy(st.u_sb[0:2, 0:256], st.u_ps[0:2, 0:256]).then_inc(st.s_ev.h, 1)
        E.tensor_copy(st.u_sb[0:2, 256:512], st.u_ps[32:34, 0:256]).then_inc(st.s_ev.h, 1)
        E.wait_ge(st.s_tp.h, t + (B["tp"] + 1))
        E.tensor_add(st.s_sb[:, 0:16:2], st.uT_ps[:, 0:16:2], rAP(st.cring[0][sl], t))
        E.tensor_add(st.s_sb[:, 1:16:2], st.uT_ps[:, 1:16:2],
                     rAP(st.cring[1][sl], t)).then_inc(st.s_sg.h, 2)
        E.wait_ge(st.s_sig.h, t + (B["sig"] + 1))
        E.wait_ge(st.s_d.h, t * 2 + (B["d"] + 2))
        E.tensor_mul(st.m_sb[:], st.g_sb[:], st.d_sb[:]).then_inc(st.s_m.h, 1)
        E.wait_ge(st.s_m.h, t + (B["m"] + 1))
        E.wait_ge(st.s_stg.h, t * 2 + B["stg"])
        E.wait_ge(st.s_cast.h, t + (B["cast"] + 1))
        E.tensor_add(st.hTf[:], st.hTf[:], st.m_sb[:]).then_inc(st.s_h.h, 1)

    def scalar_body(E, t, B):
        E.wait_ge(st.s_mm.h, t + (B["mm"] + 1))
        E.copy(st.u_sb[0:2, 512:768], st.u_ps[64:66, 0:256]).then_inc(st.s_ev.h, 1)
        E.copy(st.u_sb[0:2, 768:1024], st.u_ps[96:98, 0:256]).then_inc(st.s_ev.h, 1)
        E.wait_ge(st.s_sg.h, t * 2 + (B["sg"] + 2))
        E.activation(out=st.g_sb[:], in_=st.s_sb[:],
                     func=AF.Sigmoid).then_inc(st.s_sig.h, 1)

    def gpsimd_body(E, t, B):
        E.wait_ge(st.s_m.h, t + (B["m"] + 1))
        E.tensor_add(st.hTb[:], st.hTf[:], st.m_sb[:]).then_inc(st.s_cast.h, 1)
        E.wait_ge(st.s_h.h, t + (B["h"] + 1))
        E.tensor_copy(rAP(st.stg[0][sl], t), st.hTf[:, 0:16:2]).then_inc(st.s_stg.h, 1)
        E.tensor_copy(rAP(st.stg[1][sl], t), st.hTf[:, 1:16:2]).then_inc(st.s_stg.h, 1)

    return dict(tensor=tensor_body, vector=vector_body, scalar=scalar_body,
                gpsimd=gpsimd_body, tensor_mms=tensor_mms, tensor_tps=tensor_tps)


def emit_scan_fori_chunk(p, st, sl, identb, nsteps_in_chunk=None):
    CH = st.CH if nsteps_in_chunk is None else nsteps_in_chunk
    bodies = emit_scan_chunk_bodies(st, sl, identb)
    B = {"cast": st.s_cast.v, "ev": st.s_ev.v, "tp": st.s_tp.v, "mm": st.s_mm.v,
         "sg": st.s_sg.v, "sig": st.s_sig.v, "d": st.s_d.v, "m": st.s_m.v,
         "h": st.s_h.v, "stg": st.s_stg.v}
    nc_ref = st.nc
    for eng in ("tensor", "vector", "scalar", "gpsimd"):
        def make(E, body=bodies[eng], B=B, CH=CH, nc=nc_ref):
            with cached_fori(nc, E, 0, CH) as t:
                body(E, t, B)
        p.emit(eng, make)
    st.s_mm.v += CH
    st.s_tp.v += CH
    st.s_ev.v += 4 * CH
    st.s_sg.v += 2 * CH
    st.s_sig.v += CH
    st.s_d.v += 2 * CH
    st.s_m.v += CH
    st.s_h.v += CH
    st.s_cast.v += CH
    st.s_stg.v += 2 * CH


def build_scan_fori(nsteps=8192, CH=128):
    import concourse.bacc as bacc
    nc = bacc.Bacc("TRN2", target_bir_lowering=False, debug=False)
    WT_in = nc.dram_tensor("WT", [128, 8 * 1024], BF16, kind="ExternalInput").ap()
    h0_in = nc.dram_tensor("h0", [128, 16], F32, kind="ExternalInput").ap()
    a_pl = [nc.dram_tensor(f"a{b}", [128, nsteps, 8], F32, kind="ExternalInput").ap()
            for b in range(2)]
    c_pl = [nc.dram_tensor(f"c{b}", [128, nsteps, 8], F32, kind="ExternalInput").ap()
            for b in range(2)]
    ident_in = nc.dram_tensor("identin", [128, 128], F32, kind="ExternalInput").ap()
    hs_pl = [nc.dram_tensor(f"hs{b}", [128, nsteps, 8], F32, kind="ExternalOutput").ap()
             for b in range(2)]
    hf_out = nc.dram_tensor("hf", [128, 16], F32, kind="ExternalOutput").ap()

    nch = nsteps // CH
    with ExitStack() as ctx:
        e = ctx.enter_context
        st = ScanUnit(ctx, nc, "A", CH)
        identf = e(nc.sbuf_tensor("identf", [128, 128], F32))
        identb = e(nc.sbuf_tensor("identb", [128, 128], BF16))
        s_w = Sem(ctx, nc, "s_w")
        s_i = Sem(ctx, nc, "s_i")
        s_ra = [[Sem(ctx, nc, f"s_ra{b}_{j}") for j in range(2)] for b in range(2)]
        s_rc = [[Sem(ctx, nc, f"s_rc{b}_{j}") for j in range(2)] for b in range(2)]
        s_so = [[Sem(ctx, nc, f"s_so{b}_{j}") for j in range(2)] for b in range(2)]

        p = Prog()
        p.dma("sync", st.WTs[:], WT_in.rearrange("p (k n) -> p k n", k=8), inc=s_w)
        p.dma("sync", st.hTf[:], h0_in, inc=s_w)
        p.dma("sync", identf[:], ident_in, inc=s_w)
        p.run("scalar", lambda E: E.activation(out=identb[:], in_=identf[:], func=AF.Copy),
              deps=(s_w,), inc=s_i)
        p.run("scalar", lambda E: E.activation(out=st.hTb[:], in_=st.hTf[:], func=AF.Copy),
              inc=st.s_cast)
        p.wait("tensor", s_i)
        p.wait("vector", s_i)
        p.wait("gpsimd", s_i)
        for ch in range(2):
            for b in range(2):
                p.dma("sync", st.aring[b][ch][:], a_pl[b][:, ch * CH:(ch + 1) * CH, :],
                      inc=s_ra[b][ch])
                p.dma("sync", st.cring[b][ch][:], c_pl[b][:, ch * CH:(ch + 1) * CH, :],
                      inc=s_rc[b][ch])

        for ch in range(nch):
            sl = ch % 2
            ldv = 16 * (ch // 2 + 1)
            for b in range(2):
                p.wait("tensor", s_ra[b][sl].h, ldv)
                p.wait("tensor", s_rc[b][sl].h, ldv)
                p.wait("vector", s_ra[b][sl].h, ldv)
                p.wait("vector", s_rc[b][sl].h, ldv)
            if ch >= 2:
                sov = 16 * (ch // 2)
                p.wait("gpsimd", s_so[0][sl].h, sov)
                p.wait("gpsimd", s_so[1][sl].h, sov)
            nch_pre = ch + 2
            if nch_pre < nch:
                p.wait("sync", st.s_h, (nch_pre - 1) * CH)
                for b in range(2):
                    p.dma("sync", st.aring[b][nch_pre % 2][:],
                          a_pl[b][:, nch_pre * CH:(nch_pre + 1) * CH, :],
                          inc=s_ra[b][nch_pre % 2])
                    p.dma("sync", st.cring[b][nch_pre % 2][:],
                          c_pl[b][:, nch_pre * CH:(nch_pre + 1) * CH, :],
                          inc=s_rc[b][nch_pre % 2])
            emit_scan_fori_chunk(p, st, sl, identb)
            p.wait("sync", st.s_stg, st.s_stg.v)
            for b in range(2):
                p.dma("sync", hs_pl[b][:, ch * CH:(ch + 1) * CH, :],
                      st.stg[b][sl][:], inc=s_so[b][sl])

        p.wait("sync", st.s_h, nsteps)
        p.dma("sync", hf_out, st.hTf[:], inc=s_w)
        for b in range(2):
            for j in range(2):
                p.wait("sync", s_so[b][j].h, s_so[b][j].v)
        p.wait("sync", s_w, s_w.v)
        p.build(nc)
    nc.compile()
    return nc



def build_a0mega(nsteps=8192, CH=128):
    import concourse.bacc as bacc
    nc = bacc.Bacc("TRN2", target_bir_lowering=False, debug=False)
    WT_A = nc.dram_tensor("WTA", [128, 8 * 1024], BF16, kind="ExternalInput").ap()
    WT_B = nc.dram_tensor("WTB", [128, 8 * 1024], BF16, kind="ExternalInput").ap()
    h0A_in = nc.dram_tensor("h0A", [128, 16], F32, kind="ExternalInput").ap()
    h0B_in = nc.dram_tensor("h0B", [128, 16], F32, kind="ExternalInput").ap()
    a_pl = [nc.dram_tensor(f"a{b}", [128, nsteps, 8], F32, kind="ExternalInput").ap()
            for b in range(2)]
    c_pl = [nc.dram_tensor(f"c{b}", [128, nsteps, 8], F32, kind="ExternalInput").ap()
            for b in range(2)]
    wiT = nc.dram_tensor("wiT", [128, 8192], BF16, kind="ExternalInput").ap()
    wifT = nc.dram_tensor("wifT", [128, 8192], BF16, kind="ExternalInput").ap()
    lb_in = nc.dram_tensor("lb", [128, 8], F32, kind="ExternalInput").ap()
    ident_in = nc.dram_tensor("identin", [128, 128], F32, kind="ExternalInput").ap()
    hs_pl = [nc.dram_tensor(f"hs{b}", [128, nsteps, 8], F32, kind="ExternalOutput").ap()
             for b in range(2)]

    nch = nsteps // CH
    LAG = 2
    with ExitStack() as ctx:
        e = ctx.enter_context
        A = ScanUnit(ctx, nc, "A", CH)
        Bu = ScanUnit(ctx, nc, "B", CH)
        wi = e(nc.sbuf_tensor("wi", [128, 8, 8, 128], BF16))
        wif = e(nc.sbuf_tensor("wif", [128, 8, 8, 128], BF16))
        lb = e(nc.sbuf_tensor("lbs", [128, 8], F32))
        identf = e(nc.sbuf_tensor("identf", [128, 128], F32))
        identb = e(nc.sbuf_tensor("identb", [128, 128], BF16))
        hTm = e(nc.sbuf_tensor("hTm", [128, 8, 128], BF16))
        tmp_a = e(nc.sbuf_tensor("tmpa", [128, 128], F32))
        ps_m = e(nc.psum_tensor("ps_m", [128, 1024], F32))
        s_w = Sem(ctx, nc, "s_w")
        s_i = Sem(ctx, nc, "s_i")
        s_ra = [[Sem(ctx, nc, f"s_ra{b}_{j}") for j in range(2)] for b in range(2)]
        s_rc = [[Sem(ctx, nc, f"s_rc{b}_{j}") for j in range(2)] for b in range(2)]
        s_so = [[Sem(ctx, nc, f"s_so{b}_{j}") for j in range(2)] for b in range(2)]
        s_gm = Sem(ctx, nc, "s_gm")
        s_mmm = Sem(ctx, nc, "s_mmm")
        s_epm = Sem(ctx, nc, "s_epm")

        p = Prog()
        p.dma("sync", A.WTs[:], WT_A.rearrange("p (k n) -> p k n", k=8), inc=s_w)
        p.dma("sync", Bu.WTs[:], WT_B.rearrange("p (k n) -> p k n", k=8), inc=s_w)
        p.dma("sync", A.hTf[:], h0A_in, inc=s_w)
        p.dma("sync", Bu.hTf[:], h0B_in, inc=s_w)
        p.dma("sync", wi[:], wiT.rearrange("p (a b m) -> p a b m", a=8, b=8), inc=s_w)
        p.dma("sync", wif[:], wifT.rearrange("p (a b m) -> p a b m", a=8, b=8), inc=s_w)
        p.dma("sync", lb[:], lb_in, inc=s_w)
        p.dma("sync", identf[:], ident_in, inc=s_w)
        p.run("scalar", lambda E: E.activation(out=identb[:], in_=identf[:], func=AF.Copy),
              deps=(s_w,), inc=s_i)
        p.run("scalar", lambda E: E.activation(out=A.hTb[:], in_=A.hTf[:], func=AF.Copy),
              inc=A.s_cast)
        p.run("scalar", lambda E: E.activation(out=Bu.hTb[:], in_=Bu.hTf[:], func=AF.Copy),
              inc=Bu.s_cast)
        for eng in ("tensor", "vector", "gpsimd"):
            p.wait(eng, s_i)
        for ch in range(2):
            for b in range(2):
                p.dma("sync", A.aring[b][ch][:], a_pl[b][:, ch * CH:(ch + 1) * CH, :],
                      inc=s_ra[b][ch])
                p.dma("sync", A.cring[b][ch][:], c_pl[b][:, ch * CH:(ch + 1) * CH, :],
                      inc=s_rc[b][ch])

        gather_done = {}
        mid_done = {}
        for ch in range(nch + LAG):
            chA = ch
            chB = ch - LAG
            act_A = chA < nch
            act_B = chB >= 0
            if act_A:
                sl = chA % 2
                ldv = 16 * (chA // 2 + 1)
                for b in range(2):
                    p.wait("tensor", s_ra[b][sl].h, ldv)
                    p.wait("tensor", s_rc[b][sl].h, ldv)
                    p.wait("vector", s_ra[b][sl].h, ldv)
                    p.wait("vector", s_rc[b][sl].h, ldv)
                if chA - 2 in gather_done:
                    p.wait("gpsimd", s_gm, gather_done[chA - 2])
                nch_pre = chA + 2
                if nch_pre < nch:
                    p.wait("sync", A.s_h, (nch_pre - 1) * CH)
                    for b in range(2):
                        p.dma("sync", A.aring[b][nch_pre % 2][:],
                              a_pl[b][:, nch_pre * CH:(nch_pre + 1) * CH, :],
                              inc=s_ra[b][nch_pre % 2])
                        p.dma("sync", A.cring[b][nch_pre % 2][:],
                              c_pl[b][:, nch_pre * CH:(nch_pre + 1) * CH, :],
                              inc=s_rc[b][nch_pre % 2])
            if act_B:
                slb = chB % 2
                p.wait("tensor", s_epm, mid_done[chB])
                p.wait("vector", s_epm, mid_done[chB])
                if chB >= 2:
                    sov = 16 * (chB // 2)
                    p.wait("gpsimd", s_so[0][slb].h, sov)
                    p.wait("gpsimd", s_so[1][slb].h, sov)

            if act_A and act_B:
                bodiesA = emit_scan_chunk_bodies(A, chA % 2, identb)
                bodiesB = emit_scan_chunk_bodies(Bu, chB % 2, identb)
                BA = {k: getattr(A, "s_" + k).v for k in
                      ("cast", "ev", "tp", "mm", "sg", "sig", "d", "m", "h", "stg")}
                BB = {k: getattr(Bu, "s_" + k).v for k in
                      ("cast", "ev", "tp", "mm", "sg", "sig", "d", "m", "h", "stg")}
                def make_t(E, bA=bodiesA, bB=bodiesB, BA=BA, BB=BB, CH=CH):
                    with cached_fori(nc, E, 0, CH) as t:
                        bA["tensor_mms"](E, t, BA)
                        bB["tensor_mms"](E, t, BB)
                        bA["tensor_tps"](E, t, BA)
                        bB["tensor_tps"](E, t, BB)
                p.emit("tensor", make_t)
                for eng in ("vector", "scalar", "gpsimd"):
                    def make(E, bA=bodiesA[eng], bB=bodiesB[eng], BA=BA, BB=BB, CH=CH):
                        with cached_fori(nc, E, 0, CH) as t:
                            bA(E, t, BA)
                            bB(E, t, BB)
                    p.emit(eng, make)
                for u in (A, Bu):
                    u.s_mm.v += CH; u.s_tp.v += CH; u.s_ev.v += 4 * CH
                    u.s_sg.v += 2 * CH; u.s_sig.v += CH; u.s_d.v += 2 * CH
                    u.s_m.v += CH; u.s_h.v += CH; u.s_cast.v += CH; u.s_stg.v += 2 * CH
            elif act_A:
                emit_scan_fori_chunk(p, A, chA % 2, identb)
            elif act_B:
                emit_scan_fori_chunk(p, Bu, chB % 2, identb)

            if act_A:
                sl = chA % 2
                p.wait("vector", A.s_stg, A.s_stg.v)
                for b in range(2):
                    if chA - LAG >= 0:
                        p.wait("scalar", Bu.s_h, (chA - LAG + 1) * CH)
                        p.wait("vector", Bu.s_h, (chA - LAG + 1) * CH)
                    p.wait("vector", s_mmm)
                    p.run("vector", [(lambda E, b=b, sl=sl, k=k:
                                      E.tensor_copy(hTm[:, k, :], A.stg[b][sl][:, :, k]))
                                     for k in range(8)], inc=s_gm)
                    p.wait("tensor", s_gm)
                    for m in range(8):
                        p.wait("tensor", s_epm)
                        mms = [(lambda E, b=b, m=m, k=k, sl=sl: E.matmul(
                            ps_m[:, 0:128], wi[:, k, m, :], hTm[:, k, :],
                            start=(k == 0), stop=(k == 7))) for k in range(8)]
                        p.run("tensor", mms, inc=s_mmm)
                        p.wait("scalar", s_mmm)
                        p.run("scalar", (lambda E: E.activation(
                            out=tmp_a[:], in_=ps_m[:, 0:128], func=AF.Tanh)),
                            deps=(s_epm,), inc=s_epm)
                        p.run("scalar", (lambda E, b=b, m=m, sl=sl: E.activation(
                            out=Bu.aring[b][sl][:, :, m], in_=tmp_a[:], func=AF.Tanh)),
                            deps=(s_epm,), inc=s_epm)
                        mms = [(lambda E, b=b, m=m, k=k, sl=sl: E.matmul(
                            ps_m[:, 512:640], wif[:, k, m, :], hTm[:, k, :],
                            start=(k == 0), stop=(k == 7))) for k in range(8)]
                        p.run("tensor", mms, inc=s_mmm)
                        p.wait("scalar", s_mmm)
                        p.run("scalar", (lambda E, b=b, m=m, sl=sl: E.activation(
                            out=Bu.cring[b][sl][:, :, m], in_=ps_m[:, 512:640],
                            func=AF.Identity, bias=lb[:, m:m + 1])),
                            deps=(s_epm,), inc=s_epm)
                gather_done[chA] = s_gm.v
                mid_done[chA] = s_epm.v

            if act_B:
                slb = chB % 2
                p.wait("sync", Bu.s_stg, Bu.s_stg.v)
                for b in range(2):
                    p.dma("sync", hs_pl[b][:, chB * CH:(chB + 1) * CH, :],
                          Bu.stg[b][slb][:], inc=s_so[b][slb])

        for b in range(2):
            for j in range(2):
                p.wait("sync", s_so[b][j].h, s_so[b][j].v)
        p.build(nc)
    nc.compile()
    return nc
